# version 27
# baseline (speedup 1.0000x reference)
import numpy as np
from contextlib import ExitStack

import concourse.bass as bass
import concourse.mybir as mybir
from concourse.bass_utils import run_bass_kernel_spmd

F32 = mybir.dt.float32
BF16 = mybir.dt.bfloat16
AF = mybir.ActivationFunctionType
ALU = mybir.AluOpType
AX = mybir.AxisListType

N, C, NT, CT = 4096, 1024, 32, 8
H, HD, NPAIR, E, D = 16, 64, 8, 8, 128
INV_SQRT_D = float(1.0 / np.sqrt(128.0))
QSCALE = 0.125


def c2_g(j, t):
    return 2 * j * j + 2 * j + t


C2_PER = c2_g(7, 31) + 1


def c1_chunk_list():
    lst = []
    for i in range(NT - 1):
        k = 128 * (i + 1)
        while k < N:
            kw = min(512, N - k)
            lst.append((i, k, kw))
            k += kw
    return lst


C1_LIST = c1_chunk_list()
C1_PER = len(C1_LIST)


def c1_items():
    per_i = {}
    for (i, k0, kw) in C1_LIST:
        per_i.setdefault(i, []).append((k0, kw))
    items = []
    for i in sorted(per_i.keys(), reverse=True):
        for c, (k0, kw) in enumerate(per_i[i]):
            items.append((i, c, k0, kw, c + 1 == len(per_i[i])))
    return items


C1_ITEMS = c1_items()
C1_NIT = len(C1_ITEMS)


def p3_sched():
    import math
    c2l = [(j, t) for j in range(7, -1, -1) for t in range(4 * j + 4)]
    need = {j: len([it for it in C1_ITEMS if it[0] >= 4 * j]) for j in range(8)}
    sched = []
    c1p = 0
    for k, (j, t) in enumerate(c2l):
        if t == 4 * j + 3:
            while c1p < need[j]:
                sched.append(("C1", C1_ITEMS[c1p]))
                c1p += 1
        sched.append(("C2", (j, t)))
        target = min(C1_NIT, math.ceil((k + 1) * C1_NIT / len(c2l)))
        while c1p < target:
            sched.append(("C1", C1_ITEMS[c1p]))
            c1p += 1
    assert c1p == C1_NIT
    return sched


SCHED = p3_sched()
C2_NIT = 144


def p3_counts():
    k2 = 0
    k1 = 0
    tu2 = 0
    info = []
    tu2_after_qchunk = {}
    pend = None
    for item in SCHED:
        kind, d = item
        if kind == "C2":
            if pend is not None:
                (pj, pt) = pend
                if pt >= 4 * pj:
                    tu2 += 2
                    if pt == 4 * pj + 3:
                        tu2_after_qchunk[pj] = tu2
            info.append((k2, tu2))
            pend = d
            k2 += 1
        else:
            info.append((k1, tu2))
            k1 += 1
    (pj, pt) = pend
    if pt >= 4 * pj:
        tu2 += 2
        if pt == 4 * pj + 3:
            tu2_after_qchunk[pj] = tu2
    return info, tu2_after_qchunk, tu2


P3_INFO, TU2_AFTER_Q, TU2_PER_PAIR = p3_counts()


def build_p3_items():
    cnt = dict(tC1=0, aC1=0, tC2=0, aC2=0, tQ3=0, vQ3=0, tV3=0, vV3=0)
    pool_rel = [None, None, None]
    pool_pos = [0]

    def qkv_items(pp):
        out = []
        for n in range(8):
            out.append({"kind": "QKVa", "pp": pp, "n": n})
            out.append({"kind": "QKVb", "pp": pp, "n": n})
        return out

    def resolve(seq):
        res = []
        for it in seq:
            it = dict(it)
            k = it["kind"]
            slot = pool_pos[0] % 3
            pool_pos[0] += 1
            it["slot"] = slot
            it["pool_wait"] = pool_rel[slot]
            if k == "C1":
                cnt["tC1"] += 1
                cnt["aC1"] += 1
                it["tC1"] = cnt["tC1"]
                it["aC1"] = cnt["aC1"]
                pool_rel[slot] = ("aC1", cnt["aC1"])
            elif k == "QKVa":
                cnt["tQ3"] += 1
                cnt["vQ3"] += 1
                it["tQ3"] = cnt["tQ3"]
                it["vQ3"] = cnt["vQ3"]
                pool_rel[slot] = ("vQ3", cnt["vQ3"])
            elif k == "QKVb":
                cnt["tQ3"] += 1
                cnt["vQ3"] += 1
                cnt["tV3"] += 1
                cnt["vV3"] += 1
                it["tQ3"] = cnt["tQ3"]
                it["vQ3"] = cnt["vQ3"]
                it["tV3"] = cnt["tV3"]
                it["vV3"] = cnt["vV3"]
                pool_rel[slot] = ("vV3", cnt["vV3"])
            else:
                cnt["tC2"] += 1
                cnt["aC2"] += 1
                it["tC2"] = cnt["tC2"]
                it["aC2"] = cnt["aC2"]
                it["eslot"] = (cnt["tC2"] - 1) % 3
                pool_rel[slot] = ("aC2", cnt["aC2"])
            res.append(it)
        return res

    pre = resolve(qkv_items(0))
    pairs = []
    for p in range(NPAIR):
        base = []
        for idx, (kind, d) in enumerate(SCHED):
            if kind == "C2":
                base.append({"kind": "C2", "j": d[0], "t": d[1],
                             "k2": P3_INFO[idx][0]})
            else:
                (i, c, k0, kw, lastf) = d
                base.append({"kind": "C1", "i": i, "c": c, "k0": k0,
                             "kw": kw, "lastf": lastf,
                             "k1": P3_INFO[idx][0]})
        if p < NPAIR - 1:
            qk = qkv_items(p + 1)
            merged = []
            qi = 0
            for idx2, it in enumerate(base):
                merged.append(it)
                if qi < len(qk) and idx2 >= 30 and (idx2 - 30) % 14 == 0:
                    merged.append(qk[qi])
                    qi += 1
                    if qi < len(qk) and (idx2 - 30) % 28 == 0:
                        merged.append(qk[qi])
                        qi += 1
            while qi < len(qk):
                merged.append(qk[qi])
                qi += 1
            base = merged
        pairs.append(resolve(base))
    return pre, pairs


P3_PRE, P3_PAIRS = build_p3_items()


class Builder:
    def __init__(self, debug=False, stages=("P0", "P1", "P3", "CMB")):
        self.debug = debug
        self.stages = set(stages)
        nc = bass.Bass("TRN2", target_bir_lowering=False)
        self.nc = nc
        d = {}
        d["x"] = nc.dram_tensor("x", [N, C], F32, kind="ExternalInput").ap()
        d["wq"] = nc.dram_tensor("wq", [C, C], F32, kind="ExternalInput").ap()
        d["wk"] = nc.dram_tensor("wk", [C, C], F32, kind="ExternalInput").ap()
        d["wv"] = nc.dram_tensor("wv", [C, C], F32, kind="ExternalInput").ap()
        d["sim"] = nc.dram_tensor("sim", [C, E], F32, kind="ExternalInput").ap()
        d["gates"] = nc.dram_tensor("gates", [E], F32, kind="ExternalInput").ap()
        d["qp"] = nc.dram_tensor("qp", [E, C, D], F32, kind="ExternalInput").ap()
        d["kp"] = nc.dram_tensor("kp", [E, C, D], F32, kind="ExternalInput").ap()
        d["vp"] = nc.dram_tensor("vp", [E, C, D], F32, kind="ExternalInput").ap()
        d["op"] = nc.dram_tensor("op", [E, D, C], F32, kind="ExternalInput").ap()
        d["out"] = nc.dram_tensor("out", [N, C], F32, kind="ExternalOutput").ap()
        sk = "ExternalOutput" if debug else "Internal"
        d["g_sp"] = nc.dram_tensor("g_sp", [N, C], BF16, kind=sk).ap()
        d["s_sp"] = nc.dram_tensor("s_sp", [N, 1024], BF16, kind=sk).ap()
        if debug:
            for nm, sh, dt in [
                ("dbg_lnat", [128, 256], F32), ("dbg_mask", [128, 256], F32),
                ("dbg_rw", [128, 256], F32), ("dbg_invr", [128, 32], F32),
                ("dbg_mv", [128, 64], F32), ("dbg_xnT", [128, 4096], BF16),
                ("dbg_snn", [128, 64], BF16), ("dbg_qfT", [128, 4096], BF16),
                ("dbg_kfT", [128, 4096], BF16), ("dbg_vfa", [128, 4160], BF16),
                ("dbg_ao", [128, 4096], BF16), ("dbg_rwp", [128, 256], F32),
                ("dbg_QT", [128, 4096], BF16), ("dbg_KT", [128, 4096], BF16),
                ("dbg_Vau", [128, 4160], BF16), ("dbg_dup", [128, 64], F32),
            ]:
                d[nm] = nc.dram_tensor(nm, sh, dt, kind="ExternalOutput").ap()
        self.d = d

    def alloc(self, st):
        nc = self.nc
        sb = lambda n, s_, dt=F32: st.enter_context(nc.sbuf_tensor(n, s_, dt))
        ps = lambda n, s_, dt=F32: st.enter_context(nc.psum_tensor(n, s_, dt))
        sem = lambda n: st.enter_context(nc.semaphore(n))
        s = self
        s.xnT = sb("xnT", [128, 32768], BF16)
        s.R2 = sb("R2", [128, 24576], BF16)
        s.R3 = sb("R3", [128, 17408], BF16)
        s.QT = sb("QTt", [128, 4096], BF16)
        s.KT = sb("KTt", [128, 4096], BF16)
        s.Vau = sb("Vau", [128, 4160], BF16)
        s.stage = sb("stagef", [128, 4096], F32)
        s.s1_all = sb("s1_all", [128, 32]); s.acc_all = sb("acc_all", [128, 32])
        s.m_all = sb("m_all", [128, 32]); s.v_all = sb("v_all", [128, 32])
        s.lnv = sb("lnv", [128, 32]); s.lna = sb("lna", [128, 32])
        s.ddt = sb("ddt", [128, 32]); s.rstd_all = sb("rstd_all", [128, 32])
        s.invr_raw = sb("invr_raw", [128, 32]); s.invr_all = sb("invr_all", [128, 32])
        s.lnat = sb("lnat", [128, 256]); s.msk_all = sb("msk_all", [128, 256])
        s.gated_all = sb("gated_all", [128, 256]); s.expg_all = sb("expg_all", [128, 256])

        s.mq_all = sb("mq_all", [128, 256]); s.rwp_all = sb("rwp_all", [128, 256])
        s.Z_all = sb("Z_all", [128, 32]); s.Zi_all = sb("Zi_all", [128, 32])
        s.veps = sb("veps", [128, 32])
        s.Dup = sb("Dup", [128, 64])
        s.tmpc = sb("tmpc", [128, 16]); s.tca = sb("tca", [128, 16])
        s.sim_in = sb("sim_in", [128, 64])

        s.n2 = sb("n2", [8, 1]); s.n2b = sb("n2b", [8, 1])
        s.invn_r = sb("invn_r", [8, 1]); s.invn = sb("invn", [8, 1])
        s.snn = sb("snn", [128, 64], BF16)
        s.gates_in = sb("gates_in", [1, 8]); s.sg_row = sb("sg_row", [1, 8])
        s.sgb = sb("sgb", [128, 8]); s.ones1 = sb("ones1", [1, 128])
        s.onesb = sb("onesb", [128, 128], BF16); s.idt_bf = sb("idt_bf", [128, 128], BF16)
        s.idt_f = sb("idt_f", [128, 128])
        s.qk_tmp = sb("qk_tmp", [128, 1024], BF16)
        s.s_stage = sb("s_stage", [128, 1024], BF16)
        s.dinv = sb("dinv", [128, 8]); s.dtot = sb("dtot", [128, 8])
        s.cacc = sb("cacc", [128, 128]); s.cacc2 = sb("cacc2", [128, 128])
        s.mask2 = sb("mask2", [128, 1024], BF16)
        s.expC = sb("expC", [128, 2048], BF16)
        s.rt1 = sb("rt1", [128, 8]); s.rt2 = sb("rt2", [128, 8]); s.rt3 = sb("rt3", [128, 8])
        s.rt4 = sb("rt4", [128, 8]); s.rt5 = sb("rt5", [128, 8]); s.rti = sb("rti", [128, 1])
        s.rtm = sb("rtm", [128, 2])
        s.PB = [ps(f"PB{i}", [128, 1024], F32) for i in range(4)]
        names = [
            "bar", "dX0", "dX1", "dX20", "dX21", "dW", "dP0", "dP1",
            "dOp0", "dOp1", "dWgt0", "dWgt1", "dG0", "dG1", "dS30", "dS31",
            "dCX0", "dCX1", "dCG0", "dCG1", "dCS0", "dCS1", "dOut0", "dOut1", "dDbg",
            "vInit", "gInit", "vP0a", "aSQ", "vP0b", "aP0b1", "vP0b2", "aP0b2", "vP0b3",
            "vXN", "tTR", "vEv", "tSim", "vSim1", "vSim2", "aSim", "vSnn3", "tSnn", "vSnnF",
            "aSg", "tSgb", "vSgb", "tL", "aLT", "tLN", "vLNat", "vRt1", "aRt", "vRt2", "vPc",
            "vOpc",
            "tA2q", "tA2k", "tA2v", "vA2q", "vA2k", "vA2v", "tT2", "aT2",
            "sCq", "sCk", "sCv",
            "tSf", "aEf", "gMf", "tUf", "vUf",
            "vA4", "tT4", "aE4", "tO4", "aS4",
            "vWc", "tQ3", "vQ3", "tV3", "vV3",
            "tC1", "aC1", "vC1",
            "tC2", "aC2", "gC2", "tU2", "vC2", "vC1r", "vM", "aRS", "gCM",
            "vCM",
        ]
        s.sem = {n: sem(n) for n in names}
        s.PBb = [p[:, 0:512].bitcast(BF16) for p in s.PB] + \
                [p[:, 512:1024].bitcast(BF16) for p in s.PB]
        s.cacc3 = None
        s.simT = s.R3[:, 8192:10240].bitcast(F32)[0:8, :]
        s.sq8 = s.s_stage[0:8, 0:1024]
        s.cacc3 = s.stage[:, 3072:3200]
        s.gbuf = s.stage[:, 3072:4096].bitcast(BF16)
        s.expC3 = s.stage[:, 2048:2560].bitcast(BF16)
        s.c1x = s.stage[:, 2560:3072].bitcast(BF16)
        s.dex = s.gbuf
        s.xnb = s.R2[:, 22528:24576]
        s.maskedf = s.mask2[:, 0:512]
        s.vtt = s.qk_tmp[:, 0:512]
        s.snnT = s.R3[0:8, 16384:17408]
        s.rwn_all = s.gated_all
        s.rw_all = s.expg_all

    def build(self):
        with ExitStack() as st:
            self.alloc(st)
            blk = st.enter_context(self.nc.Block())

            @blk.sync
            def _(e):
                self.sync_prog(e)

            @blk.tensor
            def _(e):
                self.tensor_prog(e)

            @blk.vector
            def _(e):
                self.vector_prog(e)

            @blk.scalar
            def _(e):
                self.scalar_prog(e)

            @blk.gpsimd
            def _(e):
                self.gpsimd_prog(e)
        return self.nc

    def sync_prog(self, e):
        s, d, sem = self, self.d, self.sem
        bar = sem["bar"]
        xb = [s.stage[:, 0:1024], s.stage[:, 1024:2048]]
        xr = d["x"].rearrange("(i p) c -> p i c", p=128)
        if "P0" in self.stages:
            e.dma_start(s.sim_in[:].rearrange("p (a e) -> p a e", a=8),
                        d["sim"].rearrange("(a p) e -> p a e", p=128)).then_inc(sem["dW"], 16)
            e.dma_start(s.gates_in[0:1, :],
                        d["gates"].rearrange("(a e) -> a e", a=1)).then_inc(sem["dW"], 16)
            xb3 = [s.stage[:, 0:1024], s.stage[:, 1024:2048],
                   s.stage[:, 2048:3072], s.stage[:, 3072:4096]]
            dxs = ["dX0", "dX1", "dX20", "dX21"]
            for i in range(NT):
                if i >= 4:
                    e.wait_ge(sem["vXN"], i - 3)
                e.dma_start(xb3[i % 4], xr[:, i, :]).then_inc(sem[dxs[i % 4]], 16)
        e.sem_inc(bar, 1)
        e.wait_ge(bar, 5)
        if "P1" in self.stages:
            gr = d["g_sp"].rearrange("(i p) c -> p i c", p=128)
            for qt in range(NT):
                e.wait_ge(sem["aS4"], qt + 1)
                e.dma_start(gr[:, qt, :],
                            s.gbuf[:, (qt % 2) * 1024:(qt % 2) * 1024 + 1024]
                            ).then_inc(sem["dG0" if qt % 2 == 0 else "dG1"], 16)
        e.sem_inc(bar, 1)
        e.wait_ge(bar, 10)
        if "P3" in self.stages:
            if "P1" in self.stages:
                e.wait_ge(sem["dG0"], 16 * 16)
                e.wait_ge(sem["dG1"], 16 * 16)
            wmap = {0: d["wq"], 1: d["wk"], 2: d["wv"]}
            for c in range(12):
                w = wmap[c // 4]
                cc = c % 4
                if c >= 2:
                    e.wait_ge(sem["vWc"], c - 1)
                e.dma_start(s.stage[:, (c % 2) * 2048:(c % 2) * 2048 + 2048]
                            .rearrange("p (a dd) -> p a dd", a=2),
                            w.rearrange("(a p) dd -> p a dd", p=128)[:, 2 * cc:2 * cc + 2, :]
                            ).then_inc(sem["dWgt0" if c % 2 == 0 else "dWgt1"], 16)
        e.sem_inc(bar, 1)
        if "CMB" in self.stages:
            outr = d["out"].rearrange("(i p) c -> p i c", p=128)
            vauo = s.Vau.bitcast(F32)
            e.wait_ge(sem["vWc"], 12)
            for k2 in range(NT):
                i = NT - 1 - k2
                if k2 >= 2:
                    e.wait_ge(sem["vCM"], k2 - 1)
                e.dma_start(xb[k2 % 2], xr[:, i, :]
                            ).then_inc(sem["dCX0" if k2 % 2 == 0 else "dCX1"], 16)
                if k2 >= 1:
                    e.wait_ge(sem["vCM"], k2)
                    e.dma_start(outr[:, i + 1, :],
                                vauo[:, ((k2 - 1) % 2) * 1024:
                                     ((k2 - 1) % 2) * 1024 + 1024]
                                ).then_inc(sem["dOut0" if (k2 - 1) % 2 == 0
                                           else "dOut1"], 16)
            e.wait_ge(sem["vCM"], NT)
            e.dma_start(outr[:, 0, :],
                        vauo[:, ((NT - 1) % 2) * 1024:((NT - 1) % 2) * 1024 + 1024]
                        ).then_inc(sem["dOut0" if (NT - 1) % 2 == 0 else "dOut1"], 16)
        e.wait_ge(bar, 15)
        if self.debug:
            self.debug_dmas(e)
        finals = {}
        if "P0" in self.stages:
            finals.update(dX0=16 * 8, dX1=16 * 8, dX20=16 * 8, dX21=16 * 8,
                          dW=32, dP0=16 * 12, dP1=16 * 12)
        if "P1" in self.stages:
            finals.update(dOp0=16 * 2, dOp1=16 * 2, dG0=16 * 16, dG1=16 * 16)
        if "P3" in self.stages:
            finals.update(dWgt0=16 * 6, dWgt1=16 * 6, dS30=16 * 32, dS31=16 * 32)
        if "CMB" in self.stages:
            finals.update(dCX0=16 * 16, dCX1=16 * 16, dCG0=16 * 16, dCG1=16 * 16,
                          dCS0=16 * 16, dCS1=16 * 16,
                          dOut0=16 * 16, dOut1=16 * 16)
        for k, v in finals.items():
            e.wait_ge(sem[k], v)

    def debug_dmas(self, e):
        s, d, sem = self, self.d, self.sem
        n = 0
        pairs = []
        if "P0" in self.stages:
            pairs += [("dbg_lnat", s.lnat[:]), ("dbg_mask", s.msk_all[:]),
                      ("dbg_rw", s.rw_all[:]), ("dbg_invr", s.invr_all[:]),
                      ("dbg_snn", s.snn[:]), ("dbg_xnT", s.xnT[:, 0:4096])]
        if "P1" in self.stages:
            pairs += [("dbg_qfT", s.R3[:, 0:4096]), ("dbg_kfT", s.R3[:, 4096:8192]),
                      ("dbg_vfa", s.R3[:, 8192:12352]), ("dbg_ao", s.R3[:, 12352:16448]),
                      ("dbg_rwp", s.rwp_all[:])]
        if "P3" in self.stages:
            pairs += [("dbg_QT", s.QT[:]), ("dbg_KT", s.KT[:]),
                      ("dbg_Vau", s.Vau[:]), ("dbg_dup", s.Dup[:])]
        for nm, src in pairs:
            e.dma_start(d[nm][:, :], src).then_inc(sem["dDbg"], 16)
            n += 1
        if "P0" in self.stages:
            e.dma_start(d["dbg_mv"][:, 0:64], s.Dup[:]).then_inc(sem["dDbg"], 16)
            n += 1
        e.wait_ge(sem["dDbg"], 16 * n)

    def gpsimd_prog(self, e):
        s, d, sem = self, self.d, self.sem
        bar = sem["bar"]
        e.memset(s.onesb[:], 1.0)
        e.drain()
        e.affine_select(s.idt_bf[:], s.onesb[:], [[1, 128]], ALU.is_equal, 0.0,
                        base=0, channel_multiplier=-1)
        e.drain()
        e.engine_nop().then_inc(sem["gInit"], 1)
        if "P0" in self.stages:
            pmap = {0: d["qp"], 1: d["kp"], 2: d["vp"]}
            R3f = [s.R3[:, 0:8192].bitcast(F32)[:, 0:1024],
                   s.R3[:, 0:8192].bitcast(F32)[:, 1024:2048]]
            for idx in range(24):
                w = pmap[idx // 8]
                a = idx % 8
                if idx >= 2:
                    e.wait_ge(sem["vPc"], idx - 1)
                e.dma_start(R3f[idx % 2],
                            w.rearrange("ee (a p) dd -> p a ee dd", p=128)
                            [:, a:a + 1, :, :]).then_inc(sem["dP0" if idx % 2 == 0 else "dP1"], 16)
        e.drain()
        e.sem_inc(bar, 1)
        e.wait_ge(bar, 5)
        if "P1" in self.stages:
            for j in range(8):
                if j >= 1:
                    e.wait_ge(sem["vUf"], j * 4)
                for qt in range(4):
                    t = 4 * j + qt
                    g = c2_g(j, t)
                    e.wait_ge(sem["aEf"], g + 1)
                    e.affine_select(
                        s.maskedf[:, qt * 128:qt * 128 + 128],
                        s.expC[:, (g % 4) * 512 + qt * 128:(g % 4) * 512 + qt * 128 + 128],
                        [[1, 128]], ALU.is_ge, 0.0,
                        base=0, channel_multiplier=-1).then_inc(sem["gMf"], 1)
            e.wait_ge(sem["sCv"], 8 * NT)
            for c in range(4):
                if c >= 2:
                    e.wait_ge(sem["vOpc"], c - 1)
                e.dma_start(s.stage[:, (c % 2) * 2048:(c % 2) * 2048 + 2048]
                            .rearrange("p (ee cc) -> p ee cc", ee=2),
                            d["op"].rearrange("ee p cc -> p ee cc")[:, 2 * c:2 * c + 2, :]
                            ).then_inc(sem["dOp0" if c % 2 == 0 else "dOp1"], 16)
        e.drain()
        e.sem_inc(bar, 1)
        e.wait_ge(bar, 10)
        if "P3" in self.stages:
            for p in range(NPAIR):
                for it in P3_PAIRS[p]:
                    if it["kind"] != "C2":
                        continue
                    j, t = it["j"], it["t"]
                    if t >= 4 * j:
                        qt = t - 4 * j
                        es = it["eslot"]
                        e.wait_ge(sem["aC2"], it["aC2"])
                        for h in range(2):
                            prevu = p * 64 + (6 - j) * 8 + qt * 2 + h + 1
                            if prevu >= 1:
                                e.wait_ge(sem["tU2"], prevu)
                            e.affine_select(
                                s.mask2[:, h * 512 + qt * 128: h * 512 + qt * 128 + 128],
                                s.expCs(es)[:, h * 512 + qt * 128:
                                            h * 512 + qt * 128 + 128],
                                [[1, 128]], ALU.is_ge, 0.0,
                                base=0, channel_multiplier=-1).then_inc(sem["gC2"], 1)
                    if t == 4 * j + 3:
                        e.wait_ge(sem["vC2"], p * 64 + (8 - j) * 8)
                        e.dma_start(
                            self.d["s_sp"].rearrange(
                                "(jj qt p2) (pr h dd) -> p2 jj qt pr h dd",
                                p2=128, jj=8, pr=8, h=2)[:, j, :, p, :, :],
                            s.s_stage[:, (j % 2) * 512:(j % 2) * 512 + 512]
                            .rearrange("p (qt h dd) -> p qt h dd", qt=4, h=2)
                        ).then_inc(sem["dS30" if j % 2 == 0 else "dS31"], 16)
                        if p == NPAIR - 1 and "CMB" in self.stages:
                            self.gpsimd_cmb_q(e, j)
        e.drain()
        e.sem_inc(bar, 1)
        e.wait_ge(bar, 15)

    def vector_cmb_one(self, e, j, b):
        s, sem = self, self.sem
        xb = [s.stage[:, 0:1024], s.stage[:, 1024:2048]]
        vauo = s.Vau.bitcast(F32)
        k2 = (7 - j) * 4 + b
        sl = k2 % 2
        e.wait_ge(sem["dCX0" if sl == 0 else "dCX1"], 16 * (k2 // 2 + 1))
        e.wait_ge(sem["gCM"], k2 + 1)
        if k2 >= 2:
            e.wait_ge(sem["dOut0" if sl == 0 else "dOut1"], 16 * (k2 // 2))
        e.tensor_tensor(vauo[:, sl * 1024:sl * 1024 + 1024],
                        s.xnb[:, sl * 1024:sl * 1024 + 1024],
                        xb[sl], ALU.add).then_inc(sem["vCM"], 1)

    def gpsimd_cmb_q(self, e, j):
        s, d, sem = self, self.d, self.sem
        gr = d["g_sp"].rearrange("(i p) c -> p i c", p=128)
        sr = d["s_sp"].rearrange("(i2 p2) dd -> p2 i2 dd", p2=128)
        spar = j % 2
        scount = 16 * (28 + ((6 if spar == 0 else 7) - j) // 2 + 1)
        for b in range(4):
            k2 = (7 - j) * 4 + b
            i = NT - 1 - k2
            sl = k2 % 2
            if k2 == 0:
                e.wait_ge(sem["dG0"], 16 * 16)
                e.wait_ge(sem["dG1"], 16 * 16)
            if k2 >= 2:
                e.wait_ge(sem["vCM"], k2 - 1)
            e.wait_ge(sem["dS30" if spar == 0 else "dS31"], scount)
            e.dma_start(s.QT[:, sl * 1024:sl * 1024 + 1024],
                        gr[:, i, :]).then_inc(sem["dCG0" if sl == 0
                                              else "dCG1"], 16)
            e.dma_start(s.KT[:, sl * 1024:sl * 1024 + 1024],
                        sr[:, i, :]).then_inc(sem["dCS0" if sl == 0
                                              else "dCS1"], 16)
            e.wait_ge(sem["dCG0" if sl == 0 else "dCG1"], 16 * (k2 // 2 + 1))
            e.wait_ge(sem["dCS0" if sl == 0 else "dCS1"], 16 * (k2 // 2 + 1))
            e.tensor_tensor(s.xnb[:, sl * 1024:sl * 1024 + 1024],
                            s.KT[:, sl * 1024:sl * 1024 + 1024],
                            s.QT[:, sl * 1024:sl * 1024 + 1024],
                            ALU.mult)
            e.drain().then_inc(sem["gCM"], 1)

    def tensor_prog(self, e):
        s, sem = self, self.sem
        bar = sem["bar"]
        PB, PBb = s.PB, s.PBb
        if "P0" in self.stages:
            e.wait_ge(sem["gInit"], 1)
            e.wait_ge(sem["vInit"], 1)
            e.wait_ge(sem["dW"], 32)
            last = None
            for a in range(8):
                last = e.transpose(PB[0][0:8, a * 128:(a + 1) * 128],
                                   s.sim_in[:, a * 8:(a + 1) * 8], s.idt_f[:])
            last.then_inc(sem["tSim"], 1)
            for i in range(NT):
                e.wait_ge(sem["vXN"], i + 1)
                if i >= 2:
                    e.wait_ge(sem["vEv"], i - 1)
                slot = PBb[3] if i % 2 == 0 else PBb[7]
                last = None
                for ct in range(8):
                    last = e.transpose(slot[:, ct * 128:(ct + 1) * 128],
                                       s.xnb[:, (i % 2) * 1024 + ct * 128:
                                             (i % 2) * 1024 + (ct + 1) * 128],
                                       s.idt_bf[:])
                last.then_inc(sem["tTR"], 1)
                if i == 3:
                    e.wait_ge(sem["vSnn3"], 1)
                    last = None
                    for a in range(8):
                        last = e.transpose(PBb[2][:, a * 8:(a + 1) * 8],
                                           s.snnT[0:8, a * 128:(a + 1) * 128],
                                           s.idt_bf[0:8, 0:8])
                    last.then_inc(sem["tSnn"], 1)
                    e.wait_ge(sem["aSg"], 1)
                    e.wait_ge(sem["vInit"], 1)
                    e.matmul(PB[1][:, 0:8], s.ones1[0:1, :], s.sg_row[0:1, :],
                             start=True, stop=True).then_inc(sem["tSgb"], 1)
            e.wait_ge(sem["vEv"], NT)
            e.wait_ge(sem["vSnnF"], 1)
            e.wait_ge(sem["vSgb"], 1)
            for j in range(8):
                if j >= 2:
                    e.wait_ge(sem["aLT"], j - 1)
                last = None
                for ct in range(8):
                    last = e.matmul(PB[1][0:8, (j % 2) * 512:(j % 2) * 512 + 512],
                                    s.snn[:, ct * 8:(ct + 1) * 8],
                                    s.xnT[:, ct * 4096 + j * 512: ct * 4096 + (j + 1) * 512],
                                    start=(ct == 0), stop=(ct == 7))
                last.then_inc(sem["tL"], 1)
            lgT = s.R3[:, 8192:16384].bitcast(F32)
            e.wait_ge(sem["aLT"], 8)
            e.wait_ge(sem["vSgb"], 1)
            last = None
            for i in range(NT):
                last = e.transpose(PB[2][:, i * 8:(i + 1) * 8],
                                   lgT[0:8, i * 128:(i + 1) * 128], s.idt_f[0:8, 0:8])
            last.then_inc(sem["tLN"], 1)
        e.sem_inc(bar, 1)
        if "P1" in self.stages:
            if "P0" in self.stages:
                e.wait_ge(sem["vSim1"], 1)
                e.wait_ge(sem["aLT"], 8)
                e.wait_ge(sem["vSgb"], 1)
                e.wait_ge(sem["vLNat"], 1)
                e.wait_ge(sem["vEv"], NT)
                e.wait_ge(sem["vPc"], 24)
            else:
                e.wait_ge(bar, 5)
            self.tensor_p1(e)
        e.sem_inc(bar, 1)
        e.wait_ge(bar, 10)
        if "P3" in self.stages:
            self.tensor_p3(e)
        e.sem_inc(bar, 1)
        e.wait_ge(bar, 15)

    def tensor_p1(self, e):
        s, sem = self, self.sem
        PB, PBb = s.PB, s.PBb
        qp_r = s.R2[:, 0:8192]
        kp_r = s.R2[:, 8192:16384]
        vp_r = s.R2[:, 16384:24576]
        qfT = s.R3[:, 0:4096]
        kfT = s.R3[:, 4096:8192]
        vfa = s.R3[:, 8192:12352]

        def trans_qk(qt):
            e.wait_ge(sem["sCk"], 8 * (qt + 1))
            if qt >= 2:
                e.wait_ge(sem["aT2"], qt - 1)
            slot = PBb[3] if qt % 2 == 0 else PBb[7]
            e.transpose(slot[:, 0:128],
                        s.qk_tmp[:, (qt % 2) * 512:(qt % 2) * 512 + 128], s.idt_bf[:])
            e.transpose(slot[:, 128:256],
                        s.qk_tmp[:, (qt % 2) * 512 + 128:(qt % 2) * 512 + 256],
                        s.idt_bf[:]).then_inc(sem["tT2"], 1)

        for qt in range(NT):
            for src, pb, tname, vname in ((qp_r, 0, "tA2q", "vA2q"),
                                          (kp_r, 1, "tA2k", "vA2k"),
                                          (vp_r, 2, "tA2v", "vA2v")):
                if qt >= 1:
                    e.wait_ge(sem[{"tA2q": "sCq", "tA2k": "sCk", "tA2v": "sCv"}[tname]], 8 * qt)
                last = None
                for ct in range(8):
                    for half in range(2):
                        last = e.matmul(
                            PB[pb][:, half * 512:(half + 1) * 512],
                            s.xnT[:, ct * 4096 + qt * 128: ct * 4096 + (qt + 1) * 128],
                            src[:, ct * 1024 + half * 512: ct * 1024 + (half + 1) * 512],
                            start=(ct == 0), stop=(ct == 7), skip_group_check=True)
                last.then_inc(sem[tname], 1)
            if qt >= 1:
                trans_qk(qt - 1)
        trans_qk(NT - 1)
        e.wait_ge(sem["aT2"], NT)
        e.wait_ge(sem["sCq"], 8 * NT)
        e.wait_ge(sem["sCv"], 8 * NT)

        SCF = [PB[0][:, 0:512], PB[0][:, 512:1024],
               PB[2][:, 0:512], PB[2][:, 512:1024]]

        def pv_f(j, t):
            g = c2_g(j, t)
            e.wait_ge(sem["aEf"], g + 1)
            if t == 0 and j >= 1:
                e.wait_ge(sem["vUf"], j * 4)
            for qt in range(4):
                qtile = 4 * j + qt
                if t > qtile:
                    continue
                first = (t == 0 and qt in (0, 2))
                if t == qtile:
                    e.wait_ge(sem["gMf"], j * 4 + qt + 1)
                    stat = s.maskedf[:, qt * 128:qt * 128 + 128]
                else:
                    stat = s.expC[:, (g % 4) * 512 + qt * 128:(g % 4) * 512 + qt * 128 + 128]
                mm = e.matmul(PB[1][:, qt * 256:qt * 256 + 129],
                              stat, vfa[:, t * 130:t * 130 + 129],
                              start=first, stop=(t == qtile), skip_group_check=True)
                if t == qtile:
                    mm.then_inc(sem["tUf"], 1)

        pendf = []
        for j in range(8):
            T = 4 * j + 4
            for t in range(T):
                g = c2_g(j, t)
                if g >= 4:
                    e.wait_ge(sem["aEf"], g - 3)
                e.matmul(SCF[g % 4],
                         kfT[:, t * 128:(t + 1) * 128],
                         qfT[:, j * 512:(j + 1) * 512],
                         start=True, stop=True).then_inc(sem["tSf"], 1)
                if len(pendf) >= 2:
                    pv_f(*pendf.pop(0))
                pendf.append((j, t))
        for pv in pendf:
            pv_f(*pv)
        op_r = s.R2[:, 0:8192]
        aoT = [s.Vau[:, 0:1024], s.Vau[:, 1024:2048]]
        e.wait_ge(sem["vOpc"], 4)
        e.wait_ge(sem["aEf"], C2_PER)
        e.wait_ge(sem["vUf"], NT)
        e.wait_ge(sem["aT2"], NT)
        for qt in range(NT):
            e.wait_ge(sem["vA4"], qt + 1)
            if qt >= 2:
                e.wait_ge(sem["aE4"], qt - 1)
            slot = PBb[3] if qt % 2 == 0 else PBb[7]
            last = None
            for ei in range(8):
                last = e.transpose(slot[:, ei * 128:(ei + 1) * 128],
                                   s.xnb[:, (qt % 2) * 1024 + ei * 128:
                                         (qt % 2) * 1024 + (ei + 1) * 128],
                                   s.idt_bf[:])
            last.then_inc(sem["tT4"], 1)
            e.wait_ge(sem["aE4"], qt + 1)
            if qt >= 2:
                e.wait_ge(sem["aS4"], qt - 1)
            pbo = PB[0] if qt % 2 == 0 else PB[1]
            last = None
            for ei in range(8):
                for half in range(2):
                    last = e.matmul(pbo[:, half * 512:(half + 1) * 512],
                                    aoT[qt % 2][:, ei * 128:(ei + 1) * 128],
                                    op_r[:, ei * 1024 + half * 512:
                                         ei * 1024 + (half + 1) * 512],
                                    start=(ei == 0), stop=(ei == 7),
                                    skip_group_check=True)
            last.then_inc(sem["tO4"], 1)

    def c1scr(self, s3):
        return (self.dex[:, 0:1024] if s3 == 0 else
                self.dex[:, 1024:2048] if s3 == 1 else
                self.R3[:, 12352:13376] if s3 == 2 else
                self.c1x[:, 0:1024])

    def expCs(self, es):
        return (self.expC[:, es * 1024:es * 1024 + 1024] if es < 2
                else self.expC3[:, 0:1024])

    def tensor_p3(self, e):
        s, sem = self, self.sem
        PB, PBb = s.PB, s.PBb
        Wq = s.R2[:, 0:8192]
        Wk = s.R2[:, 8192:16384]
        Wv = s.R2[:, 16384:24576]
        QTb = [s.QT, s.R3[:, 0:4096]]
        KTb = [s.KT, s.R3[:, 4096:8192]]
        Vaub = [s.Vau, s.R3[:, 8192:12352]]

        POOL = [PB[0], PB[1], PB[3]]
        POOLB = [PBb[4], PBb[5], PBb[7]]

        def pb3_wait(it):
            w = it.get("pool_wait")
            if w is not None:
                e.wait_ge(sem[w[0]], w[1])

        def emit_qkva(it):
            pp, n = it["pp"], it["n"]
            pb3_wait(it)
            last = None
            for ct in range(8):
                mv = s.xnT[:, ct * 4096 + n * 512: ct * 4096 + (n + 1) * 512]
                e.matmul(POOL[it["slot"]][:, 0:512],
                         Wq[:, ct * 1024 + pp * 128: ct * 1024 + (pp + 1) * 128],
                         mv, start=(ct == 0), stop=(ct == 7), skip_group_check=True)
                last = e.matmul(POOL[it["slot"]][:, 512:1024],
                                Wk[:, ct * 1024 + pp * 128: ct * 1024 + (pp + 1) * 128],
                                mv, start=(ct == 0), stop=(ct == 7),
                                skip_group_check=True)
            last.then_inc(sem["tQ3"], 1)

        def emit_qkvb(it):
            pp, n = it["pp"], it["n"]
            pb3_wait(it)
            last = None
            for ct in range(8):
                mv = s.xnT[:, ct * 4096 + n * 512: ct * 4096 + (n + 1) * 512]
                last = e.matmul(POOL[it["slot"]][:, 0:512],
                                Wv[:, ct * 1024 + pp * 128: ct * 1024 + (pp + 1) * 128],
                                mv, start=(ct == 0), stop=(ct == 7),
                                skip_group_check=True)
            last.then_inc(sem["tQ3"], 1)
            e.wait_ge(sem["vQ3"], it["vQ3"])
            last = None
            for b in range(4):
                last = e.transpose(POOLB[it["slot"]][:, b * 128:(b + 1) * 128],
                                   s.vtt[:, b * 128:(b + 1) * 128], s.idt_bf[:])
            last.then_inc(sem["tV3"], 1)

        for it in P3_PRE:
            if it["kind"] == "QKVa":
                if it["pp"] == 0 and it["n"] == 0:
                    e.wait_ge(sem["vWc"], 12)
                emit_qkva(it)
            else:
                emit_qkvb(it)

        for p in range(NPAIR):
            par = p % 2
            QT, KT, Vau = QTb[par], KTb[par], Vaub[par]
            e.wait_ge(sem["vQ3"], 16 * (p + 1))
            e.wait_ge(sem["vV3"], 8 * (p + 1))

            def pv_outer(j, t, g, es):
                e.wait_ge(sem["aC2"], g)
                if t == 0:
                    e.wait_ge(sem["vC2"], p * 64 + (7 - j) * 8)
                for qt in range(4):
                    qtile = 4 * j + qt
                    if t > qtile:
                        continue
                    if t == qtile:
                        e.wait_ge(sem["gC2"],
                                  p * 64 + (7 - j) * 8 + (t - 4 * j + 1) * 2)
                    for h in range(2):
                        first = (t == 0 and qt == 0)
                        if t == qtile:
                            stat = s.mask2[:, h * 512 + qt * 128: h * 512 + qt * 128 + 128]
                            e.matmul(PB[2][:, h * 512 + qt * 128: h * 512 + qt * 128 + 64],
                                     stat,
                                     Vau[:, t * 130 + h * 65: t * 130 + h * 65 + 64],
                                     start=first, stop=False, skip_group_check=True)
                            statf = s.expCs(es)[:, h * 512 + qt * 128:
                                                h * 512 + qt * 128 + 128]
                            mm = e.matmul(
                                PB[2][:, h * 512 + qt * 128 + 64: h * 512 + qt * 128 + 65],
                                statf,
                                Vau[:, t * 130 + h * 65 + 64: t * 130 + h * 65 + 65],
                                start=False, stop=True, skip_group_check=True)
                            mm.then_inc(sem["tU2"], 1)
                        else:
                            stat = s.expCs(es)[:, h * 512 + qt * 128:
                                               h * 512 + qt * 128 + 128]
                            e.matmul(PB[2][:, h * 512 + qt * 128: h * 512 + qt * 128 + 65],
                                     stat,
                                     Vau[:, t * 130 + h * 65: t * 130 + h * 65 + 65],
                                     start=first, stop=False, skip_group_check=True)

            pend = []
            for it in P3_PAIRS[p]:
                k = it["kind"]
                if k == "C2":
                    j, t = it["j"], it["t"]
                    pb3_wait(it)
                    last = None
                    for h in range(2):
                        stat = KT[64 * h:64 * h + 64, t * 128:(t + 1) * 128]
                        mv = QT[64 * h:64 * h + 64, j * 512:(j + 1) * 512]
                        last = e.matmul(POOL[it["slot"]][:, h * 512:h * 512 + 512],
                                        stat, mv, start=True, stop=True,
                                        tile_position=(64 * h, 0))
                    last.then_inc(sem["tC2"], 1)
                    if len(pend) >= 2:
                        pv_outer(*pend.pop(0))
                    pend.append((j, t, it["aC2"], it["eslot"]))
                elif k == "C1":
                    pb3_wait(it)
                    i, k0, kw = it["i"], it["k0"], it["kw"]
                    last = None
                    for h in range(2):
                        statq = QT[64 * h:64 * h + 64, i * 128:(i + 1) * 128]
                        mvk = KT[64 * h:64 * h + 64, k0:k0 + kw]
                        last = e.matmul(POOL[it["slot"]][:, h * 512:h * 512 + kw],
                                        statq, mvk, start=True, stop=True,
                                        tile_position=(64 * h, 0))
                    last.then_inc(sem["tC1"], 1)
                elif k == "QKVa":
                    emit_qkva(it)
                else:
                    emit_qkvb(it)
            for pv in pend:
                pv_outer(*pv)

    def vector_prog(self, e):
        s, sem = self, self.sem
        bar = sem["bar"]
        PB, PBb = s.PB, s.PBb
        xb = [s.stage[:, 0:1024], s.stage[:, 1024:2048]]
        if "P0" in self.stages:
            e.wait_ge(sem["gInit"], 1)
            e.tensor_copy(s.idt_f[:], s.idt_bf[:])
            e.memset(s.ones1[:], 1.0)
            e.drain().then_inc(sem["vInit"], 1)
            e.wait_ge(sem["tSim"], 1)
            e.tensor_copy(s.simT[0:8, :], PB[0][0:8, 0:1024]).then_inc(sem["vSim1"], 1)
            e.drain()
            e.tensor_tensor(s.sq8[0:8, :], s.simT[0:8, :], s.simT[0:8, :], ALU.mult)
            e.drain()
            e.reduce_sum(s.n2[0:8, :], s.sq8[0:8, :], axis=AX.X).then_inc(sem["vSim2"], 1)
            e.wait_ge(sem["aSim"], 1)
            e.tensor_scalar(s.invn[0:8, :], s.invn_r[0:8, :], 1.0e12, None, ALU.min)
            e.drain()
            e.tensor_scalar(s.snnT[0:8, :], s.simT[0:8, :], s.invn[0:8, :], None,
                            ALU.mult).then_inc(sem["vSnn3"], 1)
            xb3 = [s.stage[:, 0:1024], s.stage[:, 1024:2048],
                   s.stage[:, 2048:3072], s.stage[:, 3072:4096]]
            dxs = ["dX0", "dX1", "dX20", "dX21"]
            mvD = s.Dup[:].rearrange("p (i two) -> p two i", two=2)
            bn6 = s.tmpc[:, 0:12]

            def emit_xn(j):
                e.wait_ge(sem["aRS"], j // 2 + 1)
                if j >= 2:
                    e.wait_ge(sem["tTR"], j - 1)
                e.tensor_scalar(s.xnb[:, (j % 2) * 1024:(j % 2) * 1024 + 1024],
                                xb3[j % 4], s.Dup[:, 2 * j:2 * j + 1],
                                s.rstd_all[:, j:j + 1],
                                ALU.subtract, ALU.mult).then_inc(sem["vXN"], 1)

            def emit_ev(j):
                e.wait_ge(sem["tTR"], j + 1)
                slot = PBb[3] if j % 2 == 0 else PBb[7]
                e.tensor_copy(
                    s.xnT[:].rearrange("p (ct n) -> p ct n", ct=8)
                    [:, :, j * 128:(j + 1) * 128],
                    slot[:].rearrange("p (ct n) -> p ct n", ct=8)
                ).then_inc(sem["vEv"], 1)

            for i in range(NT):
                e.wait_ge(sem[dxs[i % 4]], 16 * (i // 4 + 1))
                e.bn_stats(bn6[:, 0:6], xb3[i % 4][:, 0:512])
                e.bn_stats(bn6[:, 6:12], xb3[i % 4][:, 512:1024])
                e.drain()
                e.bn_aggr(s.Dup[:, 2 * i:2 * i + 2], bn6)
                e.drain()
                if i % 2 == 1:
                    g = i // 2
                    e.tensor_scalar(s.veps[:, 2 * g:2 * g + 2],
                                    mvD[:, 1, 2 * g:2 * g + 2], 1.0e-5, None,
                                    ALU.add)
                    e.drain().then_inc(sem["vM"], 1)
                if i >= 2:
                    emit_xn(i - 2)
                if i >= 3:
                    emit_ev(i - 3)
            emit_xn(NT - 2)
            emit_ev(NT - 3)
            emit_xn(NT - 1)
            emit_ev(NT - 2)
            emit_ev(NT - 1)
            e.wait_ge(sem["aP0b1"], 1)
            e.tensor_tensor(s.ddt[:], s.lnv[:], s.lna[:], ALU.subtract
                            ).then_inc(sem["vP0b2"], 1)
            e.wait_ge(sem["aP0b2"], 1)
            e.tensor_scalar(s.invr_all[:], s.invr_raw[:], 1.0e12, None, ALU.min
                            ).then_inc(sem["vP0b3"], 1)
            R3f = [s.R3[:, 0:8192].bitcast(F32)[:, 0:1024],
                   s.R3[:, 0:8192].bitcast(F32)[:, 1024:2048]]
            for idx in range(24):
                e.wait_ge(sem["dP0" if idx % 2 == 0 else "dP1"], 16 * (idx // 2 + 1))
                w = idx // 8
                a = idx % 8
                e.tensor_copy(s.R2[:, w * 8192 + a * 1024: w * 8192 + (a + 1) * 1024],
                              R3f[idx % 2]).then_inc(sem["vPc"], 1)
            e.wait_ge(sem["tSnn"], 1)
            e.tensor_copy(s.snn[:], PBb[2][:, 0:64]).then_inc(sem["vSnnF"], 1)
            e.wait_ge(sem["tSgb"], 1)
            e.tensor_copy(s.sgb[:], PB[1][:, 0:8]).then_inc(sem["vSgb"], 1)
            e.wait_ge(sem["tLN"], 1)
            e.tensor_copy(s.lnat[:], PB[2][:, 0:256]).then_inc(sem["vLNat"], 1)
            e.drain()
            e.wait_ge(sem["vP0b3"], 1)
            A3 = lambda t: t[:].rearrange("p (q ee) -> p q ee", ee=8)
            bcq = lambda t: bass.AP(t, 0, [[32, 128], [1, 32], [0, 8]])
            sgb_b = bass.AP(s.sgb, 0, [[8, 128], [0, 32], [1, 8]])
            e.tensor_tensor(A3(s.mq_all), A3(s.lnat), bcq(s.invr_all), ALU.mult)
            e.drain()
            e.tensor_tensor(A3(s.rwp_all), A3(s.mq_all), sgb_b, ALU.subtract)
            e.drain()
            e.tensor_reduce(s.lna[:], A3(s.rwp_all), AX.X, ALU.max)
            e.drain()
            e.tensor_tensor(A3(s.mq_all), A3(s.rwp_all), bcq(s.lna), ALU.is_ge)
            e.drain()
            e.scalar_tensor_tensor(A3(s.expg_all), A3(s.mq_all), -1.0e9,
                                   A3(s.rwp_all), ALU.mult, ALU.add)
            e.drain()
            e.tensor_reduce(s.lnv[:], A3(s.expg_all), AX.X, ALU.max)
            e.drain()
            e.tensor_tensor(A3(s.expg_all), A3(s.rwp_all), bcq(s.lnv), ALU.is_ge)
            e.tensor_scalar(s.ddt[:], s.lna[:], 0.0, None, ALU.is_le)
            e.drain()
            e.tensor_scalar(s.mq_all[:], s.rwp_all[:], 0.0, None, ALU.is_gt)
            e.tensor_tensor(A3(s.gated_all), A3(s.expg_all), bcq(s.ddt), ALU.mult)
            e.drain()
            e.tensor_tensor(s.msk_all[:], s.gated_all[:], s.mq_all[:], ALU.add)
            e.drain()
            e.tensor_scalar(s.gated_all[:], s.rwp_all[:], 0.0, None, ALU.max)
            e.drain().then_inc(sem["vRt1"], 1)
            e.wait_ge(sem["aRt"], 1)
            e.tensor_tensor(s.rwn_all[:], s.msk_all[:], s.expg_all[:], ALU.mult)
            e.drain()
            e.reduce_sum(s.Z_all[:], s.rwn_all[:].rearrange("p (q ee) -> p q ee", ee=8),
                         axis=AX.X)
            e.drain()
            e.reciprocal(s.Zi_all[:], s.Z_all[:])
            e.drain()
            bcq2 = lambda t: bass.AP(t, 0, [[32, 128], [1, 32], [0, 8]])
            e.tensor_tensor(s.rw_all[:].rearrange("p (q ee) -> p q ee", ee=8),
                            s.rwn_all[:].rearrange("p (q ee) -> p q ee", ee=8),
                            bcq2(s.Zi_all), ALU.mult)
            e.drain()
            e.tensor_scalar(s.mq_all[:], s.msk_all[:], INV_SQRT_D, None, ALU.mult
                            ).then_inc(sem["vRt2"], 1)
        e.drain()
        e.sem_inc(bar, 1)
        e.wait_ge(bar, 5)
        if "P1" in self.stages:
            self.vector_p1(e)
        e.drain()
        e.sem_inc(bar, 1)
        e.wait_ge(bar, 10)
        if "P3" in self.stages:
            self.vector_p3(e)
        e.drain()
        e.sem_inc(bar, 1)
        e.wait_ge(bar, 15)

    def vector_p1(self, e):
        s, sem = self, self.sem
        PB, PBb = s.PB, s.PBb
        vfa = s.R3[:, 8192:12352]
        ao_all = s.R3[:, 12352:16448]
        qfT = s.R3[:, 0:4096]
        kfT = s.R3[:, 4096:8192]
        e.memset(vfa.rearrange("p (nt w) -> p nt w", w=130)[:, :, 128:130], 1.0)
        for qt in range(NT):
            specs = ((0, "tA2q", "sCq", s.mq_all, s.cacc,
                      s.qk_tmp[:, (qt % 2) * 512:(qt % 2) * 512 + 128]),
                     (1, "tA2k", "sCk", s.msk_all, s.cacc2,
                      s.qk_tmp[:, (qt % 2) * 512 + 128:(qt % 2) * 512 + 256]),
                     (2, "tA2v", "sCv", s.msk_all, s.cacc3,
                      vfa[:, qt * 130:qt * 130 + 128]))
            if qt >= 2:
                e.wait_ge(sem["tT2"], qt - 1)
            for pb, tname, cname, mask, acc, dst in specs:
                e.wait_ge(sem[tname], qt + 1)
                e.tensor_scalar(acc[:], PB[pb][:, 0:128],
                                mask[:, qt * 8:qt * 8 + 1], None, ALU.mult
                                ).then_inc(sem[cname], 1)
                for ei in range(1, 8):
                    e.wait_ge(sem[cname], qt * 8 + ei)
                    out = dst if ei == 7 else acc[:]
                    e.scalar_tensor_tensor(out, PB[pb][:, ei * 128:(ei + 1) * 128],
                                           mask[:, qt * 8 + ei:qt * 8 + ei + 1],
                                           acc[:], ALU.mult, ALU.add
                                           ).then_inc(sem[cname], 1)
            if qt >= 1:
                e.wait_ge(sem["tT2"], qt)
                slot = PBb[3] if (qt - 1) % 2 == 0 else PBb[7]
                e.tensor_copy(qfT[:, (qt - 1) * 128:qt * 128], slot[:, 0:128])
                e.tensor_copy(kfT[:, (qt - 1) * 128:qt * 128], slot[:, 128:256]
                              ).then_inc(sem["aT2"], 1)
        e.wait_ge(sem["tT2"], NT)
        slot = PBb[3] if (NT - 1) % 2 == 0 else PBb[7]
        e.tensor_copy(qfT[:, (NT - 1) * 128:NT * 128], slot[:, 0:128])
        e.tensor_copy(kfT[:, (NT - 1) * 128:NT * 128], slot[:, 128:256]
                      ).then_inc(sem["aT2"], 1)
        for j in range(8):
            e.wait_ge(sem["tUf"], (j + 1) * 4)
            for qt in range(4):
                e.reciprocal(s.dinv[:, qt:qt + 1], PB[1][:, qt * 256 + 128:qt * 256 + 129])
            e.drain()
            for qt in range(4):
                qtile = 4 * j + qt
                e.tensor_scalar(s.rwp_all[:, qtile * 8:qtile * 8 + 8],
                                s.rw_all[:, qtile * 8:qtile * 8 + 8],
                                s.dinv[:, qt:qt + 1], None, ALU.mult)
                e.tensor_copy(ao_all[:, qtile * 128:(qtile + 1) * 128],
                              PB[1][:, qt * 256:qt * 256 + 128])
            e.drain().then_inc(sem["vUf"], 4)
        for c in range(4):
            e.wait_ge(sem["dOp0" if c % 2 == 0 else "dOp1"], 16 * (c // 2 + 1))
            e.tensor_copy(s.R2[:, c * 2048:(c + 1) * 2048],
                          s.stage[:, (c % 2) * 2048:(c % 2) * 2048 + 2048]
                          ).then_inc(sem["vOpc"], 1)
        aoT = [s.Vau[:, 0:1024], s.Vau[:, 1024:2048]]
        for qt in range(NT):
            if qt >= 2:
                e.wait_ge(sem["tT4"], qt - 1)
            for ei in range(8):
                e.tensor_scalar(s.xnb[:, (qt % 2) * 1024 + ei * 128:
                                      (qt % 2) * 1024 + (ei + 1) * 128],
                                ao_all[:, qt * 128:(qt + 1) * 128],
                                s.rwp_all[:, qt * 8 + ei:qt * 8 + ei + 1], None,
                                ALU.mult)
            e.drain().then_inc(sem["vA4"], 1)
            if qt >= 1:
                e.wait_ge(sem["tT4"], qt)
                slot = PBb[3] if (qt - 1) % 2 == 0 else PBb[7]
                e.tensor_copy(aoT[(qt - 1) % 2], slot[:]).then_inc(sem["aE4"], 1)
        e.wait_ge(sem["tT4"], NT)
        slot = PBb[3] if (NT - 1) % 2 == 0 else PBb[7]
        e.tensor_copy(aoT[(NT - 1) % 2], slot[:]).then_inc(sem["aE4"], 1)

    def vector_p3(self, e):
        s, sem = self, self.sem
        PB, PBb = s.PB, s.PBb
        QTb = [s.QT, s.R3[:, 0:4096]]
        KTb = [s.KT, s.R3[:, 4096:8192]]
        Vaub = [s.Vau, s.R3[:, 8192:12352]]
        POOLV = [PB[0], PB[1], PB[3]]
        POOLVB = [PBb[4], PBb[5], PBb[7]]
        for c in range(12):
            e.wait_ge(sem["dWgt0" if c % 2 == 0 else "dWgt1"], 16 * (c // 2 + 1))
            w = c // 4
            cc = c % 4
            src_ = s.stage[:, (c % 2) * 2048:(c % 2) * 2048 + 2048]
            dst = s.R2[:, w * 8192 + cc * 2048: w * 8192 + (cc + 1) * 2048]
            if w == 0:
                e.tensor_scalar(dst, src_, QSCALE, None, ALU.mult).then_inc(sem["vWc"], 1)
            else:
                e.tensor_copy(dst, src_).then_inc(sem["vWc"], 1)
        for par in range(2):
            vv = Vaub[par].rearrange("p (nt w) -> p nt w", w=130)
            e.memset(vv[:, :, 64:65], 1.0)
            e.memset(vv[:, :, 129:130], 1.0)

        def emit_qkva(it):
            pp, n = it["pp"], it["n"]
            e.wait_ge(sem["tQ3"], it["tQ3"])
            e.tensor_copy(QTb[pp % 2][:, n * 512:(n + 1) * 512],
                          POOLV[it["slot"]][:, 0:512])
            e.tensor_copy(KTb[pp % 2][:, n * 512:(n + 1) * 512],
                          POOLV[it["slot"]][:, 512:1024]).then_inc(sem["vQ3"], 1)

        def emit_qkvb(it):
            pp, n = it["pp"], it["n"]
            e.wait_ge(sem["tQ3"], it["tQ3"])
            e.tensor_copy(s.vtt[:], POOLV[it["slot"]][:, 0:512]).then_inc(sem["vQ3"], 1)
            e.wait_ge(sem["tV3"], it["tV3"])
            last = None
            for b in range(4):
                nt = 4 * n + b
                last = e.tensor_copy(
                    Vaub[pp % 2][:, nt * 130:nt * 130 + 130]
                    .rearrange("p (a y) -> p a y", a=2)[:, :, 0:64],
                    POOLVB[it["slot"]][:, b * 128:(b + 1) * 128]
                    .rearrange("p (a y) -> p a y", a=2))
            last.then_inc(sem["vV3"], 1)

        for it in P3_PRE:
            (emit_qkva if it["kind"] == "QKVa" else emit_qkvb)(it)

        for p in range(NPAIR):
            emit_evac = self.emit_evac_fn(e, p)
            cmb_q = []
            drip = [0]

            def drip_cmb(force=False):
                drip[0] += 1
                if cmb_q and (force or drip[0] % 2 == 0):
                    self.vector_cmb_one(e, *cmb_q.pop(0))
            e.memset(s.Dup[:], 0.0)
            e.drain()
            for it in P3_PAIRS[p]:
                k = it["kind"]
                if k == "QKVa":
                    emit_qkva(it)
                    continue
                if k == "QKVb":
                    emit_qkvb(it)
                    continue
                if k == "C1":
                    i, c, kw = it["i"], it["c"], it["kw"]
                    m1 = it["aC1"]
                    e.wait_ge(sem["aC1"], m1)
                    e.tensor_reduce(
                        s.tmpc[:].rearrange("p (h c) -> p h c", h=2)[:, :, c:c + 1],
                        s.c1scr((m1 - 1) % 3)[:, 0:2 * kw]
                        .rearrange("p (h k) -> p h k", h=2),
                        AX.X, ALU.add).then_inc(sem["vC1r"], 1)
                    drip_cmb()
                    if not it["lastf"]:
                        continue
                    e.drain()
                    nchunks = c + 1
                    srcs = {hh: [s.tmpc[:, hh * 8 + cc:hh * 8 + cc + 1]
                                 for cc in range(nchunks)] for hh in range(2)}
                    lvl = 0
                    while max(len(srcs[0]), len(srcs[1])) > 1:
                        base = (lvl % 2) * 8
                        scr = 0
                        nxt = {0: [], 1: []}
                        for hh in range(2):
                            lst = srcs[hh]
                            for a2 in range(0, len(lst) - 1, 2):
                                dstc = s.tca[:, base + scr:base + scr + 1]
                                scr += 1
                                e.tensor_tensor(dstc, lst[a2], lst[a2 + 1], ALU.add)
                                nxt[hh].append(dstc)
                            if len(lst) % 2 == 1:
                                nxt[hh].append(lst[-1])
                        e.drain()
                        srcs = nxt
                        lvl += 1
                    for hh in range(2):
                        e.tensor_copy(s.Dup[:, i * 2 + hh:i * 2 + hh + 1], srcs[hh][0])
                    e.drain()
                    continue
                j, t = it["j"], it["t"]
                if t == 4 * j + 3:
                    emit_evac(j)
                    if p == NPAIR - 1 and "CMB" in self.stages:
                        cmb_q.extend((j, b) for b in range(4))
                else:
                    drip_cmb()
            while cmb_q:
                drip_cmb(force=True)

    def emit_evac_fn(self, e, p):
        s, sem = self, self.sem
        PB = s.PB

        def emit_evac(j):
            prevspill = p * 4 + (5 - j) // 2 + 1
            if prevspill >= 1:
                e.wait_ge(sem["dS30" if j % 2 == 0 else "dS31"], 16 * prevspill)
            e.wait_ge(sem["tU2"], p * 64 + (7 - j) * 8 + 8)
            dcols = bass.AP(PB[2], 64, [[1024, 128], [512, 2], [128, 4]])
            e.tensor_tensor(
                s.dtot[:].rearrange("p (q h) -> p h q", h=2),
                dcols,
                s.Dup[:, 8 * j:8 * j + 8].rearrange("p (q h) -> p h q", h=2),
                ALU.add)
            e.drain()
            e.reciprocal(s.dinv[:], s.dtot[:])
            e.drain()
            for qt in range(4):
                for h in range(2):
                    e.tensor_scalar(
                        s.s_stage[:, (j % 2) * 512 + qt * 128 + h * 64:
                                  (j % 2) * 512 + qt * 128 + h * 64 + 64],
                        PB[2][:, h * 512 + qt * 128: h * 512 + qt * 128 + 64],
                        s.dinv[:, qt * 2 + h:qt * 2 + h + 1], None, ALU.mult)
            e.drain().then_inc(sem["vC2"], 8)
        return emit_evac

    def scalar_prog(self, e):
        s, sem = self, self.sem
        bar = sem["bar"]
        PB, PBb = s.PB, s.PBb
        xb = [s.stage[:, 0:1024], s.stage[:, 1024:2048]]
        if "P0" in self.stages:
            e.wait_ge(sem["dW"], 32)
            e.activation(s.sg_row[0:1, :], s.gates_in[0:1, :], AF.Sigmoid
                         ).then_inc(sem["aSg"], 1)
            e.wait_ge(sem["vSim2"], 1)
            e.activation(s.n2b[0:8, :], s.n2[0:8, :], AF.Ln)
            e.drain()
            e.activation(s.invn_r[0:8, :], s.n2b[0:8, :], AF.Exp, scale=-0.5
                         ).then_inc(sem["aSim"], 1)
            mvD = s.Dup[:].rearrange("p (i two) -> p two i", two=2)
            for g in range(16):
                e.wait_ge(sem["vM"], g + 1)
                e.activation(s.lnv[:, 2 * g:2 * g + 2],
                             s.veps[:, 2 * g:2 * g + 2], AF.Ln)
                e.drain()
                e.activation(s.rstd_all[:, 2 * g:2 * g + 2],
                             s.lnv[:, 2 * g:2 * g + 2], AF.Exp, scale=-0.5
                             ).then_inc(sem["aRS"], 1)
            e.activation(s.lna[:], mvD[:, 1, :], AF.Ln, scale=float(C)
                         ).then_inc(sem["aP0b1"], 1)
            e.wait_ge(sem["vP0b2"], 1)
            e.activation(s.invr_raw[:], s.ddt[:], AF.Exp, scale=0.5
                         ).then_inc(sem["aP0b2"], 1)
            lgT = s.R3[:, 8192:16384].bitcast(F32)
            for j in range(8):
                e.wait_ge(sem["tL"], j + 1)
                e.copy(lgT[0:8, j * 512:(j + 1) * 512],
                       PB[1][0:8, (j % 2) * 512:(j % 2) * 512 + 512]
                       ).then_inc(sem["aLT"], 1)
            e.wait_ge(sem["vRt1"], 1)
            e.activation(s.expg_all[:], s.gated_all[:], AF.Exp).then_inc(sem["aRt"], 1)
        e.drain()
        e.sem_inc(bar, 1)
        e.wait_ge(bar, 5)
        if "P1" in self.stages:
            for j in range(8):
                T = 4 * j + 4
                for t in range(T):
                    g = c2_g(j, t)
                    e.wait_ge(sem["tSf"], g + 1)
                    pbf = PB[0] if (g % 4) < 2 else PB[2]
                    e.activation(s.expC[:, (g % 4) * 512:(g % 4) * 512 + 512],
                                 pbf[:, (g % 2) * 512:(g % 2) * 512 + 512],
                                 AF.Exp).then_inc(sem["aEf"], 1)
            for qt in range(NT):
                e.wait_ge(sem["tO4"], qt + 1)
                if qt >= 2:
                    e.wait_ge(sem["dG0" if qt % 2 == 0 else "dG1"], 16 * (qt // 2))
                pbo = PB[0] if qt % 2 == 0 else PB[1]
                e.activation(s.gbuf[:, (qt % 2) * 1024:(qt % 2) * 1024 + 1024],
                             pbo[:, :], AF.Silu).then_inc(sem["aS4"], 1)
        e.drain()
        e.sem_inc(bar, 1)
        e.wait_ge(bar, 10)
        if "P3" in self.stages:
            for p in range(NPAIR):
                for it in P3_PAIRS[p]:
                    k = it["kind"]
                    if k == "C2":
                        es = it["eslot"]
                        e.wait_ge(sem["tC2"], it["tC2"])
                        e.activation(s.expCs(es),
                                     PB[[0, 1, 3][it["slot"]]][:, :],
                                     AF.Exp).then_inc(sem["aC2"], 1)
                    elif k == "C1":
                        kw = it["kw"]
                        m1 = it["aC1"]
                        if m1 == 1:
                            e.wait_ge(sem["dG0"], 16 * 16)
                            e.wait_ge(sem["dG1"], 16 * 16)
                        if m1 > 3:
                            e.wait_ge(sem["vC1r"], m1 - 3)
                        e.wait_ge(sem["tC1"], it["tC1"])
                        scr = self.c1scr((m1 - 1) % 3)
                        pbsrc = PB[[0, 1, 3][it["slot"]]]
                        if kw == 512:
                            e.activation(scr[:, 0:1024], pbsrc[:, 0:1024],
                                         AF.Exp).then_inc(sem["aC1"], 1)
                        else:
                            e.activation(
                                scr[:, 0:2 * kw].rearrange("p (h k) -> p h k", h=2),
                                pbsrc[:, 0:1024].rearrange("p (h k) -> p h k",
                                                           h=2)[:, :, 0:kw],
                                AF.Exp).then_inc(sem["aC1"], 1)
        e.sem_inc(bar, 1)
        e.wait_ge(bar, 15)


def build_program(debug=False, stages=("P0", "P1", "P3", "CMB")):
    return Builder(debug=debug, stages=stages)


def kernel(L0_attn_content, L0_attn_Wq, L0_attn_Wk, L0_attn_Wv,
           L0_ffn_sim_matrix, L0_ffn_gates, L0_ffn_q_proj, L0_ffn_k_proj,
           L0_ffn_v_proj, L0_ffn_o_proj):
    x = np.ascontiguousarray(L0_attn_content, dtype=np.float32)
    common = {
        "wq": np.ascontiguousarray(L0_attn_Wq, np.float32),
        "wk": np.ascontiguousarray(L0_attn_Wk, np.float32),
        "wv": np.ascontiguousarray(L0_attn_Wv, np.float32),
        "sim": np.ascontiguousarray(L0_ffn_sim_matrix, np.float32),
        "gates": np.ascontiguousarray(L0_ffn_gates, np.float32),
        "qp": np.ascontiguousarray(L0_ffn_q_proj, np.float32),
        "kp": np.ascontiguousarray(L0_ffn_k_proj, np.float32),
        "vp": np.ascontiguousarray(L0_ffn_v_proj, np.float32),
        "op": np.ascontiguousarray(L0_ffn_o_proj, np.float32),
    }
    in_maps = [dict(common, x=np.ascontiguousarray(x[i])) for i in range(8)]
    nc = build_program().build()
    res = run_bass_kernel_spmd(nc, in_maps, core_ids=list(range(8)))
    return np.stack([np.asarray(r["out"], np.float32) for r in res.results], axis=0)


# version 28
# speedup vs baseline: 1.0302x; 1.0302x over previous
import numpy as np
from contextlib import ExitStack

import concourse.bass as bass
import concourse.mybir as mybir
from concourse.bass_utils import run_bass_kernel_spmd

F32 = mybir.dt.float32
BF16 = mybir.dt.bfloat16
AF = mybir.ActivationFunctionType
ALU = mybir.AluOpType
AX = mybir.AxisListType

N, C, NT, CT = 4096, 1024, 32, 8
H, HD, NPAIR, E, D = 16, 64, 8, 8, 128
INV_SQRT_D = float(1.0 / np.sqrt(128.0))
QSCALE = 0.125


def c2_g(j, t):
    return 2 * j * j + 2 * j + t


C2_PER = c2_g(7, 31) + 1


def c1_chunk_list():
    lst = []
    for i in range(NT - 1):
        k = 128 * (i + 1)
        while k < N:
            kw = min(512, N - k)
            lst.append((i, k, kw))
            k += kw
    return lst


C1_LIST = c1_chunk_list()
C1_PER = len(C1_LIST)


def c1_items():
    per_i = {}
    for (i, k0, kw) in C1_LIST:
        per_i.setdefault(i, []).append((k0, kw))
    items = []
    for i in sorted(per_i.keys(), reverse=True):
        for c, (k0, kw) in enumerate(per_i[i]):
            items.append((i, c, k0, kw, c + 1 == len(per_i[i])))
    return items


C1_ITEMS = c1_items()
C1_NIT = len(C1_ITEMS)


def p3_sched():
    import math
    c2l = [(j, t) for j in range(7, -1, -1) for t in range(4 * j + 4)]
    need = {j: len([it for it in C1_ITEMS if it[0] >= 4 * j]) for j in range(8)}
    sched = []
    c1p = 0
    for k, (j, t) in enumerate(c2l):
        if t == 4 * j + 3:
            while c1p < need[j]:
                sched.append(("C1", C1_ITEMS[c1p]))
                c1p += 1
        sched.append(("C2", (j, t)))
        target = min(C1_NIT, math.ceil((k + 1) * C1_NIT / len(c2l)))
        while c1p < target:
            sched.append(("C1", C1_ITEMS[c1p]))
            c1p += 1
    assert c1p == C1_NIT
    return sched


SCHED = p3_sched()
C2_NIT = 144


def p3_counts():
    k2 = 0
    k1 = 0
    tu2 = 0
    info = []
    tu2_after_qchunk = {}
    pend = None
    for item in SCHED:
        kind, d = item
        if kind == "C2":
            if pend is not None:
                (pj, pt) = pend
                if pt >= 4 * pj:
                    tu2 += 2
                    if pt == 4 * pj + 3:
                        tu2_after_qchunk[pj] = tu2
            info.append((k2, tu2))
            pend = d
            k2 += 1
        else:
            info.append((k1, tu2))
            k1 += 1
    (pj, pt) = pend
    if pt >= 4 * pj:
        tu2 += 2
        if pt == 4 * pj + 3:
            tu2_after_qchunk[pj] = tu2
    return info, tu2_after_qchunk, tu2


P3_INFO, TU2_AFTER_Q, TU2_PER_PAIR = p3_counts()


def build_p3_items():
    cnt = dict(tC1=0, aC1=0, tC2=0, aC2=0, tQ3=0, vQ3=0, tV3=0, vV3=0)
    pool_rel = [None, None, None]
    pool_pos = [0]

    def qkv_items(pp):
        out = []
        for n in range(8):
            out.append({"kind": "QKVa", "pp": pp, "n": n})
            out.append({"kind": "QKVb", "pp": pp, "n": n})
        return out

    def resolve(seq):
        res = []
        for it in seq:
            it = dict(it)
            k = it["kind"]
            slot = pool_pos[0] % 3
            pool_pos[0] += 1
            it["slot"] = slot
            it["pool_wait"] = pool_rel[slot]
            if k == "C1":
                cnt["tC1"] += 1
                cnt["aC1"] += 1
                it["tC1"] = cnt["tC1"]
                it["aC1"] = cnt["aC1"]
                pool_rel[slot] = ("aC1", cnt["aC1"])
            elif k == "QKVa":
                cnt["tQ3"] += 1
                cnt["vQ3"] += 1
                it["tQ3"] = cnt["tQ3"]
                it["vQ3"] = cnt["vQ3"]
                pool_rel[slot] = ("vQ3", cnt["vQ3"])
            elif k == "QKVb":
                cnt["tQ3"] += 1
                cnt["vQ3"] += 1
                cnt["tV3"] += 1
                cnt["vV3"] += 1
                it["tQ3"] = cnt["tQ3"]
                it["vQ3"] = cnt["vQ3"]
                it["tV3"] = cnt["tV3"]
                it["vV3"] = cnt["vV3"]
                pool_rel[slot] = ("vV3", cnt["vV3"])
            else:
                cnt["tC2"] += 1
                cnt["aC2"] += 1
                it["tC2"] = cnt["tC2"]
                it["aC2"] = cnt["aC2"]
                it["eslot"] = (cnt["tC2"] - 1) % 3
                pool_rel[slot] = ("aC2", cnt["aC2"])
            res.append(it)
        return res

    pre = resolve(qkv_items(0))
    pairs = []
    for p in range(NPAIR):
        base = []
        for idx, (kind, d) in enumerate(SCHED):
            if kind == "C2":
                base.append({"kind": "C2", "j": d[0], "t": d[1],
                             "k2": P3_INFO[idx][0]})
            else:
                (i, c, k0, kw, lastf) = d
                base.append({"kind": "C1", "i": i, "c": c, "k0": k0,
                             "kw": kw, "lastf": lastf,
                             "k1": P3_INFO[idx][0]})
        if p < NPAIR - 1:
            qk = qkv_items(p + 1)
            merged = []
            qi = 0
            for idx2, it in enumerate(base):
                merged.append(it)
                if qi < len(qk) and idx2 >= 30 and (idx2 - 30) % 14 == 0:
                    merged.append(qk[qi])
                    qi += 1
                    if qi < len(qk) and (idx2 - 30) % 28 == 0:
                        merged.append(qk[qi])
                        qi += 1
            while qi < len(qk):
                merged.append(qk[qi])
                qi += 1
            base = merged
        pairs.append(resolve(base))
    return pre, pairs


P3_PRE, P3_PAIRS = build_p3_items()


class Builder:
    def __init__(self, debug=False, stages=("P0", "P1", "P3", "CMB")):
        self.debug = debug
        self.stages = set(stages)
        nc = bass.Bass("TRN2", target_bir_lowering=False)
        self.nc = nc
        d = {}
        d["x"] = nc.dram_tensor("x", [N, C], F32, kind="ExternalInput").ap()
        d["wq"] = nc.dram_tensor("wq", [C, C], F32, kind="ExternalInput").ap()
        d["wk"] = nc.dram_tensor("wk", [C, C], F32, kind="ExternalInput").ap()
        d["wv"] = nc.dram_tensor("wv", [C, C], F32, kind="ExternalInput").ap()
        d["sim"] = nc.dram_tensor("sim", [C, E], F32, kind="ExternalInput").ap()
        d["gates"] = nc.dram_tensor("gates", [E], F32, kind="ExternalInput").ap()
        d["qp"] = nc.dram_tensor("qp", [E, C, D], F32, kind="ExternalInput").ap()
        d["kp"] = nc.dram_tensor("kp", [E, C, D], F32, kind="ExternalInput").ap()
        d["vp"] = nc.dram_tensor("vp", [E, C, D], F32, kind="ExternalInput").ap()
        d["op"] = nc.dram_tensor("op", [E, D, C], F32, kind="ExternalInput").ap()
        d["out"] = nc.dram_tensor("out", [N, C], F32, kind="ExternalOutput").ap()
        sk = "ExternalOutput" if debug else "Internal"
        d["g_sp"] = nc.dram_tensor("g_sp", [N, C], BF16, kind=sk).ap()
        d["s_sp"] = nc.dram_tensor("s_sp", [N, 1024], BF16, kind=sk).ap()
        if debug:
            for nm, sh, dt in [
                ("dbg_lnat", [128, 256], F32), ("dbg_mask", [128, 256], F32),
                ("dbg_rw", [128, 256], F32), ("dbg_invr", [128, 32], F32),
                ("dbg_mv", [128, 64], F32), ("dbg_xnT", [128, 4096], BF16),
                ("dbg_snn", [128, 64], BF16), ("dbg_qfT", [128, 4096], BF16),
                ("dbg_kfT", [128, 4096], BF16), ("dbg_vfa", [128, 4160], BF16),
                ("dbg_ao", [128, 4096], BF16), ("dbg_rwp", [128, 256], F32),
                ("dbg_QT", [128, 4096], BF16), ("dbg_KT", [128, 4096], BF16),
                ("dbg_Vau", [128, 4160], BF16), ("dbg_dup", [128, 64], F32),
            ]:
                d[nm] = nc.dram_tensor(nm, sh, dt, kind="ExternalOutput").ap()
        self.d = d

    def alloc(self, st):
        nc = self.nc
        sb = lambda n, s_, dt=F32: st.enter_context(nc.sbuf_tensor(n, s_, dt))
        ps = lambda n, s_, dt=F32: st.enter_context(nc.psum_tensor(n, s_, dt))
        sem = lambda n: st.enter_context(nc.semaphore(n))
        s = self
        s.xnT = sb("xnT", [128, 32768], BF16)
        s.R2 = sb("R2", [128, 24576], BF16)
        s.R3 = sb("R3", [128, 17408], BF16)
        s.QT = sb("QTt", [128, 4096], BF16)
        s.KT = sb("KTt", [128, 4096], BF16)
        s.Vau = sb("Vau", [128, 4160], BF16)
        s.stage = sb("stagef", [128, 4096], F32)
        s.s1_all = sb("s1_all", [128, 32]); s.acc_all = sb("acc_all", [128, 32])
        s.m_all = sb("m_all", [128, 32]); s.v_all = sb("v_all", [128, 32])
        s.lnv = sb("lnv", [128, 32]); s.lna = sb("lna", [128, 32])
        s.ddt = sb("ddt", [128, 32]); s.rstd_all = sb("rstd_all", [128, 32])
        s.invr_raw = sb("invr_raw", [128, 32]); s.invr_all = sb("invr_all", [128, 32])
        s.lnat = sb("lnat", [128, 256]); s.msk_all = sb("msk_all", [128, 256])
        s.gated_all = sb("gated_all", [128, 256]); s.expg_all = sb("expg_all", [128, 256])

        s.mq_all = sb("mq_all", [128, 256]); s.rwp_all = sb("rwp_all", [128, 256])
        s.Z_all = sb("Z_all", [128, 32]); s.Zi_all = sb("Zi_all", [128, 32])
        s.veps = sb("veps", [128, 32])
        s.Dup = sb("Dup", [128, 64])
        s.tmpc = sb("tmpc", [128, 16]); s.tca = sb("tca", [128, 16])
        s.sim_in = sb("sim_in", [128, 64])

        s.n2 = sb("n2", [8, 1]); s.n2b = sb("n2b", [8, 1])
        s.invn_r = sb("invn_r", [8, 1]); s.invn = sb("invn", [8, 1])
        s.snn = sb("snn", [128, 64], BF16)
        s.gates_in = sb("gates_in", [1, 8]); s.sg_row = sb("sg_row", [1, 8])
        s.sgb = sb("sgb", [128, 8]); s.ones1 = sb("ones1", [1, 128])
        s.onesb = sb("onesb", [128, 128], BF16); s.idt_bf = sb("idt_bf", [128, 128], BF16)
        s.idt_f = sb("idt_f", [128, 128])
        s.qk_tmp = sb("qk_tmp", [128, 1024], BF16)
        s.s_stage = sb("s_stage", [128, 1024], BF16)
        s.dinv = sb("dinv", [128, 8]); s.dtot = sb("dtot", [128, 8])
        s.cacc = sb("cacc", [128, 128]); s.cacc2 = sb("cacc2", [128, 128])
        s.mask2 = sb("mask2", [128, 1024], BF16)
        s.expC = sb("expC", [128, 2048], BF16)
        s.rt1 = sb("rt1", [128, 8]); s.rt2 = sb("rt2", [128, 8]); s.rt3 = sb("rt3", [128, 8])
        s.rt4 = sb("rt4", [128, 8]); s.rt5 = sb("rt5", [128, 8]); s.rti = sb("rti", [128, 1])
        s.rtm = sb("rtm", [128, 2])
        s.PB = [ps(f"PB{i}", [128, 1024], F32) for i in range(4)]
        names = [
            "bar", "dX0", "dX1", "dX20", "dX21", "dW", "dP0", "dP1",
            "dOp0", "dOp1", "dWgt0", "dWgt1", "dG0", "dG1", "dS30", "dS31",
            "dCX0", "dCX1", "dCG0", "dCG1", "dCS0", "dCS1", "dOut0", "dOut1", "dDbg",
            "vInit", "gInit", "vP0a", "aSQ", "vP0b", "aP0b1", "vP0b2", "aP0b2", "vP0b3",
            "vXN", "tTR", "vEv", "tSim", "vSim1", "vSim2", "aSim", "vSnn3", "tSnn", "vSnnF",
            "aSg", "tSgb", "vSgb", "tL", "aLT", "tLN", "vLNat", "vRt1", "aRt", "vRt2", "vPc",
            "vOpc",
            "tA2q", "tA2k", "tA2v", "vA2q", "vA2k", "vA2v", "tT2", "aT2",
            "sCq", "sCk", "sCv",
            "tSf", "aEf", "gMf", "tUf", "vUf",
            "vA4", "tT4", "aE4", "tO4", "aS4",
            "vWc", "tQ3", "vQ3", "tV3", "vV3",
            "tC1", "aC1", "vC1",
            "tC2", "aC2", "gC2", "tU2", "vC2", "vC1r", "vM", "aRS", "gCM",
            "vCM",
        ]
        s.sem = {n: sem(n) for n in names}
        s.PBb = [p[:, 0:512].bitcast(BF16) for p in s.PB] + \
                [p[:, 512:1024].bitcast(BF16) for p in s.PB]
        s.cacc3 = None
        s.simT = s.R3[:, 8192:10240].bitcast(F32)[0:8, :]
        s.sq8 = s.s_stage[0:8, 0:1024]
        s.cacc3 = s.stage[:, 3072:3200]
        s.gbuf = s.stage[:, 3072:4096].bitcast(BF16)
        s.expC3 = s.stage[:, 2048:2560].bitcast(BF16)
        s.c1x = s.stage[:, 2560:3072].bitcast(BF16)
        s.dex = s.gbuf
        s.xnb = s.R2[:, 22528:24576]
        s.maskedf = s.mask2[:, 0:512]
        s.vtt = s.qk_tmp[:, 0:512]
        s.snnT = s.R3[0:8, 16384:17408]
        s.rwn_all = s.gated_all
        s.rw_all = s.expg_all

    def build(self):
        with ExitStack() as st:
            self.alloc(st)
            blk = st.enter_context(self.nc.Block())

            @blk.sync
            def _(e):
                self.sync_prog(e)

            @blk.tensor
            def _(e):
                self.tensor_prog(e)

            @blk.vector
            def _(e):
                self.vector_prog(e)

            @blk.scalar
            def _(e):
                self.scalar_prog(e)

            @blk.gpsimd
            def _(e):
                self.gpsimd_prog(e)
        return self.nc

    def sync_prog(self, e):
        s, d, sem = self, self.d, self.sem
        bar = sem["bar"]
        xb = [s.stage[:, 0:1024], s.stage[:, 1024:2048]]
        xr = d["x"].rearrange("(i p) c -> p i c", p=128)
        if "P0" in self.stages:
            e.dma_start(s.sim_in[:].rearrange("p (a e) -> p a e", a=8),
                        d["sim"].rearrange("(a p) e -> p a e", p=128)).then_inc(sem["dW"], 16)
            e.dma_start(s.gates_in[0:1, :],
                        d["gates"].rearrange("(a e) -> a e", a=1)).then_inc(sem["dW"], 16)
            xb3 = [s.stage[:, 0:1024], s.stage[:, 1024:2048],
                   s.stage[:, 2048:3072], s.stage[:, 3072:4096]]
            dxs = ["dX0", "dX1", "dX20", "dX21"]
            for i in range(NT):
                if i >= 4:
                    e.wait_ge(sem["vXN"], i - 3)
                e.dma_start(xb3[i % 4], xr[:, i, :]).then_inc(sem[dxs[i % 4]], 16)
        e.sem_inc(bar, 1)
        e.wait_ge(bar, 5)
        if "P1" in self.stages:
            gr = d["g_sp"].rearrange("(i p) c -> p i c", p=128)
            for qt in range(NT):
                e.wait_ge(sem["aS4"], qt + 1)
                e.dma_start(gr[:, qt, :],
                            s.gbuf[:, (qt % 2) * 1024:(qt % 2) * 1024 + 1024]
                            ).then_inc(sem["dG0" if qt % 2 == 0 else "dG1"], 16)
        e.sem_inc(bar, 1)
        e.wait_ge(bar, 10)
        if "P3" in self.stages:
            if "P1" in self.stages:
                e.wait_ge(sem["dG0"], 16 * 16)
                e.wait_ge(sem["dG1"], 16 * 16)
            wmap = {0: d["wq"], 1: d["wk"], 2: d["wv"]}
            for c in range(12):
                w = wmap[c // 4]
                cc = c % 4
                if c >= 2:
                    e.wait_ge(sem["vWc"], c - 1)
                e.dma_start(s.stage[:, (c % 2) * 2048:(c % 2) * 2048 + 2048]
                            .rearrange("p (a dd) -> p a dd", a=2),
                            w.rearrange("(a p) dd -> p a dd", p=128)[:, 2 * cc:2 * cc + 2, :]
                            ).then_inc(sem["dWgt0" if c % 2 == 0 else "dWgt1"], 16)
        e.sem_inc(bar, 1)
        if "CMB" in self.stages:
            outr = d["out"].rearrange("(i p) c -> p i c", p=128)
            vauo = s.Vau.bitcast(F32)
            e.wait_ge(sem["vWc"], 12)
            for k2 in range(NT):
                i = NT - 1 - k2
                if k2 >= 2:
                    e.wait_ge(sem["vCM"], k2 - 1)
                e.dma_start(xb[k2 % 2], xr[:, i, :]
                            ).then_inc(sem["dCX0" if k2 % 2 == 0 else "dCX1"], 16)
                if k2 >= 1:
                    e.wait_ge(sem["vCM"], k2)
                    e.dma_start(outr[:, i + 1, :],
                                vauo[:, ((k2 - 1) % 2) * 1024:
                                     ((k2 - 1) % 2) * 1024 + 1024]
                                ).then_inc(sem["dOut0" if (k2 - 1) % 2 == 0
                                           else "dOut1"], 16)
            e.wait_ge(sem["vCM"], NT)
            e.dma_start(outr[:, 0, :],
                        vauo[:, ((NT - 1) % 2) * 1024:((NT - 1) % 2) * 1024 + 1024]
                        ).then_inc(sem["dOut0" if (NT - 1) % 2 == 0 else "dOut1"], 16)
        e.wait_ge(bar, 15)
        if self.debug:
            self.debug_dmas(e)
        finals = {}
        if "P0" in self.stages:
            finals.update(dX0=16 * 8, dX1=16 * 8, dX20=16 * 8, dX21=16 * 8,
                          dW=32, dP0=16 * 12, dP1=16 * 12)
        if "P1" in self.stages:
            finals.update(dOp0=16 * 2, dOp1=16 * 2, dG0=16 * 16, dG1=16 * 16)
        if "P3" in self.stages:
            finals.update(dWgt0=16 * 6, dWgt1=16 * 6, dS30=16 * 32, dS31=16 * 32)
        if "CMB" in self.stages:
            finals.update(dCX0=16 * 16, dCX1=16 * 16, dCG0=16 * 16, dCG1=16 * 16,
                          dCS0=16 * 16, dCS1=16 * 16,
                          dOut0=16 * 16, dOut1=16 * 16)
        for k, v in finals.items():
            e.wait_ge(sem[k], v)

    def debug_dmas(self, e):
        s, d, sem = self, self.d, self.sem
        n = 0
        pairs = []
        if "P0" in self.stages:
            pairs += [("dbg_lnat", s.lnat[:]), ("dbg_mask", s.msk_all[:]),
                      ("dbg_rw", s.rw_all[:]), ("dbg_invr", s.invr_all[:]),
                      ("dbg_snn", s.snn[:]), ("dbg_xnT", s.xnT[:, 0:4096])]
        if "P1" in self.stages:
            pairs += [("dbg_qfT", s.R3[:, 0:4096]), ("dbg_kfT", s.R3[:, 4096:8192]),
                      ("dbg_vfa", s.R3[:, 8192:12352]), ("dbg_ao", s.R3[:, 12352:16448]),
                      ("dbg_rwp", s.rwp_all[:])]
        if "P3" in self.stages:
            pairs += [("dbg_QT", s.QT[:]), ("dbg_KT", s.KT[:]),
                      ("dbg_Vau", s.Vau[:]), ("dbg_dup", s.Dup[:])]
        for nm, src in pairs:
            e.dma_start(d[nm][:, :], src).then_inc(sem["dDbg"], 16)
            n += 1
        if "P0" in self.stages:
            e.dma_start(d["dbg_mv"][:, 0:64], s.Dup[:]).then_inc(sem["dDbg"], 16)
            n += 1
        e.wait_ge(sem["dDbg"], 16 * n)

    def gpsimd_prog(self, e):
        s, d, sem = self, self.d, self.sem
        bar = sem["bar"]
        e.memset(s.onesb[:], 1.0)
        e.drain()
        e.affine_select(s.idt_bf[:], s.onesb[:], [[1, 128]], ALU.is_equal, 0.0,
                        base=0, channel_multiplier=-1)
        e.drain()
        e.engine_nop().then_inc(sem["gInit"], 1)
        if "P0" in self.stages:
            pmap = {0: d["qp"], 1: d["kp"], 2: d["vp"]}
            R3f = [s.R3[:, 0:8192].bitcast(F32)[:, 0:1024],
                   s.R3[:, 0:8192].bitcast(F32)[:, 1024:2048]]
            for idx in range(24):
                w = pmap[idx // 8]
                a = idx % 8
                if idx >= 2:
                    e.wait_ge(sem["vPc"], idx - 1)
                e.dma_start(R3f[idx % 2],
                            w.rearrange("ee (a p) dd -> p a ee dd", p=128)
                            [:, a:a + 1, :, :]).then_inc(sem["dP0" if idx % 2 == 0 else "dP1"], 16)
        e.drain()
        e.sem_inc(bar, 1)
        e.wait_ge(bar, 5)
        if "P1" in self.stages:
            for j in range(8):
                if j >= 1:
                    e.wait_ge(sem["vUf"], j * 4)
                for qt in range(4):
                    t = 4 * j + qt
                    g = c2_g(j, t)
                    e.wait_ge(sem["aEf"], g + 1)
                    e.affine_select(
                        s.maskedf[:, qt * 128:qt * 128 + 128],
                        s.expC[:, (g % 4) * 512 + qt * 128:(g % 4) * 512 + qt * 128 + 128],
                        [[1, 128]], ALU.is_ge, 0.0,
                        base=0, channel_multiplier=-1).then_inc(sem["gMf"], 1)
            e.wait_ge(sem["sCv"], 8 * NT)
            for c in range(4):
                if c >= 2:
                    e.wait_ge(sem["vOpc"], c - 1)
                e.dma_start(s.stage[:, (c % 2) * 2048:(c % 2) * 2048 + 2048]
                            .rearrange("p (ee cc) -> p ee cc", ee=2),
                            d["op"].rearrange("ee p cc -> p ee cc")[:, 2 * c:2 * c + 2, :]
                            ).then_inc(sem["dOp0" if c % 2 == 0 else "dOp1"], 16)
        e.drain()
        e.sem_inc(bar, 1)
        e.wait_ge(bar, 10)
        if "P3" in self.stages:
            for p in range(NPAIR):
                for it in P3_PAIRS[p]:
                    if it["kind"] != "C2":
                        continue
                    j, t = it["j"], it["t"]
                    if t >= 4 * j:
                        qt = t - 4 * j
                        es = it["eslot"]
                        e.wait_ge(sem["aC2"], it["aC2"])
                        for h in range(2):
                            prevu = p * 64 + (6 - j) * 8 + qt * 2 + h + 1
                            if prevu >= 1:
                                e.wait_ge(sem["tU2"], prevu)
                            e.affine_select(
                                s.mask2[:, h * 512 + qt * 128: h * 512 + qt * 128 + 128],
                                s.expCs(es)[:, h * 512 + qt * 128:
                                            h * 512 + qt * 128 + 128],
                                [[1, 128]], ALU.is_ge, 0.0,
                                base=0, channel_multiplier=-1).then_inc(sem["gC2"], 1)
                    if t == 4 * j + 3:
                        e.wait_ge(sem["vC2"], p * 64 + (8 - j) * 8)
                        e.dma_start(
                            self.d["s_sp"].rearrange(
                                "(jj qt p2) (pr h dd) -> p2 jj qt pr h dd",
                                p2=128, jj=8, pr=8, h=2)[:, j, :, p, :, :],
                            s.s_stage[:, (j % 2) * 512:(j % 2) * 512 + 512]
                            .rearrange("p (qt h dd) -> p qt h dd", qt=4, h=2)
                        ).then_inc(sem["dS30" if j % 2 == 0 else "dS31"], 16)
                        if p == NPAIR - 1 and "CMB" in self.stages:
                            self.gpsimd_cmb_q(e, j)
        e.drain()
        e.sem_inc(bar, 1)
        e.wait_ge(bar, 15)

    def vector_cmb_one(self, e, j, b):
        s, sem = self, self.sem
        xb = [s.stage[:, 0:1024], s.stage[:, 1024:2048]]
        vauo = s.Vau.bitcast(F32)
        k2 = (7 - j) * 4 + b
        sl = k2 % 2
        e.wait_ge(sem["dCX0" if sl == 0 else "dCX1"], 16 * (k2 // 2 + 1))
        e.wait_ge(sem["dCG0" if sl == 0 else "dCG1"], 16 * (k2 // 2 + 1))
        e.wait_ge(sem["dCS0" if sl == 0 else "dCS1"], 16 * (k2 // 2 + 1))
        if k2 >= 2:
            e.wait_ge(sem["dOut0" if sl == 0 else "dOut1"], 16 * (k2 // 2))
        e.tensor_tensor(s.xnb[:, sl * 1024:sl * 1024 + 1024],
                        s.KT[:, sl * 1024:sl * 1024 + 1024],
                        s.QT[:, sl * 1024:sl * 1024 + 1024],
                        ALU.mult)
        e.drain()
        e.tensor_tensor(vauo[:, sl * 1024:sl * 1024 + 1024],
                        s.xnb[:, sl * 1024:sl * 1024 + 1024],
                        xb[sl], ALU.add).then_inc(sem["vCM"], 1)

    def gpsimd_cmb_q(self, e, j):
        s, d, sem = self, self.d, self.sem
        gr = d["g_sp"].rearrange("(i p) c -> p i c", p=128)
        sr = d["s_sp"].rearrange("(i2 p2) dd -> p2 i2 dd", p2=128)
        spar = j % 2
        scount = 16 * (28 + ((6 if spar == 0 else 7) - j) // 2 + 1)
        for b in range(4):
            k2 = (7 - j) * 4 + b
            i = NT - 1 - k2
            sl = k2 % 2
            if k2 == 0:
                e.wait_ge(sem["dG0"], 16 * 16)
                e.wait_ge(sem["dG1"], 16 * 16)
            if k2 >= 2:
                e.wait_ge(sem["vCM"], k2 - 1)
            e.wait_ge(sem["dS30" if spar == 0 else "dS31"], scount)
            e.dma_start(s.QT[:, sl * 1024:sl * 1024 + 1024],
                        gr[:, i, :]).then_inc(sem["dCG0" if sl == 0
                                              else "dCG1"], 16)
            e.dma_start(s.KT[:, sl * 1024:sl * 1024 + 1024],
                        sr[:, i, :]).then_inc(sem["dCS0" if sl == 0
                                              else "dCS1"], 16)


    def tensor_prog(self, e):
        s, sem = self, self.sem
        bar = sem["bar"]
        PB, PBb = s.PB, s.PBb
        if "P0" in self.stages:
            e.wait_ge(sem["gInit"], 1)
            e.wait_ge(sem["vInit"], 1)
            e.wait_ge(sem["dW"], 32)
            last = None
            for a in range(8):
                last = e.transpose(PB[0][0:8, a * 128:(a + 1) * 128],
                                   s.sim_in[:, a * 8:(a + 1) * 8], s.idt_f[:])
            last.then_inc(sem["tSim"], 1)
            for i in range(NT):
                e.wait_ge(sem["vXN"], i + 1)
                if i >= 2:
                    e.wait_ge(sem["vEv"], i - 1)
                slot = PBb[3] if i % 2 == 0 else PBb[7]
                last = None
                for ct in range(8):
                    last = e.transpose(slot[:, ct * 128:(ct + 1) * 128],
                                       s.xnb[:, (i % 2) * 1024 + ct * 128:
                                             (i % 2) * 1024 + (ct + 1) * 128],
                                       s.idt_bf[:])
                last.then_inc(sem["tTR"], 1)
                if i == 3:
                    e.wait_ge(sem["vSnn3"], 1)
                    last = None
                    for a in range(8):
                        last = e.transpose(PBb[2][:, a * 8:(a + 1) * 8],
                                           s.snnT[0:8, a * 128:(a + 1) * 128],
                                           s.idt_bf[0:8, 0:8])
                    last.then_inc(sem["tSnn"], 1)
                    e.wait_ge(sem["aSg"], 1)
                    e.wait_ge(sem["vInit"], 1)
                    e.matmul(PB[1][:, 0:8], s.ones1[0:1, :], s.sg_row[0:1, :],
                             start=True, stop=True).then_inc(sem["tSgb"], 1)
            e.wait_ge(sem["vEv"], NT)
            e.wait_ge(sem["vSnnF"], 1)
            e.wait_ge(sem["vSgb"], 1)
            for j in range(8):
                if j >= 2:
                    e.wait_ge(sem["aLT"], j - 1)
                last = None
                for ct in range(8):
                    last = e.matmul(PB[1][0:8, (j % 2) * 512:(j % 2) * 512 + 512],
                                    s.snn[:, ct * 8:(ct + 1) * 8],
                                    s.xnT[:, ct * 4096 + j * 512: ct * 4096 + (j + 1) * 512],
                                    start=(ct == 0), stop=(ct == 7))
                last.then_inc(sem["tL"], 1)
            lgT = s.R3[:, 8192:16384].bitcast(F32)
            e.wait_ge(sem["aLT"], 8)
            e.wait_ge(sem["vSgb"], 1)
            last = None
            for i in range(NT):
                last = e.transpose(PB[2][:, i * 8:(i + 1) * 8],
                                   lgT[0:8, i * 128:(i + 1) * 128], s.idt_f[0:8, 0:8])
            last.then_inc(sem["tLN"], 1)
        e.sem_inc(bar, 1)
        if "P1" in self.stages:
            if "P0" in self.stages:
                e.wait_ge(sem["vSim1"], 1)
                e.wait_ge(sem["aLT"], 8)
                e.wait_ge(sem["vSgb"], 1)
                e.wait_ge(sem["vLNat"], 1)
                e.wait_ge(sem["vEv"], NT)
                e.wait_ge(sem["vPc"], 24)
            else:
                e.wait_ge(bar, 5)
            self.tensor_p1(e)
        e.sem_inc(bar, 1)
        e.wait_ge(bar, 10)
        if "P3" in self.stages:
            self.tensor_p3(e)
        e.sem_inc(bar, 1)
        e.wait_ge(bar, 15)

    def tensor_p1(self, e):
        s, sem = self, self.sem
        PB, PBb = s.PB, s.PBb
        qp_r = s.R2[:, 0:8192]
        kp_r = s.R2[:, 8192:16384]
        vp_r = s.R2[:, 16384:24576]
        qfT = s.R3[:, 0:4096]
        kfT = s.R3[:, 4096:8192]
        vfa = s.R3[:, 8192:12352]

        def trans_qk(qt):
            e.wait_ge(sem["sCk"], 8 * (qt + 1))
            if qt >= 2:
                e.wait_ge(sem["aT2"], qt - 1)
            slot = PBb[3] if qt % 2 == 0 else PBb[7]
            e.transpose(slot[:, 0:128],
                        s.qk_tmp[:, (qt % 2) * 512:(qt % 2) * 512 + 128], s.idt_bf[:])
            e.transpose(slot[:, 128:256],
                        s.qk_tmp[:, (qt % 2) * 512 + 128:(qt % 2) * 512 + 256],
                        s.idt_bf[:]).then_inc(sem["tT2"], 1)

        for qt in range(NT):
            for src, pb, tname, vname in ((qp_r, 0, "tA2q", "vA2q"),
                                          (kp_r, 1, "tA2k", "vA2k"),
                                          (vp_r, 2, "tA2v", "vA2v")):
                if qt >= 1:
                    e.wait_ge(sem[{"tA2q": "sCq", "tA2k": "sCk", "tA2v": "sCv"}[tname]], 8 * qt)
                last = None
                for ct in range(8):
                    for half in range(2):
                        last = e.matmul(
                            PB[pb][:, half * 512:(half + 1) * 512],
                            s.xnT[:, ct * 4096 + qt * 128: ct * 4096 + (qt + 1) * 128],
                            src[:, ct * 1024 + half * 512: ct * 1024 + (half + 1) * 512],
                            start=(ct == 0), stop=(ct == 7), skip_group_check=True)
                last.then_inc(sem[tname], 1)
            if qt >= 1:
                trans_qk(qt - 1)
        trans_qk(NT - 1)
        e.wait_ge(sem["aT2"], NT)
        e.wait_ge(sem["sCq"], 8 * NT)
        e.wait_ge(sem["sCv"], 8 * NT)

        SCF = [PB[0][:, 0:512], PB[0][:, 512:1024],
               PB[2][:, 0:512], PB[2][:, 512:1024]]

        def pv_f(j, t):
            g = c2_g(j, t)
            e.wait_ge(sem["aEf"], g + 1)
            if t == 0 and j >= 1:
                e.wait_ge(sem["vUf"], j * 4)
            for qt in range(4):
                qtile = 4 * j + qt
                if t > qtile:
                    continue
                first = (t == 0 and qt in (0, 2))
                if t == qtile:
                    e.wait_ge(sem["gMf"], j * 4 + qt + 1)
                    stat = s.maskedf[:, qt * 128:qt * 128 + 128]
                else:
                    stat = s.expC[:, (g % 4) * 512 + qt * 128:(g % 4) * 512 + qt * 128 + 128]
                mm = e.matmul(PB[1][:, qt * 256:qt * 256 + 129],
                              stat, vfa[:, t * 130:t * 130 + 129],
                              start=first, stop=(t == qtile), skip_group_check=True)
                if t == qtile:
                    mm.then_inc(sem["tUf"], 1)

        pendf = []
        for j in range(8):
            T = 4 * j + 4
            for t in range(T):
                g = c2_g(j, t)
                if g >= 4:
                    e.wait_ge(sem["aEf"], g - 3)
                e.matmul(SCF[g % 4],
                         kfT[:, t * 128:(t + 1) * 128],
                         qfT[:, j * 512:(j + 1) * 512],
                         start=True, stop=True).then_inc(sem["tSf"], 1)
                if len(pendf) >= 2:
                    pv_f(*pendf.pop(0))
                pendf.append((j, t))
        for pv in pendf:
            pv_f(*pv)
        op_r = s.R2[:, 0:8192]
        aoT = [s.Vau[:, 0:1024], s.Vau[:, 1024:2048]]
        e.wait_ge(sem["vOpc"], 4)
        e.wait_ge(sem["aEf"], C2_PER)
        e.wait_ge(sem["vUf"], NT)
        e.wait_ge(sem["aT2"], NT)
        for qt in range(NT):
            e.wait_ge(sem["vA4"], qt + 1)
            if qt >= 2:
                e.wait_ge(sem["aE4"], qt - 1)
            slot = PBb[3] if qt % 2 == 0 else PBb[7]
            last = None
            for ei in range(8):
                last = e.transpose(slot[:, ei * 128:(ei + 1) * 128],
                                   s.xnb[:, (qt % 2) * 1024 + ei * 128:
                                         (qt % 2) * 1024 + (ei + 1) * 128],
                                   s.idt_bf[:])
            last.then_inc(sem["tT4"], 1)
            e.wait_ge(sem["aE4"], qt + 1)
            if qt >= 2:
                e.wait_ge(sem["aS4"], qt - 1)
            pbo = PB[0] if qt % 2 == 0 else PB[1]
            last = None
            for ei in range(8):
                for half in range(2):
                    last = e.matmul(pbo[:, half * 512:(half + 1) * 512],
                                    aoT[qt % 2][:, ei * 128:(ei + 1) * 128],
                                    op_r[:, ei * 1024 + half * 512:
                                         ei * 1024 + (half + 1) * 512],
                                    start=(ei == 0), stop=(ei == 7),
                                    skip_group_check=True)
            last.then_inc(sem["tO4"], 1)

    def c1scr(self, s3):
        return (self.dex[:, 0:1024] if s3 == 0 else
                self.dex[:, 1024:2048] if s3 == 1 else
                self.R3[:, 12352:13376] if s3 == 2 else
                self.c1x[:, 0:1024])

    def expCs(self, es):
        return (self.expC[:, es * 1024:es * 1024 + 1024] if es < 2
                else self.expC3[:, 0:1024])

    def tensor_p3(self, e):
        s, sem = self, self.sem
        PB, PBb = s.PB, s.PBb
        Wq = s.R2[:, 0:8192]
        Wk = s.R2[:, 8192:16384]
        Wv = s.R2[:, 16384:24576]
        QTb = [s.QT, s.R3[:, 0:4096]]
        KTb = [s.KT, s.R3[:, 4096:8192]]
        Vaub = [s.Vau, s.R3[:, 8192:12352]]

        POOL = [PB[0], PB[1], PB[3]]
        POOLB = [PBb[4], PBb[5], PBb[7]]

        def pb3_wait(it):
            w = it.get("pool_wait")
            if w is not None:
                e.wait_ge(sem[w[0]], w[1])

        def emit_qkva(it):
            pp, n = it["pp"], it["n"]
            pb3_wait(it)
            last = None
            for ct in range(8):
                mv = s.xnT[:, ct * 4096 + n * 512: ct * 4096 + (n + 1) * 512]
                e.matmul(POOL[it["slot"]][:, 0:512],
                         Wq[:, ct * 1024 + pp * 128: ct * 1024 + (pp + 1) * 128],
                         mv, start=(ct == 0), stop=(ct == 7), skip_group_check=True)
                last = e.matmul(POOL[it["slot"]][:, 512:1024],
                                Wk[:, ct * 1024 + pp * 128: ct * 1024 + (pp + 1) * 128],
                                mv, start=(ct == 0), stop=(ct == 7),
                                skip_group_check=True)
            last.then_inc(sem["tQ3"], 1)

        def emit_qkvb(it):
            pp, n = it["pp"], it["n"]
            pb3_wait(it)
            last = None
            for ct in range(8):
                mv = s.xnT[:, ct * 4096 + n * 512: ct * 4096 + (n + 1) * 512]
                last = e.matmul(POOL[it["slot"]][:, 0:512],
                                Wv[:, ct * 1024 + pp * 128: ct * 1024 + (pp + 1) * 128],
                                mv, start=(ct == 0), stop=(ct == 7),
                                skip_group_check=True)
            last.then_inc(sem["tQ3"], 1)
            e.wait_ge(sem["vQ3"], it["vQ3"])
            last = None
            for b in range(4):
                last = e.transpose(POOLB[it["slot"]][:, b * 128:(b + 1) * 128],
                                   s.vtt[:, b * 128:(b + 1) * 128], s.idt_bf[:])
            last.then_inc(sem["tV3"], 1)

        for it in P3_PRE:
            if it["kind"] == "QKVa":
                if it["pp"] == 0 and it["n"] == 0:
                    e.wait_ge(sem["vWc"], 12)
                emit_qkva(it)
            else:
                emit_qkvb(it)

        for p in range(NPAIR):
            par = p % 2
            QT, KT, Vau = QTb[par], KTb[par], Vaub[par]
            e.wait_ge(sem["vQ3"], 16 * (p + 1))
            e.wait_ge(sem["vV3"], 8 * (p + 1))

            def pv_outer(j, t, g, es):
                e.wait_ge(sem["aC2"], g)
                if t == 0:
                    e.wait_ge(sem["vC2"], p * 64 + (7 - j) * 8)
                for qt in range(4):
                    qtile = 4 * j + qt
                    if t > qtile:
                        continue
                    if t == qtile:
                        e.wait_ge(sem["gC2"],
                                  p * 64 + (7 - j) * 8 + (t - 4 * j + 1) * 2)
                    for h in range(2):
                        first = (t == 0 and qt == 0)
                        if t == qtile:
                            stat = s.mask2[:, h * 512 + qt * 128: h * 512 + qt * 128 + 128]
                            e.matmul(PB[2][:, h * 512 + qt * 128: h * 512 + qt * 128 + 64],
                                     stat,
                                     Vau[:, t * 130 + h * 65: t * 130 + h * 65 + 64],
                                     start=first, stop=False, skip_group_check=True)
                            statf = s.expCs(es)[:, h * 512 + qt * 128:
                                                h * 512 + qt * 128 + 128]
                            mm = e.matmul(
                                PB[2][:, h * 512 + qt * 128 + 64: h * 512 + qt * 128 + 65],
                                statf,
                                Vau[:, t * 130 + h * 65 + 64: t * 130 + h * 65 + 65],
                                start=False, stop=True, skip_group_check=True)
                            mm.then_inc(sem["tU2"], 1)
                        else:
                            stat = s.expCs(es)[:, h * 512 + qt * 128:
                                               h * 512 + qt * 128 + 128]
                            e.matmul(PB[2][:, h * 512 + qt * 128: h * 512 + qt * 128 + 65],
                                     stat,
                                     Vau[:, t * 130 + h * 65: t * 130 + h * 65 + 65],
                                     start=first, stop=False, skip_group_check=True)

            pend = []
            for it in P3_PAIRS[p]:
                k = it["kind"]
                if k == "C2":
                    j, t = it["j"], it["t"]
                    pb3_wait(it)
                    last = None
                    for h in range(2):
                        stat = KT[64 * h:64 * h + 64, t * 128:(t + 1) * 128]
                        mv = QT[64 * h:64 * h + 64, j * 512:(j + 1) * 512]
                        last = e.matmul(POOL[it["slot"]][:, h * 512:h * 512 + 512],
                                        stat, mv, start=True, stop=True,
                                        tile_position=(64 * h, 0))
                    last.then_inc(sem["tC2"], 1)
                    if len(pend) >= 2:
                        pv_outer(*pend.pop(0))
                    pend.append((j, t, it["aC2"], it["eslot"]))
                elif k == "C1":
                    pb3_wait(it)
                    i, k0, kw = it["i"], it["k0"], it["kw"]
                    last = None
                    for h in range(2):
                        statq = QT[64 * h:64 * h + 64, i * 128:(i + 1) * 128]
                        mvk = KT[64 * h:64 * h + 64, k0:k0 + kw]
                        last = e.matmul(POOL[it["slot"]][:, h * 512:h * 512 + kw],
                                        statq, mvk, start=True, stop=True,
                                        tile_position=(64 * h, 0))
                    last.then_inc(sem["tC1"], 1)
                elif k == "QKVa":
                    emit_qkva(it)
                else:
                    emit_qkvb(it)
            for pv in pend:
                pv_outer(*pv)

    def vector_prog(self, e):
        s, sem = self, self.sem
        bar = sem["bar"]
        PB, PBb = s.PB, s.PBb
        xb = [s.stage[:, 0:1024], s.stage[:, 1024:2048]]
        if "P0" in self.stages:
            e.wait_ge(sem["gInit"], 1)
            e.tensor_copy(s.idt_f[:], s.idt_bf[:])
            e.memset(s.ones1[:], 1.0)
            e.drain().then_inc(sem["vInit"], 1)
            e.wait_ge(sem["tSim"], 1)
            e.tensor_copy(s.simT[0:8, :], PB[0][0:8, 0:1024]).then_inc(sem["vSim1"], 1)
            e.drain()
            e.tensor_tensor(s.sq8[0:8, :], s.simT[0:8, :], s.simT[0:8, :], ALU.mult)
            e.drain()
            e.reduce_sum(s.n2[0:8, :], s.sq8[0:8, :], axis=AX.X).then_inc(sem["vSim2"], 1)
            e.wait_ge(sem["aSim"], 1)
            e.tensor_scalar(s.invn[0:8, :], s.invn_r[0:8, :], 1.0e12, None, ALU.min)
            e.drain()
            e.tensor_scalar(s.snnT[0:8, :], s.simT[0:8, :], s.invn[0:8, :], None,
                            ALU.mult).then_inc(sem["vSnn3"], 1)
            xb3 = [s.stage[:, 0:1024], s.stage[:, 1024:2048],
                   s.stage[:, 2048:3072], s.stage[:, 3072:4096]]
            dxs = ["dX0", "dX1", "dX20", "dX21"]
            mvD = s.Dup[:].rearrange("p (i two) -> p two i", two=2)
            bn6 = s.tmpc[:, 0:12]

            def emit_xn(j):
                e.wait_ge(sem["aRS"], j // 2 + 1)
                if j >= 2:
                    e.wait_ge(sem["tTR"], j - 1)
                e.tensor_scalar(s.xnb[:, (j % 2) * 1024:(j % 2) * 1024 + 1024],
                                xb3[j % 4], s.Dup[:, 2 * j:2 * j + 1],
                                s.rstd_all[:, j:j + 1],
                                ALU.subtract, ALU.mult).then_inc(sem["vXN"], 1)

            def emit_ev(j):
                e.wait_ge(sem["tTR"], j + 1)
                slot = PBb[3] if j % 2 == 0 else PBb[7]
                e.tensor_copy(
                    s.xnT[:].rearrange("p (ct n) -> p ct n", ct=8)
                    [:, :, j * 128:(j + 1) * 128],
                    slot[:].rearrange("p (ct n) -> p ct n", ct=8)
                ).then_inc(sem["vEv"], 1)

            for i in range(NT):
                e.wait_ge(sem[dxs[i % 4]], 16 * (i // 4 + 1))
                e.bn_stats(bn6[:, 0:6], xb3[i % 4][:, 0:512])
                e.bn_stats(bn6[:, 6:12], xb3[i % 4][:, 512:1024])
                e.drain()
                e.bn_aggr(s.Dup[:, 2 * i:2 * i + 2], bn6)
                e.drain()
                if i % 2 == 1:
                    g = i // 2
                    e.tensor_scalar(s.veps[:, 2 * g:2 * g + 2],
                                    mvD[:, 1, 2 * g:2 * g + 2], 1.0e-5, None,
                                    ALU.add)
                    e.drain().then_inc(sem["vM"], 1)
                if i >= 2:
                    emit_xn(i - 2)
                if i >= 3:
                    emit_ev(i - 3)
            emit_xn(NT - 2)
            emit_ev(NT - 3)
            emit_xn(NT - 1)
            emit_ev(NT - 2)
            emit_ev(NT - 1)
            e.wait_ge(sem["aP0b1"], 1)
            e.tensor_tensor(s.ddt[:], s.lnv[:], s.lna[:], ALU.subtract
                            ).then_inc(sem["vP0b2"], 1)
            e.wait_ge(sem["aP0b2"], 1)
            e.tensor_scalar(s.invr_all[:], s.invr_raw[:], 1.0e12, None, ALU.min
                            ).then_inc(sem["vP0b3"], 1)
            R3f = [s.R3[:, 0:8192].bitcast(F32)[:, 0:1024],
                   s.R3[:, 0:8192].bitcast(F32)[:, 1024:2048]]
            for idx in range(24):
                e.wait_ge(sem["dP0" if idx % 2 == 0 else "dP1"], 16 * (idx // 2 + 1))
                w = idx // 8
                a = idx % 8
                e.tensor_copy(s.R2[:, w * 8192 + a * 1024: w * 8192 + (a + 1) * 1024],
                              R3f[idx % 2]).then_inc(sem["vPc"], 1)
            e.wait_ge(sem["tSnn"], 1)
            e.tensor_copy(s.snn[:], PBb[2][:, 0:64]).then_inc(sem["vSnnF"], 1)
            e.wait_ge(sem["tSgb"], 1)
            e.tensor_copy(s.sgb[:], PB[1][:, 0:8]).then_inc(sem["vSgb"], 1)
            e.wait_ge(sem["tLN"], 1)
            e.tensor_copy(s.lnat[:], PB[2][:, 0:256]).then_inc(sem["vLNat"], 1)
            e.drain()
            e.wait_ge(sem["vP0b3"], 1)
            A3 = lambda t: t[:].rearrange("p (q ee) -> p q ee", ee=8)
            bcq = lambda t: bass.AP(t, 0, [[32, 128], [1, 32], [0, 8]])
            sgb_b = bass.AP(s.sgb, 0, [[8, 128], [0, 32], [1, 8]])
            e.tensor_tensor(A3(s.mq_all), A3(s.lnat), bcq(s.invr_all), ALU.mult)
            e.drain()
            e.tensor_tensor(A3(s.rwp_all), A3(s.mq_all), sgb_b, ALU.subtract)
            e.drain()
            e.tensor_reduce(s.lna[:], A3(s.rwp_all), AX.X, ALU.max)
            e.drain()
            e.tensor_tensor(A3(s.mq_all), A3(s.rwp_all), bcq(s.lna), ALU.is_ge)
            e.drain()
            e.scalar_tensor_tensor(A3(s.expg_all), A3(s.mq_all), -1.0e9,
                                   A3(s.rwp_all), ALU.mult, ALU.add)
            e.drain()
            e.tensor_reduce(s.lnv[:], A3(s.expg_all), AX.X, ALU.max)
            e.drain()
            e.tensor_tensor(A3(s.expg_all), A3(s.rwp_all), bcq(s.lnv), ALU.is_ge)
            e.tensor_scalar(s.ddt[:], s.lna[:], 0.0, None, ALU.is_le)
            e.drain()
            e.tensor_scalar(s.mq_all[:], s.rwp_all[:], 0.0, None, ALU.is_gt)
            e.tensor_tensor(A3(s.gated_all), A3(s.expg_all), bcq(s.ddt), ALU.mult)
            e.drain()
            e.tensor_tensor(s.msk_all[:], s.gated_all[:], s.mq_all[:], ALU.add)
            e.drain()
            e.tensor_scalar(s.gated_all[:], s.rwp_all[:], 0.0, None, ALU.max)
            e.drain().then_inc(sem["vRt1"], 1)
            e.wait_ge(sem["aRt"], 1)
            e.tensor_tensor(s.rwn_all[:], s.msk_all[:], s.expg_all[:], ALU.mult)
            e.drain()
            e.reduce_sum(s.Z_all[:], s.rwn_all[:].rearrange("p (q ee) -> p q ee", ee=8),
                         axis=AX.X)
            e.drain()
            e.reciprocal(s.Zi_all[:], s.Z_all[:])
            e.drain()
            bcq2 = lambda t: bass.AP(t, 0, [[32, 128], [1, 32], [0, 8]])
            e.tensor_tensor(s.rw_all[:].rearrange("p (q ee) -> p q ee", ee=8),
                            s.rwn_all[:].rearrange("p (q ee) -> p q ee", ee=8),
                            bcq2(s.Zi_all), ALU.mult)
            e.drain()
            e.tensor_scalar(s.mq_all[:], s.msk_all[:], INV_SQRT_D, None, ALU.mult
                            ).then_inc(sem["vRt2"], 1)
        e.drain()
        e.sem_inc(bar, 1)
        e.wait_ge(bar, 5)
        if "P1" in self.stages:
            self.vector_p1(e)
        e.drain()
        e.sem_inc(bar, 1)
        e.wait_ge(bar, 10)
        if "P3" in self.stages:
            self.vector_p3(e)
        e.drain()
        e.sem_inc(bar, 1)
        e.wait_ge(bar, 15)

    def vector_p1(self, e):
        s, sem = self, self.sem
        PB, PBb = s.PB, s.PBb
        vfa = s.R3[:, 8192:12352]
        ao_all = s.R3[:, 12352:16448]
        qfT = s.R3[:, 0:4096]
        kfT = s.R3[:, 4096:8192]
        e.memset(vfa.rearrange("p (nt w) -> p nt w", w=130)[:, :, 128:130], 1.0)
        for qt in range(NT):
            specs = ((0, "tA2q", "sCq", s.mq_all, s.cacc,
                      s.qk_tmp[:, (qt % 2) * 512:(qt % 2) * 512 + 128]),
                     (1, "tA2k", "sCk", s.msk_all, s.cacc2,
                      s.qk_tmp[:, (qt % 2) * 512 + 128:(qt % 2) * 512 + 256]),
                     (2, "tA2v", "sCv", s.msk_all, s.cacc3,
                      vfa[:, qt * 130:qt * 130 + 128]))
            if qt >= 2:
                e.wait_ge(sem["tT2"], qt - 1)
            for pb, tname, cname, mask, acc, dst in specs:
                e.wait_ge(sem[tname], qt + 1)
                e.tensor_scalar(acc[:], PB[pb][:, 0:128],
                                mask[:, qt * 8:qt * 8 + 1], None, ALU.mult
                                ).then_inc(sem[cname], 1)
                for ei in range(1, 8):
                    e.wait_ge(sem[cname], qt * 8 + ei)
                    out = dst if ei == 7 else acc[:]
                    e.scalar_tensor_tensor(out, PB[pb][:, ei * 128:(ei + 1) * 128],
                                           mask[:, qt * 8 + ei:qt * 8 + ei + 1],
                                           acc[:], ALU.mult, ALU.add
                                           ).then_inc(sem[cname], 1)
            if qt >= 1:
                e.wait_ge(sem["tT2"], qt)
                slot = PBb[3] if (qt - 1) % 2 == 0 else PBb[7]
                e.tensor_copy(qfT[:, (qt - 1) * 128:qt * 128], slot[:, 0:128])
                e.tensor_copy(kfT[:, (qt - 1) * 128:qt * 128], slot[:, 128:256]
                              ).then_inc(sem["aT2"], 1)
        e.wait_ge(sem["tT2"], NT)
        slot = PBb[3] if (NT - 1) % 2 == 0 else PBb[7]
        e.tensor_copy(qfT[:, (NT - 1) * 128:NT * 128], slot[:, 0:128])
        e.tensor_copy(kfT[:, (NT - 1) * 128:NT * 128], slot[:, 128:256]
                      ).then_inc(sem["aT2"], 1)
        for j in range(8):
            e.wait_ge(sem["tUf"], (j + 1) * 4)
            for qt in range(4):
                e.reciprocal(s.dinv[:, qt:qt + 1], PB[1][:, qt * 256 + 128:qt * 256 + 129])
            e.drain()
            for qt in range(4):
                qtile = 4 * j + qt
                e.tensor_scalar(s.rwp_all[:, qtile * 8:qtile * 8 + 8],
                                s.rw_all[:, qtile * 8:qtile * 8 + 8],
                                s.dinv[:, qt:qt + 1], None, ALU.mult)
                e.tensor_copy(ao_all[:, qtile * 128:(qtile + 1) * 128],
                              PB[1][:, qt * 256:qt * 256 + 128])
            e.drain().then_inc(sem["vUf"], 4)
        for c in range(4):
            e.wait_ge(sem["dOp0" if c % 2 == 0 else "dOp1"], 16 * (c // 2 + 1))
            e.tensor_copy(s.R2[:, c * 2048:(c + 1) * 2048],
                          s.stage[:, (c % 2) * 2048:(c % 2) * 2048 + 2048]
                          ).then_inc(sem["vOpc"], 1)
        aoT = [s.Vau[:, 0:1024], s.Vau[:, 1024:2048]]
        for qt in range(NT):
            if qt >= 2:
                e.wait_ge(sem["tT4"], qt - 1)
            for ei in range(8):
                e.tensor_scalar(s.xnb[:, (qt % 2) * 1024 + ei * 128:
                                      (qt % 2) * 1024 + (ei + 1) * 128],
                                ao_all[:, qt * 128:(qt + 1) * 128],
                                s.rwp_all[:, qt * 8 + ei:qt * 8 + ei + 1], None,
                                ALU.mult)
            e.drain().then_inc(sem["vA4"], 1)
            if qt >= 1:
                e.wait_ge(sem["tT4"], qt)
                slot = PBb[3] if (qt - 1) % 2 == 0 else PBb[7]
                e.tensor_copy(aoT[(qt - 1) % 2], slot[:]).then_inc(sem["aE4"], 1)
        e.wait_ge(sem["tT4"], NT)
        slot = PBb[3] if (NT - 1) % 2 == 0 else PBb[7]
        e.tensor_copy(aoT[(NT - 1) % 2], slot[:]).then_inc(sem["aE4"], 1)

    def vector_p3(self, e):
        s, sem = self, self.sem
        PB, PBb = s.PB, s.PBb
        QTb = [s.QT, s.R3[:, 0:4096]]
        KTb = [s.KT, s.R3[:, 4096:8192]]
        Vaub = [s.Vau, s.R3[:, 8192:12352]]
        POOLV = [PB[0], PB[1], PB[3]]
        POOLVB = [PBb[4], PBb[5], PBb[7]]
        for c in range(12):
            e.wait_ge(sem["dWgt0" if c % 2 == 0 else "dWgt1"], 16 * (c // 2 + 1))
            w = c // 4
            cc = c % 4
            src_ = s.stage[:, (c % 2) * 2048:(c % 2) * 2048 + 2048]
            dst = s.R2[:, w * 8192 + cc * 2048: w * 8192 + (cc + 1) * 2048]
            if w == 0:
                e.tensor_scalar(dst, src_, QSCALE, None, ALU.mult).then_inc(sem["vWc"], 1)
            else:
                e.tensor_copy(dst, src_).then_inc(sem["vWc"], 1)
        for par in range(2):
            vv = Vaub[par].rearrange("p (nt w) -> p nt w", w=130)
            e.memset(vv[:, :, 64:65], 1.0)
            e.memset(vv[:, :, 129:130], 1.0)

        def emit_qkva(it):
            pp, n = it["pp"], it["n"]
            e.wait_ge(sem["tQ3"], it["tQ3"])
            e.tensor_copy(QTb[pp % 2][:, n * 512:(n + 1) * 512],
                          POOLV[it["slot"]][:, 0:512])
            e.tensor_copy(KTb[pp % 2][:, n * 512:(n + 1) * 512],
                          POOLV[it["slot"]][:, 512:1024]).then_inc(sem["vQ3"], 1)

        def emit_qkvb(it):
            pp, n = it["pp"], it["n"]
            e.wait_ge(sem["tQ3"], it["tQ3"])
            e.tensor_copy(s.vtt[:], POOLV[it["slot"]][:, 0:512]).then_inc(sem["vQ3"], 1)
            e.wait_ge(sem["tV3"], it["tV3"])
            last = None
            for b in range(4):
                nt = 4 * n + b
                last = e.tensor_copy(
                    Vaub[pp % 2][:, nt * 130:nt * 130 + 130]
                    .rearrange("p (a y) -> p a y", a=2)[:, :, 0:64],
                    POOLVB[it["slot"]][:, b * 128:(b + 1) * 128]
                    .rearrange("p (a y) -> p a y", a=2))
            last.then_inc(sem["vV3"], 1)

        for it in P3_PRE:
            (emit_qkva if it["kind"] == "QKVa" else emit_qkvb)(it)

        for p in range(NPAIR):
            emit_evac = self.emit_evac_fn(e, p)
            cmb_q = []
            drip = [0]

            def drip_cmb(force=False):
                drip[0] += 1
                if cmb_q and (force or drip[0] % 2 == 0):
                    self.vector_cmb_one(e, *cmb_q.pop(0))
            e.memset(s.Dup[:], 0.0)
            e.drain()
            for it in P3_PAIRS[p]:
                k = it["kind"]
                if k == "QKVa":
                    emit_qkva(it)
                    continue
                if k == "QKVb":
                    emit_qkvb(it)
                    continue
                if k == "C1":
                    i, c, kw = it["i"], it["c"], it["kw"]
                    m1 = it["aC1"]
                    e.wait_ge(sem["aC1"], m1)
                    e.tensor_reduce(
                        s.tmpc[:].rearrange("p (h c) -> p h c", h=2)[:, :, c:c + 1],
                        s.c1scr((m1 - 1) % 3)[:, 0:2 * kw]
                        .rearrange("p (h k) -> p h k", h=2),
                        AX.X, ALU.add).then_inc(sem["vC1r"], 1)
                    drip_cmb()
                    if not it["lastf"]:
                        continue
                    e.drain()
                    nchunks = c + 1
                    srcs = {hh: [s.tmpc[:, hh * 8 + cc:hh * 8 + cc + 1]
                                 for cc in range(nchunks)] for hh in range(2)}
                    lvl = 0
                    while max(len(srcs[0]), len(srcs[1])) > 1:
                        base = (lvl % 2) * 8
                        scr = 0
                        nxt = {0: [], 1: []}
                        for hh in range(2):
                            lst = srcs[hh]
                            for a2 in range(0, len(lst) - 1, 2):
                                dstc = s.tca[:, base + scr:base + scr + 1]
                                scr += 1
                                e.tensor_tensor(dstc, lst[a2], lst[a2 + 1], ALU.add)
                                nxt[hh].append(dstc)
                            if len(lst) % 2 == 1:
                                nxt[hh].append(lst[-1])
                        e.drain()
                        srcs = nxt
                        lvl += 1
                    for hh in range(2):
                        e.tensor_copy(s.Dup[:, i * 2 + hh:i * 2 + hh + 1], srcs[hh][0])
                    e.drain()
                    continue
                j, t = it["j"], it["t"]
                if t == 4 * j + 3:
                    emit_evac(j)
                    if p == NPAIR - 1 and "CMB" in self.stages:
                        cmb_q.extend((j, b) for b in range(4))
                else:
                    drip_cmb()
            while cmb_q:
                drip_cmb(force=True)

    def emit_evac_fn(self, e, p):
        s, sem = self, self.sem
        PB = s.PB

        def emit_evac(j):
            prevspill = p * 4 + (5 - j) // 2 + 1
            if prevspill >= 1:
                e.wait_ge(sem["dS30" if j % 2 == 0 else "dS31"], 16 * prevspill)
            e.wait_ge(sem["tU2"], p * 64 + (7 - j) * 8 + 8)
            dcols = bass.AP(PB[2], 64, [[1024, 128], [512, 2], [128, 4]])
            e.tensor_tensor(
                s.dtot[:].rearrange("p (q h) -> p h q", h=2),
                dcols,
                s.Dup[:, 8 * j:8 * j + 8].rearrange("p (q h) -> p h q", h=2),
                ALU.add)
            e.drain()
            e.reciprocal(s.dinv[:], s.dtot[:])
            e.drain()
            for qt in range(4):
                for h in range(2):
                    e.tensor_scalar(
                        s.s_stage[:, (j % 2) * 512 + qt * 128 + h * 64:
                                  (j % 2) * 512 + qt * 128 + h * 64 + 64],
                        PB[2][:, h * 512 + qt * 128: h * 512 + qt * 128 + 64],
                        s.dinv[:, qt * 2 + h:qt * 2 + h + 1], None, ALU.mult)
            e.drain().then_inc(sem["vC2"], 8)
        return emit_evac

    def scalar_prog(self, e):
        s, sem = self, self.sem
        bar = sem["bar"]
        PB, PBb = s.PB, s.PBb
        xb = [s.stage[:, 0:1024], s.stage[:, 1024:2048]]
        if "P0" in self.stages:
            e.wait_ge(sem["dW"], 32)
            e.activation(s.sg_row[0:1, :], s.gates_in[0:1, :], AF.Sigmoid
                         ).then_inc(sem["aSg"], 1)
            e.wait_ge(sem["vSim2"], 1)
            e.activation(s.n2b[0:8, :], s.n2[0:8, :], AF.Ln)
            e.drain()
            e.activation(s.invn_r[0:8, :], s.n2b[0:8, :], AF.Exp, scale=-0.5
                         ).then_inc(sem["aSim"], 1)
            mvD = s.Dup[:].rearrange("p (i two) -> p two i", two=2)
            for g in range(16):
                e.wait_ge(sem["vM"], g + 1)
                e.activation(s.lnv[:, 2 * g:2 * g + 2],
                             s.veps[:, 2 * g:2 * g + 2], AF.Ln)
                e.drain()
                e.activation(s.rstd_all[:, 2 * g:2 * g + 2],
                             s.lnv[:, 2 * g:2 * g + 2], AF.Exp, scale=-0.5
                             ).then_inc(sem["aRS"], 1)
            e.activation(s.lna[:], mvD[:, 1, :], AF.Ln, scale=float(C)
                         ).then_inc(sem["aP0b1"], 1)
            e.wait_ge(sem["vP0b2"], 1)
            e.activation(s.invr_raw[:], s.ddt[:], AF.Exp, scale=0.5
                         ).then_inc(sem["aP0b2"], 1)
            lgT = s.R3[:, 8192:16384].bitcast(F32)
            for j in range(8):
                e.wait_ge(sem["tL"], j + 1)
                e.copy(lgT[0:8, j * 512:(j + 1) * 512],
                       PB[1][0:8, (j % 2) * 512:(j % 2) * 512 + 512]
                       ).then_inc(sem["aLT"], 1)
            e.wait_ge(sem["vRt1"], 1)
            e.activation(s.expg_all[:], s.gated_all[:], AF.Exp).then_inc(sem["aRt"], 1)
        e.drain()
        e.sem_inc(bar, 1)
        e.wait_ge(bar, 5)
        if "P1" in self.stages:
            for j in range(8):
                T = 4 * j + 4
                for t in range(T):
                    g = c2_g(j, t)
                    e.wait_ge(sem["tSf"], g + 1)
                    pbf = PB[0] if (g % 4) < 2 else PB[2]
                    e.activation(s.expC[:, (g % 4) * 512:(g % 4) * 512 + 512],
                                 pbf[:, (g % 2) * 512:(g % 2) * 512 + 512],
                                 AF.Exp).then_inc(sem["aEf"], 1)
            for qt in range(NT):
                e.wait_ge(sem["tO4"], qt + 1)
                if qt >= 2:
                    e.wait_ge(sem["dG0" if qt % 2 == 0 else "dG1"], 16 * (qt // 2))
                pbo = PB[0] if qt % 2 == 0 else PB[1]
                e.activation(s.gbuf[:, (qt % 2) * 1024:(qt % 2) * 1024 + 1024],
                             pbo[:, :], AF.Silu).then_inc(sem["aS4"], 1)
        e.drain()
        e.sem_inc(bar, 1)
        e.wait_ge(bar, 10)
        if "P3" in self.stages:
            for p in range(NPAIR):
                for it in P3_PAIRS[p]:
                    k = it["kind"]
                    if k == "C2":
                        es = it["eslot"]
                        e.wait_ge(sem["tC2"], it["tC2"])
                        e.activation(s.expCs(es),
                                     PB[[0, 1, 3][it["slot"]]][:, :],
                                     AF.Exp).then_inc(sem["aC2"], 1)
                    elif k == "C1":
                        kw = it["kw"]
                        m1 = it["aC1"]
                        if m1 == 1:
                            e.wait_ge(sem["dG0"], 16 * 16)
                            e.wait_ge(sem["dG1"], 16 * 16)
                        if m1 > 3:
                            e.wait_ge(sem["vC1r"], m1 - 3)
                        e.wait_ge(sem["tC1"], it["tC1"])
                        scr = self.c1scr((m1 - 1) % 3)
                        pbsrc = PB[[0, 1, 3][it["slot"]]]
                        if kw == 512:
                            e.activation(scr[:, 0:1024], pbsrc[:, 0:1024],
                                         AF.Exp).then_inc(sem["aC1"], 1)
                        else:
                            e.activation(
                                scr[:, 0:2 * kw].rearrange("p (h k) -> p h k", h=2),
                                pbsrc[:, 0:1024].rearrange("p (h k) -> p h k",
                                                           h=2)[:, :, 0:kw],
                                AF.Exp).then_inc(sem["aC1"], 1)
        e.sem_inc(bar, 1)
        e.wait_ge(bar, 15)


def build_program(debug=False, stages=("P0", "P1", "P3", "CMB")):
    return Builder(debug=debug, stages=stages)


def kernel(L0_attn_content, L0_attn_Wq, L0_attn_Wk, L0_attn_Wv,
           L0_ffn_sim_matrix, L0_ffn_gates, L0_ffn_q_proj, L0_ffn_k_proj,
           L0_ffn_v_proj, L0_ffn_o_proj):
    x = np.ascontiguousarray(L0_attn_content, dtype=np.float32)
    common = {
        "wq": np.ascontiguousarray(L0_attn_Wq, np.float32),
        "wk": np.ascontiguousarray(L0_attn_Wk, np.float32),
        "wv": np.ascontiguousarray(L0_attn_Wv, np.float32),
        "sim": np.ascontiguousarray(L0_ffn_sim_matrix, np.float32),
        "gates": np.ascontiguousarray(L0_ffn_gates, np.float32),
        "qp": np.ascontiguousarray(L0_ffn_q_proj, np.float32),
        "kp": np.ascontiguousarray(L0_ffn_k_proj, np.float32),
        "vp": np.ascontiguousarray(L0_ffn_v_proj, np.float32),
        "op": np.ascontiguousarray(L0_ffn_o_proj, np.float32),
    }
    in_maps = [dict(common, x=np.ascontiguousarray(x[i])) for i in range(8)]
    nc = build_program().build()
    res = run_bass_kernel_spmd(nc, in_maps, core_ids=list(range(8)))
    return np.stack([np.asarray(r["out"], np.float32) for r in res.results], axis=0)


# version 29 (final)
# speedup vs baseline: 1.0374x; 1.0069x over previous
import numpy as np
from contextlib import ExitStack

import concourse.bass as bass
import concourse.mybir as mybir
from concourse.bass_utils import run_bass_kernel_spmd

F32 = mybir.dt.float32
BF16 = mybir.dt.bfloat16
AF = mybir.ActivationFunctionType
ALU = mybir.AluOpType
AX = mybir.AxisListType

N, C, NT, CT = 4096, 1024, 32, 8
H, HD, NPAIR, E, D = 16, 64, 8, 8, 128
INV_SQRT_D = float(1.0 / np.sqrt(128.0))
QSCALE = 0.125


def c2_g(j, t):
    return 2 * j * j + 2 * j + t


C2_PER = c2_g(7, 31) + 1


def c1_chunk_list():
    lst = []
    for i in range(NT - 1):
        k = 128 * (i + 1)
        while k < N:
            kw = min(512, N - k)
            lst.append((i, k, kw))
            k += kw
    return lst


C1_LIST = c1_chunk_list()
C1_PER = len(C1_LIST)


def c1_items():
    per_i = {}
    for (i, k0, kw) in C1_LIST:
        per_i.setdefault(i, []).append((k0, kw))
    items = []
    for i in sorted(per_i.keys(), reverse=True):
        for c, (k0, kw) in enumerate(per_i[i]):
            items.append((i, c, k0, kw, c + 1 == len(per_i[i])))
    return items


C1_ITEMS = c1_items()
C1_NIT = len(C1_ITEMS)


def p3_sched():
    import math
    c2l = [(j, t) for j in range(7, -1, -1) for t in range(4 * j + 4)]
    need = {j: len([it for it in C1_ITEMS if it[0] >= 4 * j]) for j in range(8)}
    sched = []
    c1p = 0
    for k, (j, t) in enumerate(c2l):
        if t == 4 * j + 3:
            while c1p < need[j]:
                sched.append(("C1", C1_ITEMS[c1p]))
                c1p += 1
        sched.append(("C2", (j, t)))
        target = min(C1_NIT, math.ceil((k + 1) * C1_NIT / len(c2l)))
        while c1p < target:
            sched.append(("C1", C1_ITEMS[c1p]))
            c1p += 1
    assert c1p == C1_NIT
    return sched


SCHED = p3_sched()
C2_NIT = 144


def p3_counts():
    k2 = 0
    k1 = 0
    tu2 = 0
    info = []
    tu2_after_qchunk = {}
    pend = None
    for item in SCHED:
        kind, d = item
        if kind == "C2":
            if pend is not None:
                (pj, pt) = pend
                if pt >= 4 * pj:
                    tu2 += 2
                    if pt == 4 * pj + 3:
                        tu2_after_qchunk[pj] = tu2
            info.append((k2, tu2))
            pend = d
            k2 += 1
        else:
            info.append((k1, tu2))
            k1 += 1
    (pj, pt) = pend
    if pt >= 4 * pj:
        tu2 += 2
        if pt == 4 * pj + 3:
            tu2_after_qchunk[pj] = tu2
    return info, tu2_after_qchunk, tu2


P3_INFO, TU2_AFTER_Q, TU2_PER_PAIR = p3_counts()


def build_p3_items():
    cnt = dict(tC1=0, aC1=0, tC2=0, aC2=0, tQ3=0, vQ3=0, tV3=0, vV3=0)
    pool_rel = [None, None, None]
    pool_pos = [0]

    def qkv_items(pp):
        out = []
        for n in range(8):
            out.append({"kind": "QKVa", "pp": pp, "n": n})
            out.append({"kind": "QKVb", "pp": pp, "n": n})
        return out

    def resolve(seq):
        res = []
        for it in seq:
            it = dict(it)
            k = it["kind"]
            slot = pool_pos[0] % 3
            pool_pos[0] += 1
            it["slot"] = slot
            it["pool_wait"] = pool_rel[slot]
            if k == "C1":
                cnt["tC1"] += 1
                cnt["aC1"] += 1
                it["tC1"] = cnt["tC1"]
                it["aC1"] = cnt["aC1"]
                pool_rel[slot] = ("aC1", cnt["aC1"])
            elif k == "QKVa":
                cnt["tQ3"] += 1
                cnt["vQ3"] += 1
                it["tQ3"] = cnt["tQ3"]
                it["vQ3"] = cnt["vQ3"]
                pool_rel[slot] = ("vQ3", cnt["vQ3"])
            elif k == "QKVb":
                cnt["tQ3"] += 1
                cnt["vQ3"] += 1
                cnt["tV3"] += 1
                cnt["vV3"] += 1
                it["tQ3"] = cnt["tQ3"]
                it["vQ3"] = cnt["vQ3"]
                it["tV3"] = cnt["tV3"]
                it["vV3"] = cnt["vV3"]
                pool_rel[slot] = ("vV3", cnt["vV3"])
            else:
                cnt["tC2"] += 1
                cnt["aC2"] += 1
                it["tC2"] = cnt["tC2"]
                it["aC2"] = cnt["aC2"]
                it["eslot"] = (cnt["tC2"] - 1) % 3
                pool_rel[slot] = ("aC2", cnt["aC2"])
            res.append(it)
        return res

    pre = resolve(qkv_items(0))
    pairs = []
    for p in range(NPAIR):
        base = []
        for idx, (kind, d) in enumerate(SCHED):
            if kind == "C2":
                base.append({"kind": "C2", "j": d[0], "t": d[1],
                             "k2": P3_INFO[idx][0]})
            else:
                (i, c, k0, kw, lastf) = d
                base.append({"kind": "C1", "i": i, "c": c, "k0": k0,
                             "kw": kw, "lastf": lastf,
                             "k1": P3_INFO[idx][0]})
        if p < NPAIR - 1:
            qk = qkv_items(p + 1)
            merged = []
            qi = 0
            for idx2, it in enumerate(base):
                merged.append(it)
                if qi < len(qk) and idx2 >= 30 and (idx2 - 30) % 14 == 0:
                    merged.append(qk[qi])
                    qi += 1
                    if qi < len(qk) and (idx2 - 30) % 28 == 0:
                        merged.append(qk[qi])
                        qi += 1
            while qi < len(qk):
                merged.append(qk[qi])
                qi += 1
            base = merged
        pairs.append(resolve(base))
    return pre, pairs


P3_PRE, P3_PAIRS = build_p3_items()


class Builder:
    def __init__(self, debug=False, stages=("P0", "P1", "P3", "CMB")):
        self.debug = debug
        self.stages = set(stages)
        nc = bass.Bass("TRN2", target_bir_lowering=False)
        self.nc = nc
        d = {}
        d["x"] = nc.dram_tensor("x", [N, C], F32, kind="ExternalInput").ap()
        d["wq"] = nc.dram_tensor("wq", [C, C], F32, kind="ExternalInput").ap()
        d["wk"] = nc.dram_tensor("wk", [C, C], F32, kind="ExternalInput").ap()
        d["wv"] = nc.dram_tensor("wv", [C, C], F32, kind="ExternalInput").ap()
        d["sim"] = nc.dram_tensor("sim", [C, E], F32, kind="ExternalInput").ap()
        d["gates"] = nc.dram_tensor("gates", [E], F32, kind="ExternalInput").ap()
        d["qp"] = nc.dram_tensor("qp", [E, C, D], F32, kind="ExternalInput").ap()
        d["kp"] = nc.dram_tensor("kp", [E, C, D], F32, kind="ExternalInput").ap()
        d["vp"] = nc.dram_tensor("vp", [E, C, D], F32, kind="ExternalInput").ap()
        d["op"] = nc.dram_tensor("op", [E, D, C], F32, kind="ExternalInput").ap()
        d["out"] = nc.dram_tensor("out", [N, C], F32, kind="ExternalOutput").ap()
        sk = "ExternalOutput" if debug else "Internal"
        d["g_sp"] = nc.dram_tensor("g_sp", [N, C], BF16, kind=sk).ap()
        d["s_sp"] = nc.dram_tensor("s_sp", [N, 1024], BF16, kind=sk).ap()
        if debug:
            for nm, sh, dt in [
                ("dbg_lnat", [128, 256], F32), ("dbg_mask", [128, 256], F32),
                ("dbg_rw", [128, 256], F32), ("dbg_invr", [128, 32], F32),
                ("dbg_mv", [128, 64], F32), ("dbg_xnT", [128, 4096], BF16),
                ("dbg_snn", [128, 64], BF16), ("dbg_qfT", [128, 4096], BF16),
                ("dbg_kfT", [128, 4096], BF16), ("dbg_vfa", [128, 4160], BF16),
                ("dbg_ao", [128, 4096], BF16), ("dbg_rwp", [128, 256], F32),
                ("dbg_QT", [128, 4096], BF16), ("dbg_KT", [128, 4096], BF16),
                ("dbg_Vau", [128, 4160], BF16), ("dbg_dup", [128, 64], F32),
            ]:
                d[nm] = nc.dram_tensor(nm, sh, dt, kind="ExternalOutput").ap()
        self.d = d

    def alloc(self, st):
        nc = self.nc
        sb = lambda n, s_, dt=F32: st.enter_context(nc.sbuf_tensor(n, s_, dt))
        ps = lambda n, s_, dt=F32: st.enter_context(nc.psum_tensor(n, s_, dt))
        sem = lambda n: st.enter_context(nc.semaphore(n))
        s = self
        s.xnT = sb("xnT", [128, 32768], BF16)
        s.R2 = sb("R2", [128, 24576], BF16)
        s.R3 = sb("R3", [128, 17408], BF16)
        s.QT = sb("QTt", [128, 4096], BF16)
        s.KT = sb("KTt", [128, 4096], BF16)
        s.Vau = sb("Vau", [128, 4160], BF16)
        s.stage = sb("stagef", [128, 4096], F32)
        s.s1_all = sb("s1_all", [128, 32]); s.acc_all = sb("acc_all", [128, 32])
        s.m_all = sb("m_all", [128, 32]); s.v_all = sb("v_all", [128, 32])
        s.lnv = sb("lnv", [128, 32]); s.lna = sb("lna", [128, 32])
        s.ddt = sb("ddt", [128, 32]); s.rstd_all = sb("rstd_all", [128, 32])
        s.invr_raw = sb("invr_raw", [128, 32]); s.invr_all = sb("invr_all", [128, 32])
        s.lnat = sb("lnat", [128, 256]); s.msk_all = sb("msk_all", [128, 256])
        s.gated_all = sb("gated_all", [128, 256]); s.expg_all = sb("expg_all", [128, 256])

        s.mq_all = sb("mq_all", [128, 256]); s.rwp_all = sb("rwp_all", [128, 256])
        s.Z_all = sb("Z_all", [128, 32]); s.Zi_all = sb("Zi_all", [128, 32])
        s.veps = sb("veps", [128, 32])
        s.Dup = sb("Dup", [128, 64])
        s.tmpc = sb("tmpc", [128, 16]); s.tca = sb("tca", [128, 16])
        s.sim_in = sb("sim_in", [128, 64])

        s.n2 = sb("n2", [8, 1]); s.n2b = sb("n2b", [8, 1])
        s.invn_r = sb("invn_r", [8, 1]); s.invn = sb("invn", [8, 1])
        s.snn = sb("snn", [128, 64], BF16)
        s.gates_in = sb("gates_in", [1, 8]); s.sg_row = sb("sg_row", [1, 8])
        s.sgb = sb("sgb", [128, 8]); s.ones1 = sb("ones1", [1, 128])
        s.onesb = sb("onesb", [128, 128], BF16); s.idt_bf = sb("idt_bf", [128, 128], BF16)
        s.idt_f = sb("idt_f", [128, 128])
        s.qk_tmp = sb("qk_tmp", [128, 1024], BF16)
        s.s_stage = sb("s_stage", [128, 1024], BF16)
        s.dinv = sb("dinv", [128, 8]); s.dtot = sb("dtot", [128, 8])
        s.cacc = sb("cacc", [128, 128]); s.cacc2 = sb("cacc2", [128, 128])
        s.mask2 = sb("mask2", [128, 1024], BF16)
        s.expC = sb("expC", [128, 2048], BF16)
        s.rt1 = sb("rt1", [128, 8]); s.rt2 = sb("rt2", [128, 8]); s.rt3 = sb("rt3", [128, 8])
        s.rt4 = sb("rt4", [128, 8]); s.rt5 = sb("rt5", [128, 8]); s.rti = sb("rti", [128, 1])
        s.rtm = sb("rtm", [128, 2])
        s.PB = [ps(f"PB{i}", [128, 1024], F32) for i in range(4)]
        names = [
            "bar", "dX0", "dX1", "dX20", "dX21", "dW", "dP0", "dP1",
            "dOp0", "dOp1", "dWgt0", "dWgt1", "dG0", "dG1", "dS30", "dS31",
            "dCX0", "dCX1", "dCG0", "dCG1", "dCS0", "dCS1", "dOut0", "dOut1", "dDbg",
            "vInit", "gInit", "vP0a", "aSQ", "vP0b", "aP0b1", "vP0b2", "aP0b2", "vP0b3",
            "vXN", "tTR", "vEv", "tSim", "vSim1", "vSim2", "aSim", "vSnn3", "tSnn", "vSnnF",
            "aSg", "tSgb", "vSgb", "tL", "aLT", "tLN", "vLNat", "vRt1", "aRt", "vRt2", "vPc",
            "vOpc",
            "tA2q", "tA2k", "tA2v", "vA2q", "vA2k", "vA2v", "tT2", "aT2",
            "sCq", "sCk", "sCv",
            "tSf", "aEf", "gMf", "tUf", "vUf",
            "vA4", "tT4", "aE4", "tO4", "aS4",
            "vWc", "tQ3", "vQ3", "tV3", "vV3",
            "tC1", "aC1", "vC1",
            "tC2", "aC2", "gC2", "tU2", "vC2", "vC1r", "vM", "aRS",
            "vCM",
        ]
        s.sem = {n: sem(n) for n in names}
        s.PBb = [p[:, 0:512].bitcast(BF16) for p in s.PB] + \
                [p[:, 512:1024].bitcast(BF16) for p in s.PB]
        s.cacc3 = None
        s.simT = s.R3[:, 8192:10240].bitcast(F32)[0:8, :]
        s.sq8 = s.s_stage[0:8, 0:1024]
        s.cacc3 = s.stage[:, 3072:3200]
        s.gbuf = s.stage[:, 3072:4096].bitcast(BF16)
        s.expC3 = s.stage[:, 2048:2560].bitcast(BF16)
        s.c1x = s.stage[:, 2560:3072].bitcast(BF16)
        s.dex = s.gbuf
        s.xnb = s.R2[:, 22528:24576]
        s.maskedf = s.mask2[:, 0:512]
        s.vtt = s.qk_tmp[:, 0:512]
        s.snnT = s.R3[0:8, 16384:17408]
        s.rwn_all = s.gated_all
        s.rw_all = s.expg_all

    def build(self):
        with ExitStack() as st:
            self.alloc(st)
            blk = st.enter_context(self.nc.Block())

            @blk.sync
            def _(e):
                self.sync_prog(e)

            @blk.tensor
            def _(e):
                self.tensor_prog(e)

            @blk.vector
            def _(e):
                self.vector_prog(e)

            @blk.scalar
            def _(e):
                self.scalar_prog(e)

            @blk.gpsimd
            def _(e):
                self.gpsimd_prog(e)
        return self.nc

    def sync_prog(self, e):
        s, d, sem = self, self.d, self.sem
        bar = sem["bar"]
        xb = [s.stage[:, 0:1024], s.stage[:, 1024:2048]]
        xr = d["x"].rearrange("(i p) c -> p i c", p=128)
        if "P0" in self.stages:
            e.dma_start(s.sim_in[:].rearrange("p (a e) -> p a e", a=8),
                        d["sim"].rearrange("(a p) e -> p a e", p=128)).then_inc(sem["dW"], 16)
            e.dma_start(s.gates_in[0:1, :],
                        d["gates"].rearrange("(a e) -> a e", a=1)).then_inc(sem["dW"], 16)
            xb3 = [s.stage[:, 0:1024], s.stage[:, 1024:2048],
                   s.stage[:, 2048:3072], s.stage[:, 3072:4096]]
            dxs = ["dX0", "dX1", "dX20", "dX21"]
            for i in range(NT):
                if i >= 4:
                    e.wait_ge(sem["vXN"], i - 3)
                e.dma_start(xb3[i % 4], xr[:, i, :]).then_inc(sem[dxs[i % 4]], 16)
        e.sem_inc(bar, 1)
        e.wait_ge(bar, 5)
        if "P1" in self.stages:
            gr = d["g_sp"].rearrange("(i p) c -> p i c", p=128)
            for qt in range(NT):
                e.wait_ge(sem["aS4"], qt + 1)
                e.dma_start(gr[:, qt, :],
                            s.gbuf[:, (qt % 2) * 1024:(qt % 2) * 1024 + 1024]
                            ).then_inc(sem["dG0" if qt % 2 == 0 else "dG1"], 16)
        e.sem_inc(bar, 1)
        e.wait_ge(bar, 10)
        if "P3" in self.stages:
            if "P1" in self.stages:
                e.wait_ge(sem["dG0"], 16 * 16)
                e.wait_ge(sem["dG1"], 16 * 16)
            wmap = {0: d["wq"], 1: d["wk"], 2: d["wv"]}
            for c in range(12):
                w = wmap[c // 4]
                cc = c % 4
                if c >= 2:
                    e.wait_ge(sem["vWc"], c - 1)
                e.dma_start(s.stage[:, (c % 2) * 2048:(c % 2) * 2048 + 2048]
                            .rearrange("p (a dd) -> p a dd", a=2),
                            w.rearrange("(a p) dd -> p a dd", p=128)[:, 2 * cc:2 * cc + 2, :]
                            ).then_inc(sem["dWgt0" if c % 2 == 0 else "dWgt1"], 16)
        e.sem_inc(bar, 1)
        if "CMB" in self.stages:
            outr = d["out"].rearrange("(i p) c -> p i c", p=128)
            vauo = s.Vau.bitcast(F32)
            e.wait_ge(sem["vWc"], 12)
            for k2 in range(NT):
                i = NT - 1 - k2
                if k2 >= 2:
                    e.wait_ge(sem["vCM"], k2 - 1)
                e.dma_start(xb[k2 % 2], xr[:, i, :]
                            ).then_inc(sem["dCX0" if k2 % 2 == 0 else "dCX1"], 16)
                if k2 >= 1:
                    e.wait_ge(sem["vCM"], k2)
                    e.dma_start(outr[:, i + 1, :],
                                vauo[:, ((k2 - 1) % 2) * 1024:
                                     ((k2 - 1) % 2) * 1024 + 1024]
                                ).then_inc(sem["dOut0" if (k2 - 1) % 2 == 0
                                           else "dOut1"], 16)
            e.wait_ge(sem["vCM"], NT)
            e.dma_start(outr[:, 0, :],
                        vauo[:, ((NT - 1) % 2) * 1024:((NT - 1) % 2) * 1024 + 1024]
                        ).then_inc(sem["dOut0" if (NT - 1) % 2 == 0 else "dOut1"], 16)
        e.wait_ge(bar, 15)
        if self.debug:
            self.debug_dmas(e)
        finals = {}
        if "P0" in self.stages:
            finals.update(dX0=16 * 8, dX1=16 * 8, dX20=16 * 8, dX21=16 * 8,
                          dW=32, dP0=16 * 12, dP1=16 * 12)
        if "P1" in self.stages:
            finals.update(dOp0=16 * 2, dOp1=16 * 2, dG0=16 * 16, dG1=16 * 16)
        if "P3" in self.stages:
            finals.update(dWgt0=16 * 6, dWgt1=16 * 6, dS30=16 * 32, dS31=16 * 32)
        if "CMB" in self.stages:
            finals.update(dCX0=16 * 16, dCX1=16 * 16, dCG0=16 * 16, dCG1=16 * 16,
                          dCS0=16 * 16, dCS1=16 * 16,
                          dOut0=16 * 16, dOut1=16 * 16)
        for k, v in finals.items():
            e.wait_ge(sem[k], v)

    def debug_dmas(self, e):
        s, d, sem = self, self.d, self.sem
        n = 0
        pairs = []
        if "P0" in self.stages:
            pairs += [("dbg_lnat", s.lnat[:]), ("dbg_mask", s.msk_all[:]),
                      ("dbg_rw", s.rw_all[:]), ("dbg_invr", s.invr_all[:]),
                      ("dbg_snn", s.snn[:]), ("dbg_xnT", s.xnT[:, 0:4096])]
        if "P1" in self.stages:
            pairs += [("dbg_qfT", s.R3[:, 0:4096]), ("dbg_kfT", s.R3[:, 4096:8192]),
                      ("dbg_vfa", s.R3[:, 8192:12352]), ("dbg_ao", s.R3[:, 12352:16448]),
                      ("dbg_rwp", s.rwp_all[:])]
        if "P3" in self.stages:
            pairs += [("dbg_QT", s.QT[:]), ("dbg_KT", s.KT[:]),
                      ("dbg_Vau", s.Vau[:]), ("dbg_dup", s.Dup[:])]
        for nm, src in pairs:
            e.dma_start(d[nm][:, :], src).then_inc(sem["dDbg"], 16)
            n += 1
        if "P0" in self.stages:
            e.dma_start(d["dbg_mv"][:, 0:64], s.Dup[:]).then_inc(sem["dDbg"], 16)
            n += 1
        e.wait_ge(sem["dDbg"], 16 * n)

    def gpsimd_prog(self, e):
        s, d, sem = self, self.d, self.sem
        bar = sem["bar"]
        e.memset(s.onesb[:], 1.0)
        e.drain()
        e.affine_select(s.idt_bf[:], s.onesb[:], [[1, 128]], ALU.is_equal, 0.0,
                        base=0, channel_multiplier=-1)
        e.drain()
        e.engine_nop().then_inc(sem["gInit"], 1)
        if "P0" in self.stages:
            pmap = {0: d["qp"], 1: d["kp"], 2: d["vp"]}
            R3f = [s.R3[:, 0:8192].bitcast(F32)[:, 0:1024],
                   s.R3[:, 0:8192].bitcast(F32)[:, 1024:2048]]
            for idx in range(24):
                w = pmap[idx // 8]
                a = idx % 8
                if idx >= 2:
                    e.wait_ge(sem["vPc"], idx - 1)
                e.dma_start(R3f[idx % 2],
                            w.rearrange("ee (a p) dd -> p a ee dd", p=128)
                            [:, a:a + 1, :, :]).then_inc(sem["dP0" if idx % 2 == 0 else "dP1"], 16)
        e.drain()
        e.sem_inc(bar, 1)
        e.wait_ge(bar, 5)
        if "P1" in self.stages:
            for j in range(8):
                if j >= 1:
                    e.wait_ge(sem["vUf"], j * 4)
                for qt in range(4):
                    t = 4 * j + qt
                    g = c2_g(j, t)
                    e.wait_ge(sem["aEf"], g + 1)
                    e.affine_select(
                        s.maskedf[:, qt * 128:qt * 128 + 128],
                        s.expC[:, (g % 4) * 512 + qt * 128:(g % 4) * 512 + qt * 128 + 128],
                        [[1, 128]], ALU.is_ge, 0.0,
                        base=0, channel_multiplier=-1).then_inc(sem["gMf"], 1)
            e.wait_ge(sem["sCv"], 8 * NT)
            for c in range(4):
                if c >= 2:
                    e.wait_ge(sem["vOpc"], c - 1)
                e.dma_start(s.stage[:, (c % 2) * 2048:(c % 2) * 2048 + 2048]
                            .rearrange("p (ee cc) -> p ee cc", ee=2),
                            d["op"].rearrange("ee p cc -> p ee cc")[:, 2 * c:2 * c + 2, :]
                            ).then_inc(sem["dOp0" if c % 2 == 0 else "dOp1"], 16)
        e.drain()
        e.sem_inc(bar, 1)
        e.wait_ge(bar, 10)
        if "P3" in self.stages:
            for p in range(NPAIR):
                for it in P3_PAIRS[p]:
                    if it["kind"] != "C2":
                        continue
                    j, t = it["j"], it["t"]
                    if t >= 4 * j:
                        qt = t - 4 * j
                        es = it["eslot"]
                        e.wait_ge(sem["aC2"], it["aC2"])
                        for h in range(2):
                            prevu = p * 64 + (6 - j) * 8 + qt * 2 + h + 1
                            if prevu >= 1:
                                e.wait_ge(sem["tU2"], prevu)
                            e.affine_select(
                                s.mask2[:, h * 512 + qt * 128: h * 512 + qt * 128 + 128],
                                s.expCs(es)[:, h * 512 + qt * 128:
                                            h * 512 + qt * 128 + 128],
                                [[1, 128]], ALU.is_ge, 0.0,
                                base=0, channel_multiplier=-1).then_inc(sem["gC2"], 1)
                    if t == 4 * j + 3:
                        e.wait_ge(sem["vC2"], p * 64 + (8 - j) * 8)
                        e.dma_start(
                            self.d["s_sp"].rearrange(
                                "(jj qt p2) (pr h dd) -> p2 jj qt pr h dd",
                                p2=128, jj=8, pr=8, h=2)[:, j, :, p, :, :],
                            s.s_stage[:, (j % 2) * 512:(j % 2) * 512 + 512]
                            .rearrange("p (qt h dd) -> p qt h dd", qt=4, h=2)
                        ).then_inc(sem["dS30" if j % 2 == 0 else "dS31"], 16)
                        if p == NPAIR - 1 and "CMB" in self.stages:
                            self.gpsimd_cmb_q(e, j)
        e.drain()
        e.sem_inc(bar, 1)
        e.wait_ge(bar, 15)

    def vector_cmb_one(self, e, j, b):
        s, sem = self, self.sem
        xb = [s.stage[:, 0:1024], s.stage[:, 1024:2048]]
        vauo = s.Vau.bitcast(F32)
        k2 = (7 - j) * 4 + b
        sl = k2 % 2
        e.wait_ge(sem["dCX0" if sl == 0 else "dCX1"], 16 * (k2 // 2 + 1))
        e.wait_ge(sem["dCG0" if sl == 0 else "dCG1"], 16 * (k2 // 2 + 1))
        e.wait_ge(sem["dCS0" if sl == 0 else "dCS1"], 16 * (k2 // 2 + 1))
        if k2 >= 2:
            e.wait_ge(sem["dOut0" if sl == 0 else "dOut1"], 16 * (k2 // 2))
        e.tensor_tensor(s.xnb[:, sl * 1024:sl * 1024 + 1024],
                        s.KT[:, sl * 1024:sl * 1024 + 1024],
                        s.QT[:, sl * 1024:sl * 1024 + 1024],
                        ALU.mult)
        e.drain()
        e.tensor_tensor(vauo[:, sl * 1024:sl * 1024 + 1024],
                        s.xnb[:, sl * 1024:sl * 1024 + 1024],
                        xb[sl], ALU.add).then_inc(sem["vCM"], 1)

    def gpsimd_cmb_q(self, e, j):
        s, d, sem = self, self.d, self.sem
        gr = d["g_sp"].rearrange("(i p) c -> p i c", p=128)
        sr = d["s_sp"].rearrange("(i2 p2) dd -> p2 i2 dd", p2=128)
        spar = j % 2
        scount = 16 * (28 + ((6 if spar == 0 else 7) - j) // 2 + 1)
        for b in range(4):
            k2 = (7 - j) * 4 + b
            i = NT - 1 - k2
            if k2 == 0:
                e.wait_ge(sem["dG0"], 16 * 16)
                e.wait_ge(sem["dG1"], 16 * 16)
            if k2 >= 2:
                e.wait_ge(sem["vCM"], k2 - 1)
            e.wait_ge(sem["dS30" if spar == 0 else "dS31"], scount)
            e.dma_start(s.QT[:, (k2 % 2) * 1024:(k2 % 2) * 1024 + 1024],
                        gr[:, i, :]).then_inc(sem["dCG0" if k2 % 2 == 0
                                              else "dCG1"], 16)
            e.dma_start(s.KT[:, (k2 % 2) * 1024:(k2 % 2) * 1024 + 1024],
                        sr[:, i, :]).then_inc(sem["dCS0" if k2 % 2 == 0
                                              else "dCS1"], 16)

    def tensor_prog(self, e):
        s, sem = self, self.sem
        bar = sem["bar"]
        PB, PBb = s.PB, s.PBb
        if "P0" in self.stages:
            e.wait_ge(sem["gInit"], 1)
            e.wait_ge(sem["vInit"], 1)
            e.wait_ge(sem["dW"], 32)
            last = None
            for a in range(8):
                last = e.transpose(PB[0][0:8, a * 128:(a + 1) * 128],
                                   s.sim_in[:, a * 8:(a + 1) * 8], s.idt_f[:])
            last.then_inc(sem["tSim"], 1)
            for i in range(NT):
                e.wait_ge(sem["vXN"], i + 1)
                if i >= 2:
                    e.wait_ge(sem["vEv"], i - 1)
                slot = PBb[3] if i % 2 == 0 else PBb[7]
                last = None
                for ct in range(8):
                    last = e.transpose(slot[:, ct * 128:(ct + 1) * 128],
                                       s.xnb[:, (i % 2) * 1024 + ct * 128:
                                             (i % 2) * 1024 + (ct + 1) * 128],
                                       s.idt_bf[:])
                last.then_inc(sem["tTR"], 1)
                if i == 3:
                    e.wait_ge(sem["vSnn3"], 1)
                    last = None
                    for a in range(8):
                        last = e.transpose(PBb[2][:, a * 8:(a + 1) * 8],
                                           s.snnT[0:8, a * 128:(a + 1) * 128],
                                           s.idt_bf[0:8, 0:8])
                    last.then_inc(sem["tSnn"], 1)
                    e.wait_ge(sem["aSg"], 1)
                    e.wait_ge(sem["vInit"], 1)
                    e.matmul(PB[1][:, 0:8], s.ones1[0:1, :], s.sg_row[0:1, :],
                             start=True, stop=True).then_inc(sem["tSgb"], 1)
            e.wait_ge(sem["vEv"], NT)
            e.wait_ge(sem["vSnnF"], 1)
            e.wait_ge(sem["vSgb"], 1)
            for j in range(8):
                if j >= 2:
                    e.wait_ge(sem["aLT"], j - 1)
                last = None
                for ct in range(8):
                    last = e.matmul(PB[1][0:8, (j % 2) * 512:(j % 2) * 512 + 512],
                                    s.snn[:, ct * 8:(ct + 1) * 8],
                                    s.xnT[:, ct * 4096 + j * 512: ct * 4096 + (j + 1) * 512],
                                    start=(ct == 0), stop=(ct == 7))
                last.then_inc(sem["tL"], 1)
            lgT = s.R3[:, 8192:16384].bitcast(F32)
            e.wait_ge(sem["aLT"], 8)
            e.wait_ge(sem["vSgb"], 1)
            last = None
            for i in range(NT):
                last = e.transpose(PB[2][:, i * 8:(i + 1) * 8],
                                   lgT[0:8, i * 128:(i + 1) * 128], s.idt_f[0:8, 0:8])
            last.then_inc(sem["tLN"], 1)
        e.sem_inc(bar, 1)
        if "P1" in self.stages:
            if "P0" in self.stages:
                e.wait_ge(sem["vSim1"], 1)
                e.wait_ge(sem["aLT"], 8)
                e.wait_ge(sem["vSgb"], 1)
                e.wait_ge(sem["vLNat"], 1)
                e.wait_ge(sem["vEv"], NT)
                e.wait_ge(sem["vPc"], 24)
            else:
                e.wait_ge(bar, 5)
            self.tensor_p1(e)
        e.sem_inc(bar, 1)
        e.wait_ge(bar, 10)
        if "P3" in self.stages:
            self.tensor_p3(e)
        e.sem_inc(bar, 1)
        e.wait_ge(bar, 15)

    def tensor_p1(self, e):
        s, sem = self, self.sem
        PB, PBb = s.PB, s.PBb
        qp_r = s.R2[:, 0:8192]
        kp_r = s.R2[:, 8192:16384]
        vp_r = s.R2[:, 16384:24576]
        qfT = s.R3[:, 0:4096]
        kfT = s.R3[:, 4096:8192]
        vfa = s.R3[:, 8192:12352]

        def trans_qk(qt):
            e.wait_ge(sem["sCk"], 8 * (qt + 1))
            if qt >= 2:
                e.wait_ge(sem["aT2"], qt - 1)
            slot = PBb[3] if qt % 2 == 0 else PBb[7]
            e.transpose(slot[:, 0:128],
                        s.qk_tmp[:, (qt % 2) * 512:(qt % 2) * 512 + 128], s.idt_bf[:])
            e.transpose(slot[:, 128:256],
                        s.qk_tmp[:, (qt % 2) * 512 + 128:(qt % 2) * 512 + 256],
                        s.idt_bf[:]).then_inc(sem["tT2"], 1)

        for qt in range(NT):
            for src, pb, tname, vname in ((qp_r, 0, "tA2q", "vA2q"),
                                          (kp_r, 1, "tA2k", "vA2k"),
                                          (vp_r, 2, "tA2v", "vA2v")):
                if qt >= 1:
                    e.wait_ge(sem[{"tA2q": "sCq", "tA2k": "sCk", "tA2v": "sCv"}[tname]], 8 * qt)
                last = None
                for ct in range(8):
                    for half in range(2):
                        last = e.matmul(
                            PB[pb][:, half * 512:(half + 1) * 512],
                            s.xnT[:, ct * 4096 + qt * 128: ct * 4096 + (qt + 1) * 128],
                            src[:, ct * 1024 + half * 512: ct * 1024 + (half + 1) * 512],
                            start=(ct == 0), stop=(ct == 7), skip_group_check=True)
                last.then_inc(sem[tname], 1)
            if qt >= 1:
                trans_qk(qt - 1)
        trans_qk(NT - 1)
        e.wait_ge(sem["aT2"], NT)
        e.wait_ge(sem["sCq"], 8 * NT)
        e.wait_ge(sem["sCv"], 8 * NT)

        SCF = [PB[0][:, 0:512], PB[0][:, 512:1024],
               PB[2][:, 0:512], PB[2][:, 512:1024]]

        def pv_f(j, t):
            g = c2_g(j, t)
            e.wait_ge(sem["aEf"], g + 1)
            if t == 0 and j >= 1:
                e.wait_ge(sem["vUf"], j * 4)
            for qt in range(4):
                qtile = 4 * j + qt
                if t > qtile:
                    continue
                first = (t == 0 and qt in (0, 2))
                if t == qtile:
                    e.wait_ge(sem["gMf"], j * 4 + qt + 1)
                    stat = s.maskedf[:, qt * 128:qt * 128 + 128]
                else:
                    stat = s.expC[:, (g % 4) * 512 + qt * 128:(g % 4) * 512 + qt * 128 + 128]
                mm = e.matmul(PB[1][:, qt * 256:qt * 256 + 129],
                              stat, vfa[:, t * 130:t * 130 + 129],
                              start=first, stop=(t == qtile), skip_group_check=True)
                if t == qtile:
                    mm.then_inc(sem["tUf"], 1)

        pendf = []
        for j in range(8):
            T = 4 * j + 4
            for t in range(T):
                g = c2_g(j, t)
                if g >= 4:
                    e.wait_ge(sem["aEf"], g - 3)
                e.matmul(SCF[g % 4],
                         kfT[:, t * 128:(t + 1) * 128],
                         qfT[:, j * 512:(j + 1) * 512],
                         start=True, stop=True).then_inc(sem["tSf"], 1)
                if len(pendf) >= 2:
                    pv_f(*pendf.pop(0))
                pendf.append((j, t))
        for pv in pendf:
            pv_f(*pv)
        op_r = s.R2[:, 0:8192]
        aoT = [s.Vau[:, 0:1024], s.Vau[:, 1024:2048]]
        e.wait_ge(sem["vOpc"], 4)
        e.wait_ge(sem["aEf"], C2_PER)
        e.wait_ge(sem["vUf"], NT)
        e.wait_ge(sem["aT2"], NT)
        for qt in range(NT):
            e.wait_ge(sem["vA4"], qt + 1)
            if qt >= 2:
                e.wait_ge(sem["aE4"], qt - 1)
            slot = PBb[3] if qt % 2 == 0 else PBb[7]
            last = None
            for ei in range(8):
                last = e.transpose(slot[:, ei * 128:(ei + 1) * 128],
                                   s.xnb[:, (qt % 2) * 1024 + ei * 128:
                                         (qt % 2) * 1024 + (ei + 1) * 128],
                                   s.idt_bf[:])
            last.then_inc(sem["tT4"], 1)
            e.wait_ge(sem["aE4"], qt + 1)
            if qt >= 2:
                e.wait_ge(sem["aS4"], qt - 1)
            pbo = PB[0] if qt % 2 == 0 else PB[1]
            last = None
            for ei in range(8):
                for half in range(2):
                    last = e.matmul(pbo[:, half * 512:(half + 1) * 512],
                                    aoT[qt % 2][:, ei * 128:(ei + 1) * 128],
                                    op_r[:, ei * 1024 + half * 512:
                                         ei * 1024 + (half + 1) * 512],
                                    start=(ei == 0), stop=(ei == 7),
                                    skip_group_check=True)
            last.then_inc(sem["tO4"], 1)

    def c1scr(self, s3):
        return (self.dex[:, 0:1024] if s3 == 0 else
                self.dex[:, 1024:2048] if s3 == 1 else
                self.R3[:, 12352:13376] if s3 == 2 else
                self.c1x[:, 0:1024])

    def expCs(self, es):
        return (self.expC[:, es * 1024:es * 1024 + 1024] if es < 2
                else self.expC3[:, 0:1024])

    def tensor_p3(self, e):
        s, sem = self, self.sem
        PB, PBb = s.PB, s.PBb
        Wq = s.R2[:, 0:8192]
        Wk = s.R2[:, 8192:16384]
        Wv = s.R2[:, 16384:24576]
        QTb = [s.QT, s.R3[:, 0:4096]]
        KTb = [s.KT, s.R3[:, 4096:8192]]
        Vaub = [s.Vau, s.R3[:, 8192:12352]]

        POOL = [PB[0], PB[1], PB[3]]
        POOLB = [PBb[4], PBb[5], PBb[7]]

        def pb3_wait(it):
            w = it.get("pool_wait")
            if w is not None:
                e.wait_ge(sem[w[0]], w[1])

        def emit_qkva(it):
            pp, n = it["pp"], it["n"]
            pb3_wait(it)
            last = None
            for ct in range(8):
                mv = s.xnT[:, ct * 4096 + n * 512: ct * 4096 + (n + 1) * 512]
                e.matmul(POOL[it["slot"]][:, 0:512],
                         Wq[:, ct * 1024 + pp * 128: ct * 1024 + (pp + 1) * 128],
                         mv, start=(ct == 0), stop=(ct == 7), skip_group_check=True)
                last = e.matmul(POOL[it["slot"]][:, 512:1024],
                                Wk[:, ct * 1024 + pp * 128: ct * 1024 + (pp + 1) * 128],
                                mv, start=(ct == 0), stop=(ct == 7),
                                skip_group_check=True)
            last.then_inc(sem["tQ3"], 1)

        def emit_qkvb(it):
            pp, n = it["pp"], it["n"]
            pb3_wait(it)
            last = None
            for ct in range(8):
                mv = s.xnT[:, ct * 4096 + n * 512: ct * 4096 + (n + 1) * 512]
                last = e.matmul(POOL[it["slot"]][:, 0:512],
                                Wv[:, ct * 1024 + pp * 128: ct * 1024 + (pp + 1) * 128],
                                mv, start=(ct == 0), stop=(ct == 7),
                                skip_group_check=True)
            last.then_inc(sem["tQ3"], 1)
            e.wait_ge(sem["vQ3"], it["vQ3"])
            last = None
            for b in range(4):
                last = e.transpose(POOLB[it["slot"]][:, b * 128:(b + 1) * 128],
                                   s.vtt[:, b * 128:(b + 1) * 128], s.idt_bf[:])
            last.then_inc(sem["tV3"], 1)

        for it in P3_PRE:
            if it["kind"] == "QKVa":
                if it["pp"] == 0 and it["n"] == 0:
                    e.wait_ge(sem["vWc"], 12)
                emit_qkva(it)
            else:
                emit_qkvb(it)

        for p in range(NPAIR):
            par = p % 2
            QT, KT, Vau = QTb[par], KTb[par], Vaub[par]
            e.wait_ge(sem["vQ3"], 16 * (p + 1))
            e.wait_ge(sem["vV3"], 8 * (p + 1))

            def pv_outer(j, t, g, es):
                e.wait_ge(sem["aC2"], g)
                if t == 0:
                    e.wait_ge(sem["vC2"], p * 64 + (7 - j) * 8)
                for qt in range(4):
                    qtile = 4 * j + qt
                    if t > qtile:
                        continue
                    if t == qtile:
                        e.wait_ge(sem["gC2"],
                                  p * 64 + (7 - j) * 8 + (t - 4 * j + 1) * 2)
                    for h in range(2):
                        first = (t == 0 and qt == 0)
                        if t == qtile:
                            stat = s.mask2[:, h * 512 + qt * 128: h * 512 + qt * 128 + 128]
                            e.matmul(PB[2][:, h * 512 + qt * 128: h * 512 + qt * 128 + 64],
                                     stat,
                                     Vau[:, t * 130 + h * 65: t * 130 + h * 65 + 64],
                                     start=first, stop=False, skip_group_check=True)
                            statf = s.expCs(es)[:, h * 512 + qt * 128:
                                                h * 512 + qt * 128 + 128]
                            mm = e.matmul(
                                PB[2][:, h * 512 + qt * 128 + 64: h * 512 + qt * 128 + 65],
                                statf,
                                Vau[:, t * 130 + h * 65 + 64: t * 130 + h * 65 + 65],
                                start=False, stop=True, skip_group_check=True)
                            mm.then_inc(sem["tU2"], 1)
                        else:
                            stat = s.expCs(es)[:, h * 512 + qt * 128:
                                               h * 512 + qt * 128 + 128]
                            e.matmul(PB[2][:, h * 512 + qt * 128: h * 512 + qt * 128 + 65],
                                     stat,
                                     Vau[:, t * 130 + h * 65: t * 130 + h * 65 + 65],
                                     start=first, stop=False, skip_group_check=True)

            pend = []
            for it in P3_PAIRS[p]:
                k = it["kind"]
                if k == "C2":
                    j, t = it["j"], it["t"]
                    pb3_wait(it)
                    last = None
                    for h in range(2):
                        stat = KT[64 * h:64 * h + 64, t * 128:(t + 1) * 128]
                        mv = QT[64 * h:64 * h + 64, j * 512:(j + 1) * 512]
                        last = e.matmul(POOL[it["slot"]][:, h * 512:h * 512 + 512],
                                        stat, mv, start=True, stop=True,
                                        tile_position=(64 * h, 0))
                    last.then_inc(sem["tC2"], 1)
                    if len(pend) >= 2:
                        pv_outer(*pend.pop(0))
                    pend.append((j, t, it["aC2"], it["eslot"]))
                elif k == "C1":
                    pb3_wait(it)
                    i, k0, kw = it["i"], it["k0"], it["kw"]
                    last = None
                    for h in range(2):
                        statq = QT[64 * h:64 * h + 64, i * 128:(i + 1) * 128]
                        mvk = KT[64 * h:64 * h + 64, k0:k0 + kw]
                        last = e.matmul(POOL[it["slot"]][:, h * 512:h * 512 + kw],
                                        statq, mvk, start=True, stop=True,
                                        tile_position=(64 * h, 0))
                    last.then_inc(sem["tC1"], 1)
                elif k == "QKVa":
                    emit_qkva(it)
                else:
                    emit_qkvb(it)
            for pv in pend:
                pv_outer(*pv)

    def vector_prog(self, e):
        s, sem = self, self.sem
        bar = sem["bar"]
        PB, PBb = s.PB, s.PBb
        xb = [s.stage[:, 0:1024], s.stage[:, 1024:2048]]
        if "P0" in self.stages:
            e.wait_ge(sem["gInit"], 1)
            e.tensor_copy(s.idt_f[:], s.idt_bf[:])
            e.memset(s.ones1[:], 1.0)
            e.drain().then_inc(sem["vInit"], 1)
            e.wait_ge(sem["tSim"], 1)
            e.tensor_copy(s.simT[0:8, :], PB[0][0:8, 0:1024]).then_inc(sem["vSim1"], 1)
            e.drain()
            e.tensor_tensor(s.sq8[0:8, :], s.simT[0:8, :], s.simT[0:8, :], ALU.mult)
            e.drain()
            e.reduce_sum(s.n2[0:8, :], s.sq8[0:8, :], axis=AX.X).then_inc(sem["vSim2"], 1)
            e.wait_ge(sem["aSim"], 1)
            e.tensor_scalar(s.invn[0:8, :], s.invn_r[0:8, :], 1.0e12, None, ALU.min)
            e.drain()
            e.tensor_scalar(s.snnT[0:8, :], s.simT[0:8, :], s.invn[0:8, :], None,
                            ALU.mult).then_inc(sem["vSnn3"], 1)
            xb3 = [s.stage[:, 0:1024], s.stage[:, 1024:2048],
                   s.stage[:, 2048:3072], s.stage[:, 3072:4096]]
            dxs = ["dX0", "dX1", "dX20", "dX21"]
            mvD = s.Dup[:].rearrange("p (i two) -> p two i", two=2)
            bn6 = s.tmpc[:, 0:12]

            def emit_xn(j):
                e.wait_ge(sem["aRS"], j // 2 + 1)
                if j >= 2:
                    e.wait_ge(sem["tTR"], j - 1)
                e.tensor_scalar(s.xnb[:, (j % 2) * 1024:(j % 2) * 1024 + 1024],
                                xb3[j % 4], s.Dup[:, 2 * j:2 * j + 1],
                                s.rstd_all[:, j:j + 1],
                                ALU.subtract, ALU.mult).then_inc(sem["vXN"], 1)

            def emit_ev(j):
                e.wait_ge(sem["tTR"], j + 1)
                slot = PBb[3] if j % 2 == 0 else PBb[7]
                e.tensor_copy(
                    s.xnT[:].rearrange("p (ct n) -> p ct n", ct=8)
                    [:, :, j * 128:(j + 1) * 128],
                    slot[:].rearrange("p (ct n) -> p ct n", ct=8)
                ).then_inc(sem["vEv"], 1)

            for i in range(NT):
                e.wait_ge(sem[dxs[i % 4]], 16 * (i // 4 + 1))
                e.bn_stats(bn6[:, 0:6], xb3[i % 4][:, 0:512])
                e.bn_stats(bn6[:, 6:12], xb3[i % 4][:, 512:1024])
                e.drain()
                e.bn_aggr(s.Dup[:, 2 * i:2 * i + 2], bn6)
                e.drain()
                if i % 2 == 1:
                    g = i // 2
                    e.tensor_scalar(s.veps[:, 2 * g:2 * g + 2],
                                    mvD[:, 1, 2 * g:2 * g + 2], 1.0e-5, None,
                                    ALU.add)
                    e.drain().then_inc(sem["vM"], 1)
                if i >= 2:
                    emit_xn(i - 2)
                if i >= 3:
                    emit_ev(i - 3)
            emit_xn(NT - 2)
            emit_ev(NT - 3)
            emit_xn(NT - 1)
            emit_ev(NT - 2)
            emit_ev(NT - 1)
            e.wait_ge(sem["aP0b1"], 1)
            e.tensor_tensor(s.ddt[:], s.lnv[:], s.lna[:], ALU.subtract
                            ).then_inc(sem["vP0b2"], 1)
            e.wait_ge(sem["aP0b2"], 1)
            e.tensor_scalar(s.invr_all[:], s.invr_raw[:], 1.0e12, None, ALU.min
                            ).then_inc(sem["vP0b3"], 1)
            R3f = [s.R3[:, 0:8192].bitcast(F32)[:, 0:1024],
                   s.R3[:, 0:8192].bitcast(F32)[:, 1024:2048]]
            for idx in range(24):
                e.wait_ge(sem["dP0" if idx % 2 == 0 else "dP1"], 16 * (idx // 2 + 1))
                w = idx // 8
                a = idx % 8
                e.tensor_copy(s.R2[:, w * 8192 + a * 1024: w * 8192 + (a + 1) * 1024],
                              R3f[idx % 2]).then_inc(sem["vPc"], 1)
            e.wait_ge(sem["tSnn"], 1)
            e.tensor_copy(s.snn[:], PBb[2][:, 0:64]).then_inc(sem["vSnnF"], 1)
            e.wait_ge(sem["tSgb"], 1)
            e.tensor_copy(s.sgb[:], PB[1][:, 0:8]).then_inc(sem["vSgb"], 1)
            e.wait_ge(sem["tLN"], 1)
            e.tensor_copy(s.lnat[:], PB[2][:, 0:256]).then_inc(sem["vLNat"], 1)
            e.drain()
            e.wait_ge(sem["vP0b3"], 1)
            A3 = lambda t: t[:].rearrange("p (q ee) -> p q ee", ee=8)
            bcq = lambda t: bass.AP(t, 0, [[32, 128], [1, 32], [0, 8]])
            sgb_b = bass.AP(s.sgb, 0, [[8, 128], [0, 32], [1, 8]])
            e.tensor_tensor(A3(s.mq_all), A3(s.lnat), bcq(s.invr_all), ALU.mult)
            e.drain()
            e.tensor_tensor(A3(s.rwp_all), A3(s.mq_all), sgb_b, ALU.subtract)
            e.drain()
            e.tensor_reduce(s.lna[:], A3(s.rwp_all), AX.X, ALU.max)
            e.drain()
            e.tensor_tensor(A3(s.mq_all), A3(s.rwp_all), bcq(s.lna), ALU.is_ge)
            e.drain()
            e.scalar_tensor_tensor(A3(s.expg_all), A3(s.mq_all), -1.0e9,
                                   A3(s.rwp_all), ALU.mult, ALU.add)
            e.drain()
            e.tensor_reduce(s.lnv[:], A3(s.expg_all), AX.X, ALU.max)
            e.drain()
            e.tensor_tensor(A3(s.expg_all), A3(s.rwp_all), bcq(s.lnv), ALU.is_ge)
            e.tensor_scalar(s.ddt[:], s.lna[:], 0.0, None, ALU.is_le)
            e.drain()
            e.tensor_scalar(s.mq_all[:], s.rwp_all[:], 0.0, None, ALU.is_gt)
            e.tensor_tensor(A3(s.gated_all), A3(s.expg_all), bcq(s.ddt), ALU.mult)
            e.drain()
            e.tensor_tensor(s.msk_all[:], s.gated_all[:], s.mq_all[:], ALU.add)
            e.drain()
            e.tensor_scalar(s.gated_all[:], s.rwp_all[:], 0.0, None, ALU.max)
            e.drain().then_inc(sem["vRt1"], 1)
            e.wait_ge(sem["aRt"], 1)
            e.tensor_tensor(s.rwn_all[:], s.msk_all[:], s.expg_all[:], ALU.mult)
            e.drain()
            e.reduce_sum(s.Z_all[:], s.rwn_all[:].rearrange("p (q ee) -> p q ee", ee=8),
                         axis=AX.X)
            e.drain()
            e.reciprocal(s.Zi_all[:], s.Z_all[:])
            e.drain()
            bcq2 = lambda t: bass.AP(t, 0, [[32, 128], [1, 32], [0, 8]])
            e.tensor_tensor(s.rw_all[:].rearrange("p (q ee) -> p q ee", ee=8),
                            s.rwn_all[:].rearrange("p (q ee) -> p q ee", ee=8),
                            bcq2(s.Zi_all), ALU.mult)
            e.drain()
            e.tensor_scalar(s.mq_all[:], s.msk_all[:], INV_SQRT_D, None, ALU.mult
                            ).then_inc(sem["vRt2"], 1)
        e.drain()
        e.sem_inc(bar, 1)
        e.wait_ge(bar, 5)
        if "P1" in self.stages:
            self.vector_p1(e)
        e.drain()
        e.sem_inc(bar, 1)
        e.wait_ge(bar, 10)
        if "P3" in self.stages:
            self.vector_p3(e)
        e.drain()
        e.sem_inc(bar, 1)
        e.wait_ge(bar, 15)

    def vector_p1(self, e):
        s, sem = self, self.sem
        PB, PBb = s.PB, s.PBb
        vfa = s.R3[:, 8192:12352]
        ao_all = s.R3[:, 12352:16448]
        qfT = s.R3[:, 0:4096]
        kfT = s.R3[:, 4096:8192]
        e.memset(vfa.rearrange("p (nt w) -> p nt w", w=130)[:, :, 128:130], 1.0)
        for qt in range(NT):
            specs = ((0, "tA2q", "sCq", s.mq_all, s.cacc,
                      s.qk_tmp[:, (qt % 2) * 512:(qt % 2) * 512 + 128]),
                     (1, "tA2k", "sCk", s.msk_all, s.cacc2,
                      s.qk_tmp[:, (qt % 2) * 512 + 128:(qt % 2) * 512 + 256]),
                     (2, "tA2v", "sCv", s.msk_all, s.cacc3,
                      vfa[:, qt * 130:qt * 130 + 128]))
            if qt >= 2:
                e.wait_ge(sem["tT2"], qt - 1)
            for pb, tname, cname, mask, acc, dst in specs:
                e.wait_ge(sem[tname], qt + 1)
                e.tensor_scalar(acc[:], PB[pb][:, 0:128],
                                mask[:, qt * 8:qt * 8 + 1], None, ALU.mult
                                ).then_inc(sem[cname], 1)
                for ei in range(1, 8):
                    e.wait_ge(sem[cname], qt * 8 + ei)
                    out = dst if ei == 7 else acc[:]
                    e.scalar_tensor_tensor(out, PB[pb][:, ei * 128:(ei + 1) * 128],
                                           mask[:, qt * 8 + ei:qt * 8 + ei + 1],
                                           acc[:], ALU.mult, ALU.add
                                           ).then_inc(sem[cname], 1)
            if qt >= 1:
                e.wait_ge(sem["tT2"], qt)
                slot = PBb[3] if (qt - 1) % 2 == 0 else PBb[7]
                e.tensor_copy(qfT[:, (qt - 1) * 128:qt * 128], slot[:, 0:128])
                e.tensor_copy(kfT[:, (qt - 1) * 128:qt * 128], slot[:, 128:256]
                              ).then_inc(sem["aT2"], 1)
        e.wait_ge(sem["tT2"], NT)
        slot = PBb[3] if (NT - 1) % 2 == 0 else PBb[7]
        e.tensor_copy(qfT[:, (NT - 1) * 128:NT * 128], slot[:, 0:128])
        e.tensor_copy(kfT[:, (NT - 1) * 128:NT * 128], slot[:, 128:256]
                      ).then_inc(sem["aT2"], 1)
        for j in range(8):
            e.wait_ge(sem["tUf"], (j + 1) * 4)
            for qt in range(4):
                e.reciprocal(s.dinv[:, qt:qt + 1], PB[1][:, qt * 256 + 128:qt * 256 + 129])
            e.drain()
            for qt in range(4):
                qtile = 4 * j + qt
                e.tensor_scalar(s.rwp_all[:, qtile * 8:qtile * 8 + 8],
                                s.rw_all[:, qtile * 8:qtile * 8 + 8],
                                s.dinv[:, qt:qt + 1], None, ALU.mult)
                e.tensor_copy(ao_all[:, qtile * 128:(qtile + 1) * 128],
                              PB[1][:, qt * 256:qt * 256 + 128])
            e.drain().then_inc(sem["vUf"], 4)
        for c in range(4):
            e.wait_ge(sem["dOp0" if c % 2 == 0 else "dOp1"], 16 * (c // 2 + 1))
            e.tensor_copy(s.R2[:, c * 2048:(c + 1) * 2048],
                          s.stage[:, (c % 2) * 2048:(c % 2) * 2048 + 2048]
                          ).then_inc(sem["vOpc"], 1)
        aoT = [s.Vau[:, 0:1024], s.Vau[:, 1024:2048]]
        for qt in range(NT):
            if qt >= 2:
                e.wait_ge(sem["tT4"], qt - 1)
            for ei in range(8):
                e.tensor_scalar(s.xnb[:, (qt % 2) * 1024 + ei * 128:
                                      (qt % 2) * 1024 + (ei + 1) * 128],
                                ao_all[:, qt * 128:(qt + 1) * 128],
                                s.rwp_all[:, qt * 8 + ei:qt * 8 + ei + 1], None,
                                ALU.mult)
            e.drain().then_inc(sem["vA4"], 1)
            if qt >= 1:
                e.wait_ge(sem["tT4"], qt)
                slot = PBb[3] if (qt - 1) % 2 == 0 else PBb[7]
                e.tensor_copy(aoT[(qt - 1) % 2], slot[:]).then_inc(sem["aE4"], 1)
        e.wait_ge(sem["tT4"], NT)
        slot = PBb[3] if (NT - 1) % 2 == 0 else PBb[7]
        e.tensor_copy(aoT[(NT - 1) % 2], slot[:]).then_inc(sem["aE4"], 1)

    def vector_p3(self, e):
        s, sem = self, self.sem
        PB, PBb = s.PB, s.PBb
        QTb = [s.QT, s.R3[:, 0:4096]]
        KTb = [s.KT, s.R3[:, 4096:8192]]
        Vaub = [s.Vau, s.R3[:, 8192:12352]]
        POOLV = [PB[0], PB[1], PB[3]]
        POOLVB = [PBb[4], PBb[5], PBb[7]]
        for c in range(12):
            e.wait_ge(sem["dWgt0" if c % 2 == 0 else "dWgt1"], 16 * (c // 2 + 1))
            w = c // 4
            cc = c % 4
            src_ = s.stage[:, (c % 2) * 2048:(c % 2) * 2048 + 2048]
            dst = s.R2[:, w * 8192 + cc * 2048: w * 8192 + (cc + 1) * 2048]
            if w == 0:
                e.tensor_scalar(dst, src_, QSCALE, None, ALU.mult).then_inc(sem["vWc"], 1)
            else:
                e.tensor_copy(dst, src_).then_inc(sem["vWc"], 1)
        for par in range(2):
            vv = Vaub[par].rearrange("p (nt w) -> p nt w", w=130)
            e.memset(vv[:, :, 64:65], 1.0)
            e.memset(vv[:, :, 129:130], 1.0)

        def emit_qkva(it):
            pp, n = it["pp"], it["n"]
            e.wait_ge(sem["tQ3"], it["tQ3"])
            e.tensor_copy(QTb[pp % 2][:, n * 512:(n + 1) * 512],
                          POOLV[it["slot"]][:, 0:512])
            e.tensor_copy(KTb[pp % 2][:, n * 512:(n + 1) * 512],
                          POOLV[it["slot"]][:, 512:1024]).then_inc(sem["vQ3"], 1)

        def emit_qkvb(it):
            pp, n = it["pp"], it["n"]
            e.wait_ge(sem["tQ3"], it["tQ3"])
            e.tensor_copy(s.vtt[:], POOLV[it["slot"]][:, 0:512]).then_inc(sem["vQ3"], 1)
            e.wait_ge(sem["tV3"], it["tV3"])
            last = None
            for b in range(4):
                nt = 4 * n + b
                last = e.tensor_copy(
                    Vaub[pp % 2][:, nt * 130:nt * 130 + 130]
                    .rearrange("p (a y) -> p a y", a=2)[:, :, 0:64],
                    POOLVB[it["slot"]][:, b * 128:(b + 1) * 128]
                    .rearrange("p (a y) -> p a y", a=2))
            last.then_inc(sem["vV3"], 1)

        for it in P3_PRE:
            (emit_qkva if it["kind"] == "QKVa" else emit_qkvb)(it)

        for p in range(NPAIR):
            emit_evac = self.emit_evac_fn(e, p)
            cmb_q = []
            drip = [0]

            def drip_cmb(force=False):
                drip[0] += 1
                if cmb_q and (force or drip[0] % 3 == 0):
                    self.vector_cmb_one(e, *cmb_q.pop(0))
            e.memset(s.Dup[:], 0.0)
            e.drain()
            for it in P3_PAIRS[p]:
                k = it["kind"]
                if k == "QKVa":
                    emit_qkva(it)
                    continue
                if k == "QKVb":
                    emit_qkvb(it)
                    continue
                if k == "C1":
                    i, c, kw = it["i"], it["c"], it["kw"]
                    m1 = it["aC1"]
                    e.wait_ge(sem["aC1"], m1)
                    e.tensor_reduce(
                        s.tmpc[:].rearrange("p (h c) -> p h c", h=2)[:, :, c:c + 1],
                        s.c1scr((m1 - 1) % 3)[:, 0:2 * kw]
                        .rearrange("p (h k) -> p h k", h=2),
                        AX.X, ALU.add).then_inc(sem["vC1r"], 1)
                    drip_cmb()
                    if not it["lastf"]:
                        continue
                    e.drain()
                    nchunks = c + 1
                    srcs = {hh: [s.tmpc[:, hh * 8 + cc:hh * 8 + cc + 1]
                                 for cc in range(nchunks)] for hh in range(2)}
                    lvl = 0
                    while max(len(srcs[0]), len(srcs[1])) > 1:
                        base = (lvl % 2) * 8
                        scr = 0
                        nxt = {0: [], 1: []}
                        for hh in range(2):
                            lst = srcs[hh]
                            for a2 in range(0, len(lst) - 1, 2):
                                dstc = s.tca[:, base + scr:base + scr + 1]
                                scr += 1
                                e.tensor_tensor(dstc, lst[a2], lst[a2 + 1], ALU.add)
                                nxt[hh].append(dstc)
                            if len(lst) % 2 == 1:
                                nxt[hh].append(lst[-1])
                        e.drain()
                        srcs = nxt
                        lvl += 1
                    for hh in range(2):
                        e.tensor_copy(s.Dup[:, i * 2 + hh:i * 2 + hh + 1], srcs[hh][0])
                    e.drain()
                    continue
                j, t = it["j"], it["t"]
                if t == 4 * j + 3:
                    emit_evac(j)
                    if p == NPAIR - 1 and "CMB" in self.stages:
                        cmb_q.extend((j, b) for b in range(4))
                else:
                    drip_cmb()
            while cmb_q:
                drip_cmb(force=True)

    def emit_evac_fn(self, e, p):
        s, sem = self, self.sem
        PB = s.PB

        def emit_evac(j):
            prevspill = p * 4 + (5 - j) // 2 + 1
            if prevspill >= 1:
                e.wait_ge(sem["dS30" if j % 2 == 0 else "dS31"], 16 * prevspill)
            e.wait_ge(sem["tU2"], p * 64 + (7 - j) * 8 + 8)
            dcols = bass.AP(PB[2], 64, [[1024, 128], [512, 2], [128, 4]])
            e.tensor_tensor(
                s.dtot[:].rearrange("p (q h) -> p h q", h=2),
                dcols,
                s.Dup[:, 8 * j:8 * j + 8].rearrange("p (q h) -> p h q", h=2),
                ALU.add)
            e.drain()
            e.reciprocal(s.dinv[:], s.dtot[:])
            e.drain()
            for qt in range(4):
                for h in range(2):
                    e.tensor_scalar(
                        s.s_stage[:, (j % 2) * 512 + qt * 128 + h * 64:
                                  (j % 2) * 512 + qt * 128 + h * 64 + 64],
                        PB[2][:, h * 512 + qt * 128: h * 512 + qt * 128 + 64],
                        s.dinv[:, qt * 2 + h:qt * 2 + h + 1], None, ALU.mult)
            e.drain().then_inc(sem["vC2"], 8)
        return emit_evac

    def scalar_prog(self, e):
        s, sem = self, self.sem
        bar = sem["bar"]
        PB, PBb = s.PB, s.PBb
        xb = [s.stage[:, 0:1024], s.stage[:, 1024:2048]]
        if "P0" in self.stages:
            e.wait_ge(sem["dW"], 32)
            e.activation(s.sg_row[0:1, :], s.gates_in[0:1, :], AF.Sigmoid
                         ).then_inc(sem["aSg"], 1)
            e.wait_ge(sem["vSim2"], 1)
            e.activation(s.n2b[0:8, :], s.n2[0:8, :], AF.Ln)
            e.drain()
            e.activation(s.invn_r[0:8, :], s.n2b[0:8, :], AF.Exp, scale=-0.5
                         ).then_inc(sem["aSim"], 1)
            mvD = s.Dup[:].rearrange("p (i two) -> p two i", two=2)
            for g in range(16):
                e.wait_ge(sem["vM"], g + 1)
                e.activation(s.lnv[:, 2 * g:2 * g + 2],
                             s.veps[:, 2 * g:2 * g + 2], AF.Ln)
                e.drain()
                e.activation(s.rstd_all[:, 2 * g:2 * g + 2],
                             s.lnv[:, 2 * g:2 * g + 2], AF.Exp, scale=-0.5
                             ).then_inc(sem["aRS"], 1)
            e.activation(s.lna[:], mvD[:, 1, :], AF.Ln, scale=float(C)
                         ).then_inc(sem["aP0b1"], 1)
            e.wait_ge(sem["vP0b2"], 1)
            e.activation(s.invr_raw[:], s.ddt[:], AF.Exp, scale=0.5
                         ).then_inc(sem["aP0b2"], 1)
            lgT = s.R3[:, 8192:16384].bitcast(F32)
            for j in range(8):
                e.wait_ge(sem["tL"], j + 1)
                e.copy(lgT[0:8, j * 512:(j + 1) * 512],
                       PB[1][0:8, (j % 2) * 512:(j % 2) * 512 + 512]
                       ).then_inc(sem["aLT"], 1)
            e.wait_ge(sem["vRt1"], 1)
            e.activation(s.expg_all[:], s.gated_all[:], AF.Exp).then_inc(sem["aRt"], 1)
        e.drain()
        e.sem_inc(bar, 1)
        e.wait_ge(bar, 5)
        if "P1" in self.stages:
            for j in range(8):
                T = 4 * j + 4
                for t in range(T):
                    g = c2_g(j, t)
                    e.wait_ge(sem["tSf"], g + 1)
                    pbf = PB[0] if (g % 4) < 2 else PB[2]
                    e.activation(s.expC[:, (g % 4) * 512:(g % 4) * 512 + 512],
                                 pbf[:, (g % 2) * 512:(g % 2) * 512 + 512],
                                 AF.Exp).then_inc(sem["aEf"], 1)
            for qt in range(NT):
                e.wait_ge(sem["tO4"], qt + 1)
                if qt >= 2:
                    e.wait_ge(sem["dG0" if qt % 2 == 0 else "dG1"], 16 * (qt // 2))
                pbo = PB[0] if qt % 2 == 0 else PB[1]
                e.activation(s.gbuf[:, (qt % 2) * 1024:(qt % 2) * 1024 + 1024],
                             pbo[:, :], AF.Silu).then_inc(sem["aS4"], 1)
        e.drain()
        e.sem_inc(bar, 1)
        e.wait_ge(bar, 10)
        if "P3" in self.stages:
            for p in range(NPAIR):
                for it in P3_PAIRS[p]:
                    k = it["kind"]
                    if k == "C2":
                        es = it["eslot"]
                        e.wait_ge(sem["tC2"], it["tC2"])
                        e.activation(s.expCs(es),
                                     PB[[0, 1, 3][it["slot"]]][:, :],
                                     AF.Exp).then_inc(sem["aC2"], 1)
                    elif k == "C1":
                        kw = it["kw"]
                        m1 = it["aC1"]
                        if m1 == 1:
                            e.wait_ge(sem["dG0"], 16 * 16)
                            e.wait_ge(sem["dG1"], 16 * 16)
                        if m1 > 3:
                            e.wait_ge(sem["vC1r"], m1 - 3)
                        e.wait_ge(sem["tC1"], it["tC1"])
                        scr = self.c1scr((m1 - 1) % 3)
                        pbsrc = PB[[0, 1, 3][it["slot"]]]
                        if kw == 512:
                            e.activation(scr[:, 0:1024], pbsrc[:, 0:1024],
                                         AF.Exp).then_inc(sem["aC1"], 1)
                        else:
                            e.activation(
                                scr[:, 0:2 * kw].rearrange("p (h k) -> p h k", h=2),
                                pbsrc[:, 0:1024].rearrange("p (h k) -> p h k",
                                                           h=2)[:, :, 0:kw],
                                AF.Exp).then_inc(sem["aC1"], 1)
        e.sem_inc(bar, 1)
        e.wait_ge(bar, 15)


def build_program(debug=False, stages=("P0", "P1", "P3", "CMB")):
    return Builder(debug=debug, stages=stages)


def kernel(L0_attn_content, L0_attn_Wq, L0_attn_Wk, L0_attn_Wv,
           L0_ffn_sim_matrix, L0_ffn_gates, L0_ffn_q_proj, L0_ffn_k_proj,
           L0_ffn_v_proj, L0_ffn_o_proj):
    x = np.ascontiguousarray(L0_attn_content, dtype=np.float32)
    common = {
        "wq": np.ascontiguousarray(L0_attn_Wq, np.float32),
        "wk": np.ascontiguousarray(L0_attn_Wk, np.float32),
        "wv": np.ascontiguousarray(L0_attn_Wv, np.float32),
        "sim": np.ascontiguousarray(L0_ffn_sim_matrix, np.float32),
        "gates": np.ascontiguousarray(L0_ffn_gates, np.float32),
        "qp": np.ascontiguousarray(L0_ffn_q_proj, np.float32),
        "kp": np.ascontiguousarray(L0_ffn_k_proj, np.float32),
        "vp": np.ascontiguousarray(L0_ffn_v_proj, np.float32),
        "op": np.ascontiguousarray(L0_ffn_o_proj, np.float32),
    }
    in_maps = [dict(common, x=np.ascontiguousarray(x[i])) for i in range(8)]
    nc = build_program().build()
    res = run_bass_kernel_spmd(nc, in_maps, core_ids=list(range(8)))
    return np.stack([np.asarray(r["out"], np.float32) for r in res.results], axis=0)
